# version 02
# baseline (speedup 1.0000x reference)
import numpy as np
from contextlib import ExitStack

import concourse.bass as bass
import concourse.mybir as mybir
from concourse.bass_utils import run_bass_kernel_spmd

F32 = mybir.dt.float32
BF16 = mybir.dt.bfloat16
AF = mybir.ActivationFunctionType
OP = mybir.AluOpType
AX = mybir.AxisListType

NCORES = 8
S = 4096
D = 1024
NT = S // 128
DCH = D // 128
H = 16
DH = 64
SOFTMAX_SHIFT = 20.0
P1_LIMIT = None
GC0 = 0.7978845608028654
GC2 = GC0 * 0.044715


class Prog:

    def __init__(self, nc):
        self.nc = nc
        self.epochs = [[]]
        self.totals = [0]

    @property
    def total(self):
        return self.totals[-1]

    def new_epoch(self):
        if self.epochs[-1]:
            self.epochs.append([])
            self.totals.append(0)

    def op(self, eng, fn, inc=1):
        self.epochs[-1].append((eng, "op", fn, self.totals[-1], inc))
        self.totals[-1] += inc

    def dma(self, eng, pairs, **kw):
        n = len(pairs)
        self.epochs[-1].append((eng, "dma", (pairs, kw), self.totals[-1], 16 * n))
        self.totals[-1] += 16 * n

    def build(self, sems):
        nc = self.nc
        epochs = self.epochs
        totals = self.totals

        def runner(q, name):
            for ei, steps in enumerate(epochs):
                first = True
                for (eng, kind, payload, wait, inc) in steps:
                    if eng != name:
                        continue
                    if first and ei > 0:
                        q.wait_ge(sems[ei - 1], totals[ei - 1])
                    first = False
                    if wait > 0:
                        q.wait_ge(sems[ei], wait)
                    if kind == "dma":
                        pairs, kw = payload
                        for (o, i) in pairs:
                            q.dma_start(out=o, in_=i, **kw).then_inc(sems[ei], 16)
                    else:
                        last = payload(q)
                        last.then_inc(sems[ei], inc)
            if name == "sync":
                if len(epochs) > 1:
                    q.wait_ge(sems[len(epochs) - 2], totals[-2])
                q.wait_ge(sems[len(epochs) - 1], totals[-1])

        with nc.Block() as block:
            @block.sync
            def _(q):
                runner(q, "sync")

            @block.scalar
            def _(q):
                runner(q, "scalar")

            @block.vector
            def _(q):
                runner(q, "vector")

            @block.tensor
            def _(q):
                runner(q, "tensor")

            @block.gpsimd
            def _(q):
                runner(q, "gpsimd")



def _const_pool(nc, P, es, vals):
    vals = sorted(set(float(v) for v in vals))
    cp = es.enter_context(nc.sbuf_tensor("cpool", [128, max(len(vals), 1)], F32))
    def fill(q):
        last = None
        for j, v in enumerate(vals):
            last = q.memset(cp[:, j:j + 1], v)
        return last
    P.op("vector", fill)
    for j, v in enumerate(vals):
        nc.const_aps.register(F32, v, cp[:, j:j + 1])
    return cp

def _ln_tile(P, xt, ln, sq, st, eps=1e-5):
    P.op("vector", lambda q: q.reduce_sum(out=st[:, 0:1], in_=xt, axis=AX.X))
    P.op("scalar", lambda q: q.mul(st[:, 1:2], st[:, 0:1], -1.0 / D))
    P.op("scalar", lambda q: q.activation(out=ln, in_=xt, func=AF.Identity,
                                          bias=st[:, 1:2], scale=1.0))
    P.op("scalar", lambda q: q.square(sq, ln))
    P.op("vector", lambda q: q.reduce_sum(out=st[:, 2:3], in_=sq, axis=AX.X))
    P.op("scalar", lambda q: q.activation(out=st[:, 3:4], in_=st[:, 2:3], func=AF.Sqrt,
                                          bias=float(eps), scale=1.0 / D))
    P.op("vector", lambda q: q.reciprocal(out=st[:, 4:5], in_=st[:, 3:4]))
    P.op("vector", lambda q: q.tensor_scalar(out=ln, in0=ln,
                                             scalar1=st[:, 4:5], scalar2=None,
                                             op0=OP.mult))


def _transpose_tile(P, src_bf, ptb, dst_bf, idb, nchunk=8):
    def mm(q):
        last = None
        for c in range(nchunk):
            last = q.transpose(out=ptb[:, 128 * c:128 * (c + 1)],
                               in_=src_bf[:, 128 * c:128 * (c + 1)],
                               identity=idb[:, :])
        return last
    P.op("tensor", mm)
    P.op("scalar", lambda q: q.copy(dst_bf, ptb[:, 0:128 * nchunk]))



def build_p1(nc=None, P=None, io=None, pref="", consts=True):
    standalone = nc is None
    if standalone:
        nc = bass.Bass("TRN2", target_bir_lowering=False)
        P = Prog(nc)
    io = io or {}
    def dram(name, shape, dt_, kind=None):
        if name in io:
            return io[name]
        if kind is None:
            return nc.dram_tensor(pref + name, shape, dt_).ap()
        return nc.dram_tensor(pref + name, shape, dt_, kind=kind).ap()
    x = dram("x", [S, D], F32, "ExternalInput")
    wq = dram("wq", [D, D], F32, "ExternalInput")
    wk = dram("wk", [D, D], F32, "ExternalInput")
    wv = dram("wv", [D, D], F32, "ExternalInput")
    wo = dram("wo", [D, D], F32, "ExternalInput")
    bout_t = dram("bout_t", [128, D], F32, "ExternalInput")
    ns_t = dram("ns_t", [128, 8], F32, "ExternalInput")
    ident = dram("ident", [128, 128], F32, "ExternalInput")
    x1 = dram("x1", [S, D], F32, "ExternalOutput")

    qT_h = dram("qT_h", [NT * 64, 2048], BF16)
    kT_h = dram("kT_h", [NT * 64, 2048], BF16)
    v_h = dram("v_h", [NT * 128, H * 65], BF16)

    with ExitStack() as es:
        sem = es.enter_context(nc.semaphore(pref + "m")) if standalone else None
        sb = lambda name, shape, dt=F32: es.enter_context(nc.sbuf_tensor(pref + name, shape, dt))
        ps = lambda name, shape, dt=F32: es.enter_context(nc.psum_tensor(pref + name, shape, dt))

        if consts:
            _const_pool(nc, P, es, [0.0, 1e-5, 1e-6, -SOFTMAX_SHIFT])
        wqb = sb("wqb", [128, DCH * D], BF16)
        wkb = sb("wkb", [128, DCH * D], BF16)
        wvb = sb("wvb", [128, DCH * D], BF16)
        wob = sb("wob", [128, DCH * D], BF16)
        bob = sb("bob", [128, D])
        nsc = sb("nsc", [128, 8])
        idf = sb("idf", [128, 128])
        idb = sb("idb", [128, 128], BF16)
        stg = sb("stg", [128, D])
        xt = sb("xt", [128, D])
        lnb = sb("ln", [128, D])
        sq = sb("sq", [128, D])
        stt = sb("st", [128, 8])
        hb = sb("hb", [128, D], BF16)
        hT = sb("hT", [128, D], BF16)
        qTt = sb("qTt", [128, D], BF16)
        kTt = sb("kTt", [128, D], BF16)
        vst = sb("vst", [128, H * 65], BF16)
        qTc = sb("qTc", [64, 4096], BF16)
        ktc = sb("ktc", [64, 2048], BF16)
        vtc = sb("vtc", [128, H * 65], BF16)
        PTk = sb("PTk", [128, H * 256], BF16)
        oq65 = sb("oq65", [128, H * 65])
        rcol = sb("rcol", [128, H])
        oqb = sb("oqb", [128, D], BF16)
        oqT = sb("oqT", [128, D], BF16)
        xr = sb("xr", [128, D])
        res = sb("res", [128, D])
        ptb = ps("ptb", [128, D], BF16)
        pacc = ps("pacc", [128, 512])
        pav0 = ps("pav0", [128, 455])
        pav1 = ps("pav1", [128, 455])
        pav2 = ps("pav2", [128, 130])
        pav3 = ps("pav3", [128, 455])
        pav4 = ps("pav4", [128, 455])
        pav5 = ps("pav5", [128, 130])
        pav = [[pav0, pav1, pav2], [pav3, pav4, pav5]]

        def av_slice(qs_, h):
            si = min(h // 7, 2)
            loc = h - 7 * si
            return pav[qs_][si][:, 65 * loc:65 * loc + 65]

        P.dma("sync", [(idf[:, :], ident), (bob[:, :], bout_t), (nsc[:, :], ns_t)])
        P.op("vector", lambda q: q.tensor_copy(out=idb[:, :], in_=idf[:, :]))
        for dc in range(DCH):
            for (src, dst, sc) in ((wq, wqb, 0.125), (wk, wkb, 1.0), (wv, wvb, 1.0),
                                   (wo, wob, None)):
                P.dma("sync", [(stg[:, :], src[128 * dc:128 * (dc + 1), :])])
                if sc is not None:
                    P.op("vector", lambda q, sc=sc, dc=dc: q.tensor_scalar(
                        out=stg[:, :], in0=stg[:, :], scalar1=nsc[:, dc:dc + 1],
                        scalar2=sc, op0=OP.mult, op1=OP.mult))
                P.op("vector", lambda q, dst=dst, dc=dc: q.tensor_copy(
                    out=dst[:, D * dc:D * (dc + 1)], in_=stg[:, :]))

        def set_ones(q):
            last = None
            for h in range(H):
                last = q.memset(vst[:, 65 * h + 64:65 * h + 65], 1.0)
            return last
        P.op("vector", set_ones)

        for i in range(NT if P1_LIMIT != "W" else 0):
            P.dma("sync", [(xt[:, :], x[128 * i:128 * (i + 1), :])])
            _ln_tile(P, xt[:, :], lnb[:, :], sq[:, :], stt)
            P.op("scalar", lambda q: q.square(sq[:, :], lnb[:, :]))
            P.op("vector", lambda q: q.reduce_sum(out=stt[:, 5:6], in_=sq[:, :], axis=AX.X))
            P.op("scalar", lambda q: q.activation(out=stt[:, 6:7], in_=stt[:, 5:6],
                                                  func=AF.Sqrt, bias=1e-6, scale=1.0 / D))
            P.op("vector", lambda q: q.reciprocal(out=stt[:, 7:8], in_=stt[:, 6:7]))
            P.op("vector", lambda q: q.tensor_scalar(out=hb[:, :], in0=lnb[:, :],
                                                     scalar1=stt[:, 7:8], scalar2=None,
                                                     op0=OP.mult))
            _transpose_tile(P, hb, ptb, hT[:, :], idb)
            for (wb, dst) in ((wqb, qTt), (wkb, kTt)):
                for og in range(2):
                    def mmq(q, wb=wb, og=og):
                        last = None
                        for oc2 in range(4):
                            oc = 4 * og + oc2
                            for dc2 in range(DCH):
                                last = q.matmul(pacc[:, 128 * oc2:128 * (oc2 + 1)],
                                                wb[:, D * dc2 + 128 * oc:D * dc2 + 128 * (oc + 1)],
                                                hT[:, 128 * dc2:128 * (dc2 + 1)],
                                                start=(dc2 == 0), stop=(dc2 == DCH - 1))
                        return last
                    P.op("tensor", mmq)
                    P.op("scalar", lambda q, dst=dst, og=og:
                         q.copy(dst[:, 512 * og:512 * (og + 1)], pacc[:, 0:512]))
            for half in range(2):
                def mmv(q, half=half):
                    last = None
                    for dc2 in range(DCH):
                        last = q.matmul(pacc[:, 0:512],
                                        hT[:, 128 * dc2:128 * (dc2 + 1)],
                                        wvb[:, D * dc2 + 512 * half:D * dc2 + 512 * (half + 1)],
                                        start=(dc2 == 0), stop=(dc2 == DCH - 1))
                    return last
                P.op("tensor", mmv)
                def evacv(q, half=half):
                    last = None
                    for hh in range(8):
                        h = 8 * half + hh
                        last = q.copy(vst[:, 65 * h:65 * h + 64],
                                      pacc[:, 64 * hh:64 * (hh + 1)])
                    return last
                P.op("scalar", evacv)
            def hm_dst(dr, i, par):
                return dr[64 * i:64 * (i + 1), :].rearrange(
                    "p (c two j) -> p c two j", two=2, j=128)[:, :, par:par + 1, :]
            def hm_src(t, par):
                return t[64 * par:64 * (par + 1), :].rearrange(
                    "p (c one j) -> p c one j", one=1, j=128)
            P.dma("gpsimd", [(hm_dst(qT_h, i, 0), hm_src(qTt, 0)),
                             (hm_dst(qT_h, i, 1), hm_src(qTt, 1)),
                             (hm_dst(kT_h, i, 0), hm_src(kTt, 0)),
                             (hm_dst(kT_h, i, 1), hm_src(kTt, 1)),
                             (v_h[128 * i:128 * (i + 1), :], vst[:, :])])

        for jc in range(NT // 2 if (P1_LIMIT is None or P1_LIMIT.startswith("B")) else 0):
            t0 = 2 * jc
            def clr_pav(q):
                last = None
                for qs_ in range(2):
                    for si in range(3):
                        last = q.memset(pav[qs_][si][:, :], 0.0)
                return last
            P.op("vector", clr_pav)
            P.dma("sync", [(qTc[0:64, 2048 * t:2048 * (t + 1)],
                            qT_h[64 * (t0 + t):64 * (t0 + t + 1), :])
                           for t in range(2)])
            for kt in range(NT):
                P.dma("sync", [(ktc[0:64, :], kT_h[64 * kt:64 * (kt + 1), :])])
                P.dma("gpsimd", [(vtc[:, :], v_h[128 * kt:128 * (kt + 1), :])])
                for hp in range(8):
                    def mmS(q, hp=hp):
                        last = None
                        for ho in range(2):
                            h = 2 * hp + ho
                            for tt in range(2):
                                last = q.matmul(
                                    pacc[:, 256 * ho + 128 * tt:256 * ho + 128 * (tt + 1)],
                                    ktc[0:64, 128 * h:128 * (h + 1)],
                                    qTc[0:64, 2048 * tt + 128 * h:2048 * tt + 128 * (h + 1)],
                                    start=True, stop=True)
                        return last
                    P.op("tensor", mmS)
                    if P1_LIMIT in ("B0", "B0a"):
                        continue
                    P.op("scalar", lambda q, hp=hp: q.activation(
                        out=PTk[:, 512 * hp:512 * (hp + 1)], in_=pacc[:, 0:512],
                        func=AF.Exp, bias=-SOFTMAX_SHIFT, scale=1.0))
                if P1_LIMIT in ("B0", "B0a", "B1"):
                    continue
                def mmav(q, kt=kt):
                    last = None
                    for qs_ in range(2):
                        for h in range(H):
                            last = q.matmul(av_slice(qs_, h),
                                            PTk[:, 256 * h + 128 * qs_:256 * h + 128 * (qs_ + 1)],
                                            vtc[:, 65 * h:65 * (h + 1)],
                                            start=False, stop=(kt == NT - 1),
                                            skip_group_check=True)
                    return last
                P.op("tensor", mmav)
            for qs_ in range(2):
                if P1_LIMIT in ("B0", "B0a", "B1", "B2"):
                    continue
                def evac(q, qs_=qs_):
                    last = None
                    for h in range(H):
                        last = q.copy(oq65[:, 65 * h:65 * (h + 1)], av_slice(qs_, h))
                    return last
                P.op("scalar", evac)
                def recips(q):
                    last = None
                    for h in range(H):
                        last = q.reciprocal(out=rcol[:, h:h + 1],
                                            in_=oq65[:, 65 * h + 64:65 * h + 65])
                    return last
                P.op("vector", recips)
                def divs(q):
                    last = None
                    for h in range(H):
                        last = q.tensor_scalar(out=oqb[:, 64 * h:64 * (h + 1)],
                                               in0=oq65[:, 65 * h:65 * h + 64],
                                               scalar1=rcol[:, h:h + 1], scalar2=None,
                                               op0=OP.mult)
                    return last
                P.op("vector", divs)
                _transpose_tile(P, oqb, ptb, oqT[:, :], idb)
                ti = 2 * jc + qs_
                P.dma("sync", [(xr[:, :], x[128 * ti:128 * (ti + 1), :])])
                for half in range(2):
                    def mmo(q, half=half):
                        last = None
                        for dc2 in range(DCH):
                            last = q.matmul(pacc[:, 0:512],
                                            oqT[:, 128 * dc2:128 * (dc2 + 1)],
                                            wob[:, D * dc2 + 512 * half:D * dc2 + 512 * (half + 1)],
                                            start=(dc2 == 0), stop=(dc2 == DCH - 1))
                        return last
                    P.op("tensor", mmo)
                    P.op("vector", lambda q, half=half: q.tensor_tensor(
                        out=res[:, 512 * half:512 * (half + 1)], in0=pacc[:, 0:512],
                        in1=bob[:, 512 * half:512 * (half + 1)], op=OP.add))
                P.op("vector", lambda q: q.tensor_tensor(
                    out=res[:, :], in0=res[:, :], in1=xr[:, :], op=OP.add))
                P.dma("gpsimd", [(x1[128 * ti:128 * (ti + 1), :], res[:, :])])

        if standalone:
            P.build([sem])
    return nc



def build_p2(nc=None, P=None, io=None, pref="", consts=True):
    standalone = nc is None
    if standalone:
        nc = bass.Bass("TRN2", target_bir_lowering=False)
        P = Prog(nc)
    io = io or {}
    def dram(name, shape, dt_, kind=None):
        if name in io:
            return io[name]
        if kind is None:
            return nc.dram_tensor(pref + name, shape, dt_).ap()
        return nc.dram_tensor(pref + name, shape, dt_, kind=kind).ap()
    x1 = dram("x1", [S, D], F32, "ExternalInput")
    Ws = dram("Ws", [D, 8], F32, "ExternalInput")
    bs_t = dram("bs_t", [128, 8], F32, "ExternalInput")
    ident = dram("ident", [128, 128], F32, "ExternalInput")
    probs = dram("probs", [S, 8], F32, "ExternalOutput")
    cc_in = io.get("cc_in")

    with ExitStack() as es:
        sem = es.enter_context(nc.semaphore(pref + "m")) if standalone else None
        sb = lambda name, shape, dt=F32: es.enter_context(nc.sbuf_tensor(pref + name, shape, dt))
        if consts:
            _const_pool(nc, P, es, [0.0, 1e-5])
        wsb = sb("wsb", [128, 64], BF16)
        wss = sb("wss", [128, 64])
        bsb = sb("bsb", [128, 8])
        idf = sb("idf", [128, 128])
        idb = sb("idb", [128, 128], BF16)
        xt = sb("xt", [128, D])
        lnb = sb("ln", [128, D])
        sq = sb("sq", [128, D])
        stt = sb("st", [128, 8])
        hb = sb("hb", [128, D], BF16)
        hT = sb("hT", [128, D], BF16)
        lg = sb("lg", [128, 8])
        pr = sb("pr", [128, 8])
        prT = sb("prT", [8, S]) if cc_in is not None else None
        ptb = es.enter_context(nc.psum_tensor(pref + "ptb", [128, D], BF16))
        plg = es.enter_context(nc.psum_tensor(pref + "plg", [128, 8], F32))
        pprT = (es.enter_context(nc.psum_tensor(pref + "pprT", [8, 128], F32))
                if cc_in is not None else None)

        P.dma("sync", [(idf[:, :], ident), (bsb[:, :], bs_t)] +
                      [(wss[:, 8 * dc:8 * (dc + 1)], Ws[128 * dc:128 * (dc + 1), :])
                       for dc in range(DCH)])
        P.op("vector", lambda q: q.tensor_copy(out=idb[:, :], in_=idf[:, :]))
        P.op("vector", lambda q: q.tensor_copy(out=wsb[:, :], in_=wss[:, :]))

        for i in range(NT):
            P.dma("sync", [(xt[:, :], x1[128 * i:128 * (i + 1), :])])
            _ln_tile(P, xt[:, :], lnb[:, :], sq[:, :], stt)
            P.op("vector", lambda q: q.tensor_copy(out=hb[:, :], in_=lnb[:, :]))
            _transpose_tile(P, hb, ptb, hT[:, :], idb)
            def mml(q):
                last = None
                for dc in range(DCH):
                    last = q.matmul(plg[:, 0:8], hT[:, 128 * dc:128 * (dc + 1)],
                                    wsb[:, 8 * dc:8 * (dc + 1)],
                                    start=(dc == 0), stop=(dc == DCH - 1))
                return last
            P.op("tensor", mml)
            P.op("vector", lambda q: q.tensor_tensor(out=lg[:, :], in0=plg[:, 0:8],
                                                     in1=bsb[:, :], op=OP.add))
            P.op("vector", lambda q: q.reduce_max(out=stt[:, 0:1], in_=lg[:, :],
                                                  axis=AX.X, negate=True))
            P.op("scalar", lambda q: q.activation(out=pr[:, :], in_=lg[:, :],
                                                  func=AF.Exp, bias=stt[:, 0:1], scale=1.0))
            P.op("vector", lambda q: q.reduce_sum(out=stt[:, 1:2], in_=pr[:, :], axis=AX.X))
            P.op("vector", lambda q: q.reciprocal(out=stt[:, 2:3], in_=stt[:, 1:2]))
            P.op("vector", lambda q: q.tensor_scalar(out=pr[:, :], in0=pr[:, :],
                                                     scalar1=stt[:, 2:3], scalar2=None,
                                                     op0=OP.mult))
            P.dma("gpsimd", [(probs[128 * i:128 * (i + 1), :], pr[:, :])])
            if cc_in is not None:
                P.op("tensor", lambda q: q.transpose(out=pprT[:, :], in_=pr[:, :],
                                                     identity=idf[:, :]))
                P.op("scalar", lambda q, i=i: q.copy(prT[:, 128 * i:128 * (i + 1)],
                                                     pprT[:, :]))
        if cc_in is not None:
            P.dma("gpsimd", [(cc_in.rearrange("(p h) c -> p (h c)", h=2), prT[:, :])])
        if standalone:
            P.build([sem])
    return nc



BISECT_ITERS = 38
TOPK = 4096
FF = 4096


def build_p3(nc=None, P=None, io=None, pref="", consts=True):
    standalone = nc is None
    if standalone:
        nc = bass.Bass("TRN2", target_bir_lowering=False)
        P = Prog(nc)
    io = io or {}
    def dram(name, shape, dt_, kind=None):
        if name in io:
            return io[name]
        if kind is None:
            return nc.dram_tensor(pref + name, shape, dt_).ap()
        return nc.dram_tensor(pref + name, shape, dt_, kind=kind).ap()
    x1 = dram("x1", [S, D], F32, "ExternalInput")
    pb_in = dram("pb", [128, 2048], F32, "ExternalInput")
    pown = dram("pown", [S, 8], F32, "ExternalInput")
    W1 = dram("W1", [8 * D, FF], F32, "ExternalInput")
    W2 = dram("W2", [8 * FF, D], F32, "ExternalInput")
    b1c = dram("b1c", [8 * 128, 32], F32, "ExternalInput")
    b2t = dram("b2t", [8 * 128, D], F32, "ExternalInput")
    sel8 = dram("sel8", [128, 8], F32, "ExternalInput")
    sel8T = dram("sel8T", [8, 128], F32, "ExternalInput")
    ones1 = dram("ones1", [1, 128], F32, "ExternalInput")
    ident = dram("ident", [128, 128], F32, "ExternalInput")
    x2 = dram("x2", [S, D], F32, "ExternalOutput")

    lnT_h = dram("lnT_h", [NT * 128, D], BF16)

    TC = 256
    NTC = S // TC

    with ExitStack() as es:
        sem = es.enter_context(nc.semaphore(pref + "m")) if standalone else None
        sb = lambda name, shape, dt=F32: es.enter_context(nc.sbuf_tensor(pref + name, shape, dt))
        ps = lambda name, shape, dt=F32: es.enter_context(nc.psum_tensor(pref + name, shape, dt))

        if consts:
            _const_pool(nc, P, es, [0.0, 1e-5])
        big = sb("big", [128, 5120])
        idf = sb("idf", [128, 128])
        idb = sb("idb", [128, 128], BF16)
        w_all = sb("w_all", [128, NT * 8])
        hb = sb("hb", [128, D], BF16)
        hT = sb("hT", [128, D], BF16)
        stt = sb("st", [128, 8])
        bis = sb("bis", [128, 8])
        sel_s = sb("sel_s", [128, 8])
        selT_s = sb("selT_s", [8, 128])
        on_s = sb("on_s", [1, 128])
        c8 = sb("c8", [8, 2])
        lo8 = sb("lo8", [1, 8])
        losb = sb("losb", [128, 8])
        pot = sb("pot", [128, 8])
        w1b = sb("w1b", [128, DCH * FF], BF16)
        w2b = sb("w2b", [128, 32 * D], BF16)
        lnc = sb("lnc", [128, 2 * D], BF16)
        atb = sb("atb", [128, 32 * TC], BF16)
        b1s = sb("b1s", [128, 32])
        b2s = sb("b2s", [128, D])
        ptb = ps("ptb", [128, D], BF16)
        ptiny = ps("ptiny", [128, 16])
        pA = ps("pA", [128, 512])
        pY = ps("pY", [128, D])

        xt = big[:, 0:1024]
        lnv = big[:, 1024:2048]
        sqv = big[:, 2048:3072]
        pbs = big[:, 0:2048]
        msk = big[:, 2048:4096]
        wst = big[:, 0:1024]
        ga = big[:, 1024:1280]
        gb = big[:, 1280:1536]
        gc = big[:, 1536:1792]
        acc = big[:, 2048:3072]
        yt = big[:, 3072:4096]

        P.dma("sync", [(idf[:, :], ident)])
        P.op("vector", lambda q: q.tensor_copy(out=idb[:, :], in_=idf[:, :]))

        for i in range(NT):
            P.dma("sync", [(xt, x1[128 * i:128 * (i + 1), :])])
            _ln_tile(P, xt, lnv, sqv, stt)
            P.op("vector", lambda q: q.tensor_copy(out=hb[:, :], in_=lnv))
            _transpose_tile(P, hb, ptb, hT[:, :], idb)
            P.dma("gpsimd", [(lnT_h[128 * i:128 * (i + 1), :], hT[:, :])])

        P.dma("sync", [(pbs, pb_in), (sel_s[:, :], sel8), (selT_s[:, :], sel8T),
                       (on_s[:, :], ones1)])
        LO, HI, MID, CNT = 0, 1, 2, 3
        P.op("vector", lambda q: q.memset(bis[:, LO:LO + 1], 0.0))
        P.op("vector", lambda q: q.memset(bis[:, HI:HI + 1], 2.0))
        cl, ch, nl, nh = 0, 1, 4, 5
        for _ in range(BISECT_ITERS):
            P.op("vector", lambda q, cl=cl, ch=ch: q.tensor_tensor(
                out=bis[:, MID:MID + 1], in0=bis[:, cl:cl + 1],
                in1=bis[:, ch:ch + 1], op=OP.add))
            P.op("scalar", lambda q: q.mul(bis[:, MID:MID + 1], bis[:, MID:MID + 1], 0.5))
            P.op("vector", lambda q: q.tensor_scalar(
                out=msk, in0=pbs, scalar1=bis[:, MID:MID + 1],
                scalar2=None, op0=OP.is_ge))
            P.op("vector", lambda q: q.reduce_sum(
                out=bis[:, CNT:CNT + 1], in_=msk, axis=AX.X))
            P.op("tensor", lambda q: q.matmul(ptiny[0:8, 0:1], sel_s[:, :],
                                              bis[:, CNT:CNT + 1], start=True, stop=True))
            P.op("scalar", lambda q: q.copy(c8[:, 0:1], ptiny[0:8, 0:1]))
            P.op("vector", lambda q: q.tensor_scalar(
                out=c8[:, 1:2], in0=c8[:, 0:1], scalar1=float(TOPK), scalar2=None,
                op0=OP.is_ge))
            P.op("tensor", lambda q: q.matmul(ptiny[:, 1:2], selT_s[:, :],
                                              c8[:, 1:2], start=True, stop=True))
            P.op("scalar", lambda q: q.copy(c8[:, 0:1] if False else losb[:, 0:1],
                                            ptiny[:, 1:2]))
            P.op("vector", lambda q, cl=cl: q.tensor_tensor(
                out=losb[:, 1:2], in0=bis[:, MID:MID + 1], in1=bis[:, cl:cl + 1],
                op=OP.subtract))
            P.op("vector", lambda q: q.tensor_tensor(
                out=losb[:, 1:2], in0=losb[:, 1:2], in1=losb[:, 0:1], op=OP.mult))
            P.op("vector", lambda q, cl=cl, nl=nl: q.tensor_tensor(
                out=bis[:, nl:nl + 1], in0=bis[:, cl:cl + 1], in1=losb[:, 1:2],
                op=OP.add))
            P.op("vector", lambda q, ch=ch: q.tensor_tensor(
                out=losb[:, 2:3], in0=bis[:, ch:ch + 1], in1=bis[:, MID:MID + 1],
                op=OP.subtract))
            P.op("vector", lambda q: q.tensor_tensor(
                out=losb[:, 2:3], in0=losb[:, 2:3], in1=losb[:, 0:1], op=OP.mult))
            P.op("vector", lambda q, nh=nh: q.tensor_tensor(
                out=bis[:, nh:nh + 1], in0=bis[:, MID:MID + 1], in1=losb[:, 2:3],
                op=OP.add))
            cl, nl = nl, cl
            ch, nh = nh, ch
        P.op("tensor", lambda q, cl=cl: q.matmul(ptiny[0:1, 2:10], bis[:, cl:cl + 1],
                                                 sel_s[:, :], start=True, stop=True))
        P.op("scalar", lambda q: q.mul(lo8[:, :], ptiny[0:1, 2:10], 1.0 / 16.0))
        P.op("tensor", lambda q: q.matmul(ptiny[:, 2:10], on_s[0:1, :],
                                          lo8[0:1, :], start=True, stop=True))
        P.op("scalar", lambda q: q.copy(losb[:, :], ptiny[:, 2:10]))
        for i in range(NT):
            P.dma("sync", [(pot[:, :], pown[128 * i:128 * (i + 1), :])])
            P.op("vector", lambda q: q.tensor_tensor(out=stt[:, :], in0=pot[:, :],
                                                     in1=losb[:, :], op=OP.is_ge))
            P.op("vector", lambda q, i=i: q.tensor_tensor(
                out=w_all[:, 8 * i:8 * (i + 1)], in0=pot[:, :], in1=stt[:, :],
                op=OP.mult))

        for e in range(8):
            P.dma("sync", [(b1s[:, :], b1c[128 * e:128 * (e + 1), :]),
                           (b2s[:, :], b2t[128 * e:128 * (e + 1), :])])
            for dc in range(DCH):
                for qt in range(4):
                    P.dma("sync", [(wst, W1[D * e + 128 * dc:D * e + 128 * (dc + 1),
                                            1024 * qt:1024 * (qt + 1)])])
                    P.op("vector", lambda q, dc=dc, qt=qt: q.tensor_copy(
                        out=w1b[:, FF * dc + 1024 * qt:FF * dc + 1024 * (qt + 1)],
                        in_=wst))
            for fc in range(32):
                P.dma("sync", [(wst, W2[FF * e + 128 * fc:FF * e + 128 * (fc + 1), :])])
                P.op("vector", lambda q, fc=fc: q.tensor_copy(
                    out=w2b[:, D * fc:D * (fc + 1)], in_=wst))
            for tc in range(NTC):
                P.dma("sync", [(lnc[:, D * t:D * (t + 1)],
                                lnT_h[128 * (2 * tc + t):128 * (2 * tc + t + 1), :])
                               for t in range(2)])
                for fc in range(32):
                    def mmA(q, fc=fc):
                        last = None
                        for tt in range(2):
                            for dc in range(DCH):
                                last = q.matmul(
                                    pA[:, 128 * tt:128 * (tt + 1)],
                                    w1b[:, FF * dc + 128 * fc:FF * dc + 128 * (fc + 1)],
                                    lnc[:, D * tt + 128 * dc:D * tt + 128 * (dc + 1)],
                                    start=(dc == 0), stop=(dc == DCH - 1),
                                    skip_group_check=True)
                        return last
                    P.op("tensor", mmA)
                    b1col = b1s[:, fc:fc + 1]
                    P.op("scalar", lambda q, b1col=b1col: q.activation(
                        out=ga, in_=pA[:, 0:TC], func=AF.Identity, bias=b1col, scale=1.0))
                    P.op("scalar", lambda q: q.square(gb, ga))
                    P.op("vector", lambda q: q.tensor_scalar(
                        out=gb, in0=gb, scalar1=GC2, scalar2=GC0,
                        op0=OP.mult, op1=OP.add))
                    P.op("vector", lambda q: q.tensor_tensor(
                        out=gb, in0=gb, in1=ga, op=OP.mult))
                    P.op("scalar", lambda q: q.activation(
                        out=gc, in_=gb, func=AF.Tanh, bias=0.0, scale=1.0))
                    P.op("vector", lambda q: q.scalar_tensor_tensor(
                        out=gc, in0=gc, scalar=1.0, in1=ga, op0=OP.add, op1=OP.mult))
                    P.op("scalar", lambda q, fc=fc: q.activation(
                        out=atb[:, TC * fc:TC * (fc + 1)], in_=gc,
                        func=AF.Copy, bias=0.0, scale=0.5))
                for ts in range(2):
                    ti = 2 * tc + ts
                    def mmY(q, ts=ts):
                        last = None
                        for fc in range(32):
                            for oh in range(2):
                                last = q.matmul(
                                    pY[:, 512 * oh:512 * (oh + 1)],
                                    atb[:, TC * fc + 128 * ts:TC * fc + 128 * (ts + 1)],
                                    w2b[:, D * fc + 512 * oh:D * fc + 512 * (oh + 1)],
                                    start=(fc == 0), stop=(fc == 31),
                                    skip_group_check=True)
                        return last
                    P.op("tensor", mmY)
                    src = x1 if e == 0 else x2
                    P.dma("sync", [(acc, src[128 * ti:128 * (ti + 1), :])])
                    P.op("vector", lambda q: q.tensor_tensor(
                        out=yt, in0=pY[:, 0:D], in1=b2s[:, :], op=OP.add))
                    P.op("vector", lambda q, ti=ti, e=e: q.tensor_scalar(
                        out=yt, in0=yt,
                        scalar1=w_all[:, 8 * ti + e:8 * ti + e + 1],
                        scalar2=None, op0=OP.mult))
                    P.op("vector", lambda q: q.tensor_tensor(
                        out=yt, in0=yt, in1=acc, op=OP.add))
                    P.dma("gpsimd", [(x2[128 * ti:128 * (ti + 1), :], yt)])
        if standalone:
            P.build([sem])
    return nc



def build_p4(nc=None, P=None, io=None, pref="", consts=True):
    standalone = nc is None
    if standalone:
        nc = bass.Bass("TRN2", target_bir_lowering=False)
        P = Prog(nc)
    io = io or {}
    def dram(name, shape, dt_, kind=None):
        if name in io:
            return io[name]
        if kind is None:
            return nc.dram_tensor(pref + name, shape, dt_).ap()
        return nc.dram_tensor(pref + name, shape, dt_, kind=kind).ap()
    x2 = dram("x2", [S, D], F32, "ExternalInput")
    Wq = dram("Wq", [D, D], F32, "ExternalInput")
    Wk = dram("Wk", [D, D], F32, "ExternalInput")
    Wv = dram("Wv", [D, D], F32, "ExternalInput")
    Wo = dram("Wo", [D, D], F32, "ExternalInput")
    bqc = dram("bqc", [128, 8], F32, "ExternalInput")
    bkb = dram("bkb", [128, D], F32, "ExternalInput")
    bvb = dram("bvb", [128, D], F32, "ExternalInput")
    bob = dram("bob", [128, D], F32, "ExternalInput")
    ones1 = dram("ones1", [1, 128], F32, "ExternalInput")
    ident = dram("ident", [128, 128], F32, "ExternalInput")
    x3 = dram("x3", [S, D], F32, "ExternalOutput")

    with ExitStack() as es:
        sem = es.enter_context(nc.semaphore(pref + "m")) if standalone else None
        sb = lambda name, shape, dt=F32: es.enter_context(nc.sbuf_tensor(pref + name, shape, dt))
        ps = lambda name, shape, dt=F32: es.enter_context(nc.psum_tensor(pref + name, shape, dt))

        if consts:
            _const_pool(nc, P, es, [0.0, 1e-5])
        wqT = sb("wqT", [128, DCH * D], BF16)
        wkT = sb("wkT", [128, DCH * D], BF16)
        wvT = sb("wvT", [128, DCH * D], BF16)
        woT = sb("woT", [128, DCH * D], BF16)
        qTr = sb("qTr", [128, DCH * S], BF16)
        bq_s = sb("bq_s", [128, 8])
        bk_s = sb("bk_s", [128, D])
        bv_s = sb("bv_s", [128, D])
        bo_s = sb("bo_s", [128, D])
        on_s = sb("on_s", [1, 128])
        onb = sb("onb", [128, 1], BF16)
        idf = sb("idf", [128, 128])
        idb = sb("idb", [128, 128], BF16)
        KVb = sb("KVb", [64, 8 * 130], BF16)
        xt = sb("xt", [128, D])
        lnb = sb("ln", [128, D])
        sq = sb("sq", [128, D])
        stt = sb("st", [128, 8])
        hb = sb("hb", [128, D], BF16)
        hT = sb("hT", [128, D], BF16)
        f1 = sb("f1", [128, 512])
        f2 = sb("f2", [128, 512])
        f3 = sb("f3", [128, 512])
        kb = sb("kb", [128, D], BF16)
        vb = sb("vb", [128, D], BF16)
        atb = sb("atb", [128, 8 * 512], BF16)
        qtmp = sb("qtmp", [64, 16 * 512], BF16)
        atm = sb("atm", [64, 512], BF16)
        res = sb("res", [128, D])
        ptb = ps("ptb", [128, D], BF16)
        pq = ps("pq", [128, 512])
        pkv0 = ps("pkv0", [64, 390])
        pkv1 = ps("pkv1", [64, 390])
        pkv2 = ps("pkv2", [64, 260])
        pbc = ps("pbc", [64, 512])
        po = ps("po", [128, D])
        pkv = (pkv0, pkv1, pkv2)
        pat = pq[0:65, :]

        def kv_slice(pair, col0, w):
            si = min(pair // 3, 2)
            loc = pair - 3 * si
            return pkv[si][:, 130 * loc + col0:130 * loc + col0 + w]

        P.dma("sync", [(idf[:, :], ident), (bq_s[:, :], bqc), (bk_s[:, :], bkb),
                       (bv_s[:, :], bvb), (bo_s[:, :], bob), (on_s[:, :], ones1)])
        P.op("vector", lambda q: q.tensor_copy(out=idb[:, :], in_=idf[:, :]))
        P.op("vector", lambda q: q.memset(onb[:, :], 1.0))

        for (wsrc, wdst) in ((Wq, wqT), (Wk, wkT), (Wv, wvT), (Wo, woT)):
            for r in range(8):
                P.dma("sync", [(xt[:, :], wsrc[128 * r:128 * (r + 1), :])])
                P.op("vector", lambda q: q.tensor_copy(out=hb[:, :], in_=xt[:, :]))
                def mmt(q):
                    last = None
                    for dc in range(DCH):
                        last = q.transpose(out=ptb[:, 128 * dc:128 * (dc + 1)],
                                           in_=hb[:, 128 * dc:128 * (dc + 1)],
                                           identity=idb[:, :])
                    return last
                P.op("tensor", mmt)
                def cpw(q, wdst=wdst, r=r):
                    last = None
                    for dc in range(DCH):
                        last = q.copy(wdst[:, D * dc + 128 * r:D * dc + 128 * (r + 1)],
                                      ptb[:, 128 * dc:128 * (dc + 1)])
                    return last
                P.op("scalar", cpw)

        def clr_pkv(q):
            last = None
            for t in pkv:
                last = q.memset(t[:, :], 0.0)
            return last
        P.op("vector", clr_pkv)
        for i in range(NT):
            P.dma("sync", [(xt[:, :], x2[128 * i:128 * (i + 1), :])])
            _ln_tile(P, xt[:, :], lnb[:, :], sq[:, :], stt)
            P.op("vector", lambda q: q.tensor_copy(out=hb[:, :], in_=lnb[:, :]))
            _transpose_tile(P, hb, ptb, hT[:, :], idb)
            for (wT, bias_b, dst, feat) in ((wkT, bk_s, kb, True),
                                            (wvT, bv_s, vb, False)):
                for half in range(2):
                    def mmkv(q, wT=wT, half=half):
                        last = None
                        for dc in range(DCH):
                            last = q.matmul(pq[:, 0:512],
                                            hT[:, 128 * dc:128 * (dc + 1)],
                                            wT[:, D * dc + 512 * half:D * dc + 512 * (half + 1)],
                                            start=(dc == 0), stop=(dc == DCH - 1))
                        return last
                    P.op("tensor", mmkv)
                    bs_ = bias_b[:, 512 * half:512 * (half + 1)]
                    if feat:
                        P.op("vector", lambda q, bs_=bs_: q.tensor_tensor(
                            out=f1[:, :], in0=pq[:, 0:512], in1=bs_, op=OP.add))
                        P.op("vector", lambda q: q.tensor_scalar(
                            out=f2[:, :], in0=f1[:, :], scalar1=0.0, scalar2=None,
                            op0=OP.min))
                        P.op("scalar", lambda q: q.activation(
                            out=f3[:, :], in_=f2[:, :], func=AF.Exp, bias=0.0, scale=1.0))
                        P.op("vector", lambda q: q.tensor_scalar(
                            out=f1[:, :], in0=f1[:, :], scalar1=0.0, scalar2=None,
                            op0=OP.max))
                        P.op("vector", lambda q, dst=dst, half=half: q.tensor_tensor(
                            out=dst[:, 512 * half:512 * (half + 1)], in0=f1[:, :],
                            in1=f3[:, :], op=OP.add))
                    else:
                        P.op("vector", lambda q, dst=dst, half=half, bs_=bs_:
                             q.tensor_tensor(out=dst[:, 512 * half:512 * (half + 1)],
                                             in0=pq[:, 0:512], in1=bs_, op=OP.add))
            for og in range(2):
                def mmq(q, og=og):
                    last = None
                    for oc2 in range(4):
                        oc = 4 * og + oc2
                        for dc in range(DCH):
                            last = q.matmul(pq[:, 128 * oc2:128 * (oc2 + 1)],
                                            wqT[:, D * dc + 128 * oc:D * dc + 128 * (oc + 1)],
                                            hT[:, 128 * dc:128 * (dc + 1)],
                                            start=(dc == 0), stop=(dc == DCH - 1))
                    return last
                P.op("tensor", mmq)
                for oc2 in range(4):
                    oc = 4 * og + oc2
                    pslice = pq[:, 128 * oc2:128 * (oc2 + 1)]
                    P.op("scalar", lambda q, pslice=pslice, oc=oc: q.activation(
                        out=f1[:, 0:128], in_=pslice, func=AF.Identity,
                        bias=bq_s[:, oc:oc + 1], scale=1.0))
                    P.op("vector", lambda q: q.tensor_scalar(
                        out=f2[:, 0:128], in0=f1[:, 0:128], scalar1=0.0,
                        scalar2=None, op0=OP.min))
                    P.op("scalar", lambda q: q.activation(
                        out=f3[:, 0:128], in_=f2[:, 0:128], func=AF.Exp,
                        bias=0.0, scale=1.0))
                    P.op("vector", lambda q: q.tensor_scalar(
                        out=f1[:, 0:128], in0=f1[:, 0:128], scalar1=0.0,
                        scalar2=None, op0=OP.max))
                    P.op("vector", lambda q, oc=oc, i=i: q.tensor_tensor(
                        out=qTr[:, S * oc + 128 * i:S * oc + 128 * (i + 1)],
                        in0=f1[:, 0:128], in1=f3[:, 0:128], op=OP.add))
            def mmkvacc(q, i=i):
                last = None
                for h in range(H):
                    pair = h // 2
                    c0 = 65 * (h % 2)
                    last = q.matmul(kv_slice(pair, c0, 64),
                                    kb[:, 64 * h:64 * (h + 1)],
                                    vb[:, 64 * h:64 * (h + 1)],
                                    start=False, stop=(i == NT - 1),
                                    skip_group_check=True)
                    last = q.matmul(kv_slice(pair, c0 + 64, 1),
                                    kb[:, 64 * h:64 * (h + 1)], onb[:, :],
                                    start=False, stop=(i == NT - 1),
                                    skip_group_check=True)
                return last
            P.op("tensor", mmkvacc)
        def evkv(q):
            last = None
            for pair in range(8):
                last = q.copy(KVb[:, 130 * pair:130 * (pair + 1)],
                              kv_slice(pair, 0, 130))
            return last
        P.op("scalar", evkv)

        for tc in range(8):
            def qsrc(par, tc=tc):
                return qTr[64 * par:64 * (par + 1), :].rearrange(
                    "p (oc t) -> p oc t", t=S)[:, :, 512 * tc:512 * (tc + 1)
                    ].rearrange("p oc j -> p oc () j")
            def qdst(par):
                return qtmp[0:64, :].rearrange(
                    "p (oc two j) -> p oc two j", two=2, j=512)[:, :, par:par + 1, :]
            P.dma("sync", [(qdst(0), qsrc(0)), (qdst(1), qsrc(1))])
            for h in range(H):
                oc = h // 2
                pair = h // 2
                c0 = 65 * (h % 2)
                P.op("tensor", lambda q, pair=pair, c0=c0, h=h:
                     q.matmul(pat[:, :],
                              KVb[:, 130 * pair + c0:130 * pair + c0 + 65],
                              qtmp[0:64, 512 * h:512 * (h + 1)],
                              start=True, stop=True))
                P.op("scalar", lambda q: q.copy(f3[0:64, :], pat[0:64, :]))
                P.op("scalar", lambda q: q.copy(f2[64:65, 0:512], pat[64:65, :]))
                P.dma("gpsimd", [(f1[0:1, 0:512], f2[64:65, 0:512])])
                P.op("vector", lambda q: q.tensor_scalar(
                    out=f2[0:1, 0:512], in0=f1[0:1, 0:512], scalar1=1e-6,
                    scalar2=None, op0=OP.add))
                P.op("vector", lambda q: q.reciprocal(out=f1[0:1, 0:512],
                                                      in_=f2[0:1, 0:512]))
                P.op("tensor", lambda q: q.matmul(pbc[:, :], on_s[0:1, 0:64],
                                                  f1[0:1, 0:512], start=True, stop=True))
                P.op("vector", lambda q: q.tensor_tensor(
                    out=atm[:, :], in0=f3[0:64, :], in1=pbc[:, :], op=OP.mult))
                pb0 = 64 * (h % 2)
                P.dma("gpsimd", [(atb[pb0:pb0 + 64, 512 * oc:512 * (oc + 1)],
                                  atm[:, :])])
            for ts in range(4):
                ti = 4 * tc + ts
                P.dma("sync", [(xt[:, :], x2[128 * ti:128 * (ti + 1), :])])
                for half in range(2):
                    def mmo(q, ts=ts, half=half):
                        last = None
                        for dc in range(DCH):
                            last = q.matmul(po[:, 512 * half:512 * (half + 1)],
                                            atb[:, 512 * dc + 128 * ts:512 * dc + 128 * (ts + 1)],
                                            woT[:, D * dc + 512 * half:D * dc + 512 * (half + 1)],
                                            start=(dc == 0), stop=(dc == DCH - 1))
                        return last
                    P.op("tensor", mmo)
                P.op("vector", lambda q: q.tensor_tensor(
                    out=res[:, :], in0=po[:, 0:D], in1=bo_s[:, :], op=OP.add))
                P.op("vector", lambda q: q.tensor_tensor(
                    out=res[:, :], in0=res[:, :], in1=xt[:, :], op=OP.add))
                P.dma("gpsimd", [(x3[128 * ti:128 * (ti + 1), :], res[:, :])])
        if standalone:
            P.build([sem])
    return nc



FI = 2816
NKC = FI // 128
QMIDS = (0.25, 0.75, 1.25, 1.75, 2.5, 3.5, 5.0)
QINCS = (0.5, 0.5, 0.5, 0.5, 1.0, 1.0, 2.0)


def _pair(ap, cw, idx):
    return ap[:, 0:cw].rearrange("p (a two) -> p a two", two=2)[:, :, idx:idx + 1]


def _quant_steps(P, src, dst_bf, width, B):
    nb = width // 16
    qa, qm, lv, qsc, sg = B["qa"], B["qm"], B["lv"], B["qsc"], B["sg"]
    P.op("scalar", lambda q: q.activation(out=qa[:, 0:width], in_=src, func=AF.Abs,
                                          bias=0.0, scale=1.0))
    P.op("scalar", lambda q: q.activation(out=sg[:, 0:width], in_=src, func=AF.Sign,
                                          bias=0.0, scale=1.0))
    cur, cw = qa, width
    for stp in range(4):
        nxt = qm if stp % 2 == 0 else lv
        P.op("vector", lambda q, cur=cur, nxt=nxt, cw=cw: q.tensor_tensor(
            out=nxt[:, 0:cw // 2].rearrange("p (a one) -> p a one", one=1),
            in0=_pair(cur, cw, 0), in1=_pair(cur, cw, 1),
            op=OP.max))
        cur, cw = nxt, cw // 2
    P.op("vector", lambda q, cur=cur: q.tensor_scalar(
        out=qsc[:, 0:nb], in0=cur[:, 0:nb], scalar1=1.0 / 6.0, scalar2=1e-12,
        op0=OP.mult, op1=OP.max))
    qsc3 = qsc[:, 0:nb].rearrange("p (a one) -> p a one", one=1).to_broadcast(
        [128, nb, 16])
    th3 = qsc[:, nb:2 * nb].rearrange("p (a one) -> p a one", one=1).to_broadcast(
        [128, nb, 16])
    a3 = qa[:, 0:width].rearrange("p (a b) -> p a b", b=16)
    m3 = qm[:, 0:width].rearrange("p (a b) -> p a b", b=16)
    l3 = lv[:, 0:width].rearrange("p (a b) -> p a b", b=16)
    for j, (mid, inc) in enumerate(zip(QMIDS, QINCS)):
        P.op("vector", lambda q, mid=mid: q.tensor_scalar(
            out=qsc[:, nb:2 * nb], in0=qsc[:, 0:nb], scalar1=mid, scalar2=None,
            op0=OP.mult))
        if j == 0:
            P.op("vector", lambda q: q.tensor_tensor(out=l3, in0=a3, in1=th3,
                                                     op=OP.is_gt))
            P.op("vector", lambda q, inc=inc: q.tensor_scalar(
                out=lv[:, 0:width], in0=lv[:, 0:width], scalar1=inc, scalar2=None,
                op0=OP.mult))
        else:
            P.op("vector", lambda q: q.tensor_tensor(out=m3, in0=a3, in1=th3,
                                                     op=OP.is_gt))
            P.op("vector", lambda q, inc=inc: q.scalar_tensor_tensor(
                out=lv[:, 0:width], in0=qm[:, 0:width], scalar=inc,
                in1=lv[:, 0:width], op0=OP.mult, op1=OP.add))
    P.op("vector", lambda q: q.tensor_tensor(out=lv[:, 0:width], in0=lv[:, 0:width],
                                             in1=sg[:, 0:width], op=OP.mult))
    d3 = dst_bf.rearrange("p (a b) -> p a b", b=16)
    P.op("vector", lambda q: q.tensor_tensor(out=d3, in0=l3, in1=qsc3, op=OP.mult))


def build_p5(nc=None, P=None, io=None, pref="", consts=True):
    standalone = nc is None
    if standalone:
        nc = bass.Bass("TRN2", target_bir_lowering=False)
        P = Prog(nc)
    io = io or {}
    def dram(name, shape, dt_, kind=None):
        if name in io:
            return io[name]
        if kind is None:
            return nc.dram_tensor(pref + name, shape, dt_).ap()
        return nc.dram_tensor(pref + name, shape, dt_, kind=kind).ap()
    x3 = dram("x3", [S, D], F32, "ExternalInput")
    wg = dram("wg", [FI, D], F32, "ExternalInput")
    wu = dram("wu", [FI, D], F32, "ExternalInput")
    wd = dram("wd", [D, FI], F32, "ExternalInput")
    ident = dram("ident", [128, 128], F32, "ExternalInput")
    out = dram("out", [S, D], F32, "ExternalOutput")

    NH = (512, 512, 512, 512, 512, 256)

    with ExitStack() as es:
        sem = es.enter_context(nc.semaphore(pref + "m")) if standalone else None
        sb = lambda name, shape, dt=F32: es.enter_context(nc.sbuf_tensor(pref + name, shape, dt))
        ps = lambda name, shape, dt=F32: es.enter_context(nc.psum_tensor(pref + name, shape, dt))

        if consts:
            _const_pool(nc, P, es, [0.0, 1e-5])
        wgT = sb("wgT", [128, DCH * FI], BF16)
        wuT = sb("wuT", [128, DCH * FI], BF16)
        wdT = sb("wdT", [128, NKC * D], BF16)
        idf = sb("idf", [128, 128])
        idb = sb("idb", [128, 128], BF16)
        qa = sb("qa", [128, 1408])
        qm = sb("qm", [128, 1408])
        lv = sb("lv", [128, 1408])
        sg_ = sb("sg_", [128, 1408])
        qsc = sb("qsc", [128, 2 * 88])
        stg = sb("stg", [128, 1408])
        qb = sb("qb", [128, FI], BF16)
        xt = sb("xt", [128, D])
        lnb = sb("lnv", [128, D])
        stt = sb("st", [128, 8])
        xqT = sb("xqT", [128, D], BF16)
        hq = sb("hq", [128, FI], BF16)
        hqT = sb("hqT", [128, FI], BF16)
        hc = sb("hc", [128, 512])
        ptb = ps("ptb", [128, D], BF16)
        pg = ps("pg", [128, 512])
        pu = ps("pu", [128, 512])
        po = ps("po", [128, D])
        B = {"qa": qa, "qm": qm, "lv": lv, "qsc": qsc, "sg": sg_}
        sgv = qa[:, 0:512]
        res = lnb

        P.dma("sync", [(idf[:, :], ident)])
        P.op("vector", lambda q: q.tensor_copy(out=idb[:, :], in_=idf[:, :]))

        for (wsrc, wdst) in ((wg, wgT), (wu, wuT)):
            for r in range(NKC):
                P.dma("sync", [(stg[:, 0:D], wsrc[128 * r:128 * (r + 1), :])])
                _quant_steps(P, stg[:, 0:D], qb[:, 0:D], D, B)
                def mmt(q):
                    last = None
                    for dc in range(DCH):
                        last = q.transpose(out=ptb[:, 128 * dc:128 * (dc + 1)],
                                           in_=qb[:, 128 * dc:128 * (dc + 1)],
                                           identity=idb[:, :])
                    return last
                P.op("tensor", mmt)
                def cpw(q, wdst=wdst, r=r):
                    last = None
                    for dc in range(DCH):
                        last = q.copy(wdst[:, FI * dc + 128 * r:FI * dc + 128 * (r + 1)],
                                      ptb[:, 128 * dc:128 * (dc + 1)])
                    return last
                P.op("scalar", cpw)
        for r in range(DCH):
            for hf in range(2):
                P.dma("sync", [(stg[:, :], wd[128 * r:128 * (r + 1),
                                              1408 * hf:1408 * (hf + 1)])])
                _quant_steps(P, stg[:, :], qb[:, 1408 * hf:1408 * (hf + 1)], 1408, B)
            for kg in range(3):
                nblk = min(8, NKC - 8 * kg)
                def mmt2(q, kg=kg, nblk=nblk):
                    last = None
                    for kk in range(nblk):
                        kc = 8 * kg + kk
                        last = q.transpose(out=ptb[:, 128 * kk:128 * (kk + 1)],
                                           in_=qb[:, 128 * kc:128 * (kc + 1)],
                                           identity=idb[:, :])
                    return last
                P.op("tensor", mmt2)
                def cpw2(q, kg=kg, nblk=nblk, r=r):
                    last = None
                    for kk in range(nblk):
                        kc = 8 * kg + kk
                        last = q.copy(wdT[:, D * kc + 128 * r:D * kc + 128 * (r + 1)],
                                      ptb[:, 128 * kk:128 * (kk + 1)])
                    return last
                P.op("scalar", cpw2)

        for i in range(NT):
            P.dma("sync", [(xt[:, :], x3[128 * i:128 * (i + 1), :])])
            _ln_tile(P, xt[:, :], lnb[:, :], qm[:, 0:D], stt)
            _quant_steps(P, lnb[:, :], qb[:, 0:D], D, B)
            _transpose_tile(P, qb[:, 0:D], ptb, xqT[:, :], idb)
            off = 0
            for w in NH:
                for (wT, pp) in ((wgT, pg), (wuT, pu)):
                    def mmg(q, wT=wT, pp=pp, off=off, w=w):
                        last = None
                        for dc in range(DCH):
                            last = q.matmul(pp[:, 0:w],
                                            xqT[:, 128 * dc:128 * (dc + 1)],
                                            wT[:, FI * dc + off:FI * dc + off + w],
                                            start=(dc == 0), stop=(dc == DCH - 1))
                        return last
                    P.op("tensor", mmg)
                P.op("scalar", lambda q, w=w: q.activation(
                    out=sgv[:, 0:w], in_=pg[:, 0:w], func=AF.Silu, bias=0.0, scale=1.0))
                P.op("vector", lambda q, w=w: q.tensor_tensor(
                    out=hc[:, 0:w], in0=sgv[:, 0:w], in1=pu[:, 0:w], op=OP.mult))
                _quant_steps(P, hc[:, 0:w], hq[:, off:off + w], w, B)
                off += w
            for kg in range(3):
                nblk = min(8, NKC - 8 * kg)
                def mmth(q, kg=kg, nblk=nblk):
                    last = None
                    for kk in range(nblk):
                        kc = 8 * kg + kk
                        last = q.transpose(out=ptb[:, 128 * kk:128 * (kk + 1)],
                                           in_=hq[:, 128 * kc:128 * (kc + 1)],
                                           identity=idb[:, :])
                    return last
                P.op("tensor", mmth)
                P.op("scalar", lambda q, kg=kg, nblk=nblk: q.copy(
                    hqT[:, 128 * 8 * kg:128 * (8 * kg + nblk)], ptb[:, 0:128 * nblk]))
            def mmd(q):
                last = None
                for oh in range(2):
                    for kc in range(NKC):
                        last = q.matmul(po[:, 512 * oh:512 * (oh + 1)],
                                        hqT[:, 128 * kc:128 * (kc + 1)],
                                        wdT[:, D * kc + 512 * oh:D * kc + 512 * (oh + 1)],
                                        start=(kc == 0), stop=(kc == NKC - 1),
                                        skip_group_check=True)
                return last
            P.op("tensor", mmd)
            P.op("vector", lambda q: q.tensor_tensor(
                out=res[:, :], in0=po[:, 0:D], in1=xt[:, :], op=OP.add))
            P.dma("gpsimd", [(out[128 * i:128 * (i + 1), :], res[:, :])])
        if standalone:
            P.build([sem])
    return nc



_IDENT = np.eye(128, dtype=np.float32)


def _tile128(v):
    return np.ascontiguousarray(np.tile(np.asarray(v, np.float32)[None, :], (128, 1)))


def _f(a):
    return np.ascontiguousarray(np.asarray(a, np.float32))





def build_fused():
    nc = bass.Bass("TRN2", target_bir_lowering=False)
    P = Prog(nc)
    E = lambda n, s, d: nc.dram_tensor(n, s, d, kind="ExternalInput").ap()
    ident = E("ident", [128, 128], F32)
    ones1 = E("ones1", [1, 128], F32)
    x = E("x", [S, D], F32)
    wq = E("wq", [D, D], F32); wk = E("wk", [D, D], F32)
    wv = E("wv", [D, D], F32); wo = E("wo", [D, D], F32)
    bout_t = E("bout_t", [128, D], F32); ns_t = E("ns_t", [128, 8], F32)
    Ws = E("Ws", [D, 8], F32); bs_t = E("bs_t", [128, 8], F32)
    W1 = E("W1", [8 * D, FF], F32); W2 = E("W2", [8 * FF, D], F32)
    b1c = E("b1c", [8 * 128, 32], F32); b2t = E("b2t", [8 * 128, D], F32)
    sel8 = E("sel8", [128, 8], F32); sel8T = E("sel8T", [8, 128], F32)
    Wq = E("Wq", [D, D], F32); Wk = E("Wk", [D, D], F32)
    Wv = E("Wv", [D, D], F32); Wo = E("Wo", [D, D], F32)
    bqc = E("bqc", [128, 8], F32); bkb = E("bkb", [128, D], F32)
    bvb = E("bvb", [128, D], F32); bob = E("bob", [128, D], F32)
    wg = E("wg", [FI, D], F32); wu = E("wu", [FI, D], F32)
    wd = E("wd", [D, FI], F32)
    out = nc.dram_tensor("out", [S, D], F32, kind="ExternalOutput").ap()
    x1_h = nc.dram_tensor("x1_h", [S, D], F32).ap()
    x2_h = nc.dram_tensor("x2_h", [S, D], F32).ap()
    x3_h = nc.dram_tensor("x3_h", [S, D], F32).ap()
    probs_h = nc.dram_tensor("probs_h", [S, 8], F32).ap()
    cc_in_t = nc.dram_tensor("cc_in", [16, 2048], F32)
    cc_out_t = nc.dram_tensor("cc_out", [128, 2048], F32)

    with ExitStack() as top:
        sems = [top.enter_context(nc.semaphore("ep%d" % i)) for i in range(5)]
        cc_sem = top.enter_context(nc.semaphore("ccs"))
        _const_pool(nc, P, top, [0.0, 1e-5, 1e-6, -SOFTMAX_SHIFT])
        build_p1(nc, P, io=dict(x=x, wq=wq, wk=wk, wv=wv, wo=wo, bout_t=bout_t,
                                ns_t=ns_t, ident=ident, x1=x1_h),
                 pref="a_", consts=False)
        P.new_epoch()
        build_p2(nc, P, io=dict(x1=x1_h, Ws=Ws, bs_t=bs_t, ident=ident,
                                probs=probs_h, cc_in=cc_in_t.ap()),
                 pref="b_", consts=False)
        P.new_epoch()

        def coll(q):
            q.collective_compute("AllGather", OP.bypass,
                                 replica_groups=[list(range(NCORES))],
                                 ins=[cc_in_t.ap()],
                                 outs=[cc_out_t.ap()]).then_inc(cc_sem)
            q.wait_ge(cc_sem, 1)
            return q.nop()
        P.op("gpsimd", coll)
        build_p3(nc, P, io=dict(x1=x1_h, pb=cc_out_t.ap(), pown=probs_h, W1=W1,
                                W2=W2, b1c=b1c, b2t=b2t, sel8=sel8, sel8T=sel8T,
                                ones1=ones1, ident=ident, x2=x2_h),
                 pref="c_", consts=False)
        P.new_epoch()
        build_p4(nc, P, io=dict(x2=x2_h, Wq=Wq, Wk=Wk, Wv=Wv, Wo=Wo, bqc=bqc,
                                bkb=bkb, bvb=bvb, bob=bob, ones1=ones1,
                                ident=ident, x3=x3_h),
                 pref="d_", consts=False)
        P.new_epoch()
        build_p5(nc, P, io=dict(x3=x3_h, wg=wg, wu=wu, wd=wd, ident=ident,
                                out=out),
                 pref="e_", consts=False)
        P.build(sems)
    return nc


def kernel(**inp):
    x = _f(inp["L0_attn_x"])
    cores = list(range(NCORES))
    sel = np.zeros((128, 8), np.float32)
    for p in range(128):
        sel[p, (p % 16) // 2] = 1.0
    nc = build_fused()
    m = {"ident": _IDENT, "ones1": np.ones((1, 128), np.float32),
         "wq": _f(inp["L0_attn_wq"]).reshape(D, D),
         "wk": _f(inp["L0_attn_wk"]).reshape(D, D),
         "wv": _f(inp["L0_attn_wv"]).reshape(D, D),
         "wo": _f(inp["L0_attn_w_out"]).reshape(D, D),
         "bout_t": _tile128(inp["L0_attn_b_out"]),
         "ns_t": np.ascontiguousarray(_f(inp["L0_attn_norm_scale"]).reshape(8, 128).T),
         "Ws": _f(inp["L0_ffn_Ws"]), "bs_t": _tile128(inp["L0_ffn_bs"]),
         "W1": _f(inp["L0_ffn_W1"]).reshape(8 * D, FF),
         "W2": _f(inp["L0_ffn_W2"]).reshape(8 * FF, D),
         "b1c": np.ascontiguousarray(_f(inp["L0_ffn_b1"]).reshape(8, 32, 128)
                                     .transpose(0, 2, 1).reshape(8 * 128, 32)),
         "b2t": np.ascontiguousarray(np.repeat(_f(inp["L0_ffn_b2"])[:, None, :],
                                               128, axis=1).reshape(8 * 128, D)),
         "sel8": sel, "sel8T": np.ascontiguousarray(sel.T),
         "Wq": _f(inp["L1_attn_Wq"]), "Wk": _f(inp["L1_attn_Wk"]),
         "Wv": _f(inp["L1_attn_Wv"]), "Wo": _f(inp["L1_attn_Wo"]),
         "bqc": np.ascontiguousarray(_f(inp["L1_attn_bq"]).reshape(8, 128).T),
         "bkb": _tile128(inp["L1_attn_bk"]), "bvb": _tile128(inp["L1_attn_bv"]),
         "bob": _tile128(inp["L1_attn_bo"]),
         "wg": _f(inp["L1_ffn_w_gate"]), "wu": _f(inp["L1_ffn_w_up"]),
         "wd": _f(inp["L1_ffn_w_down"])}
    r = run_bass_kernel_spmd(nc, [dict(m, x=_f(x[c])) for c in cores],
                             core_ids=cores)
    return np.stack([r.results[c]["out"] for c in cores]).astype(np.float32)


def kernel_unfused(**inp):
    x = _f(inp["L0_attn_x"])
    cores = list(range(NCORES))

    nc1 = build_p1()
    m1 = {"wq": _f(inp["L0_attn_wq"]).reshape(D, D),
          "wk": _f(inp["L0_attn_wk"]).reshape(D, D),
          "wv": _f(inp["L0_attn_wv"]).reshape(D, D),
          "wo": _f(inp["L0_attn_w_out"]).reshape(D, D),
          "bout_t": _tile128(inp["L0_attn_b_out"]),
          "ns_t": np.ascontiguousarray(_f(inp["L0_attn_norm_scale"]).reshape(8, 128).T),
          "ident": _IDENT}
    r1 = run_bass_kernel_spmd(nc1, [dict(m1, x=_f(x[c])) for c in cores],
                              core_ids=cores)
    x1 = np.stack([r1.results[c]["x1"] for c in cores])

    nc2 = build_p2()
    m2 = {"Ws": _f(inp["L0_ffn_Ws"]), "bs_t": _tile128(inp["L0_ffn_bs"]),
          "ident": _IDENT}
    r2 = run_bass_kernel_spmd(nc2, [dict(m2, x1=_f(x1[c])) for c in cores],
                              core_ids=cores)
    probs = np.stack([r2.results[c]["probs"] for c in cores])

    probs_T = np.ascontiguousarray(probs.reshape(NCORES * S, 8).T)
    sel = np.zeros((128, 8), np.float32)
    for p in range(128):
        sel[p, p // 16] = 1.0
    nc3 = build_p3()
    m3 = {"pb": np.ascontiguousarray(probs_T.reshape(128, 2048)),
          "W1": _f(inp["L0_ffn_W1"]).reshape(8 * D, FF),
          "W2": _f(inp["L0_ffn_W2"]).reshape(8 * FF, D),
          "b1c": np.ascontiguousarray(_f(inp["L0_ffn_b1"]).reshape(8, 32, 128)
                                      .transpose(0, 2, 1).reshape(8 * 128, 32)),
          "b2t": np.ascontiguousarray(np.repeat(_f(inp["L0_ffn_b2"])[:, None, :],
                                                128, axis=1).reshape(8 * 128, D)),
          "sel8": sel, "sel8T": np.ascontiguousarray(sel.T),
          "ones1": np.ones((1, 128), np.float32), "ident": _IDENT}
    r3 = run_bass_kernel_spmd(
        nc3, [dict(m3, x1=_f(x1[c]), pown=_f(probs[c])) for c in cores],
        core_ids=cores)
    x2 = np.stack([r3.results[c]["x2"] for c in cores])

    nc4 = build_p4()
    m4 = {"Wq": _f(inp["L1_attn_Wq"]), "Wk": _f(inp["L1_attn_Wk"]),
          "Wv": _f(inp["L1_attn_Wv"]), "Wo": _f(inp["L1_attn_Wo"]),
          "bqc": np.ascontiguousarray(_f(inp["L1_attn_bq"]).reshape(8, 128).T),
          "bkb": _tile128(inp["L1_attn_bk"]), "bvb": _tile128(inp["L1_attn_bv"]),
          "bob": _tile128(inp["L1_attn_bo"]),
          "ones1": np.ones((1, 128), np.float32), "ident": _IDENT}
    r4 = run_bass_kernel_spmd(nc4, [dict(m4, x2=_f(x2[c])) for c in cores],
                              core_ids=cores)
    x3 = np.stack([r4.results[c]["x3"] for c in cores])

    nc5 = build_p5()
    m5 = {"wg": _f(inp["L1_ffn_w_gate"]), "wu": _f(inp["L1_ffn_w_up"]),
          "wd": _f(inp["L1_ffn_w_down"]), "ident": _IDENT}
    r5 = run_bass_kernel_spmd(nc5, [dict(m5, x3=_f(x3[c])) for c in cores],
                              core_ids=cores)
    return np.stack([r5.results[c]["out"] for c in cores]).astype(np.float32)


# version 04
# speedup vs baseline: 1.2700x; 1.2700x over previous
import numpy as np
from contextlib import ExitStack

import concourse.bass as bass
import concourse.mybir as mybir
from concourse.bass_utils import run_bass_kernel_spmd

F32 = mybir.dt.float32
BF16 = mybir.dt.bfloat16
AF = mybir.ActivationFunctionType
OP = mybir.AluOpType
AX = mybir.AxisListType

NCORES = 8
S = 4096
D = 1024
NT = S // 128
DCH = D // 128
H = 16
DH = 64
SOFTMAX_SHIFT = 20.0
P1_LIMIT = None
GC0 = 0.7978845608028654
GC2 = GC0 * 0.044715


class Prog:

    def __init__(self, nc):
        self.nc = nc
        self.epochs = [[]]
        self.totals = [0]

    @property
    def total(self):
        return self.totals[-1]

    def new_epoch(self):
        if self.epochs[-1]:
            self.epochs.append([])
            self.totals.append(0)

    def op(self, eng, fn, inc=1):
        self.epochs[-1].append((eng, "op", fn, self.totals[-1], inc))
        self.totals[-1] += inc

    def dma(self, eng, pairs, **kw):
        n = len(pairs)
        self.epochs[-1].append((eng, "dma", (pairs, kw), self.totals[-1], 16 * n))
        self.totals[-1] += 16 * n

    def build(self, sems):
        nc = self.nc
        epochs = self.epochs
        totals = self.totals

        def runner(q, name):
            for ei, steps in enumerate(epochs):
                first = True
                for (eng, kind, payload, wait, inc) in steps:
                    if eng != name:
                        continue
                    if first and ei > 0:
                        q.wait_ge(sems[ei - 1], totals[ei - 1])
                    first = False
                    if wait > 0:
                        q.wait_ge(sems[ei], wait)
                    if kind == "dma":
                        pairs, kw = payload
                        for (o, i) in pairs:
                            q.dma_start(out=o, in_=i, **kw).then_inc(sems[ei], 16)
                    else:
                        last = payload(q)
                        last.then_inc(sems[ei], inc)
            if name == "sync":
                if len(epochs) > 1:
                    q.wait_ge(sems[len(epochs) - 2], totals[-2])
                q.wait_ge(sems[len(epochs) - 1], totals[-1])

        with nc.Block() as block:
            @block.sync
            def _(q):
                runner(q, "sync")

            @block.scalar
            def _(q):
                runner(q, "scalar")

            @block.vector
            def _(q):
                runner(q, "vector")

            @block.tensor
            def _(q):
                runner(q, "tensor")

            @block.gpsimd
            def _(q):
                runner(q, "gpsimd")



def _const_pool(nc, P, es, vals):
    vals = sorted(set(float(v) for v in vals))
    cp = es.enter_context(nc.sbuf_tensor("cpool", [128, max(len(vals), 1)], F32))
    def fill(q):
        last = None
        for j, v in enumerate(vals):
            last = q.memset(cp[:, j:j + 1], v)
        return last
    P.op("vector", fill)
    for j, v in enumerate(vals):
        nc.const_aps.register(F32, v, cp[:, j:j + 1])
    return cp

def _ln_tile(P, xt, ln, sq, st, eps=1e-5):
    P.op("vector", lambda q: q.reduce_sum(out=st[:, 0:1], in_=xt, axis=AX.X))
    P.op("scalar", lambda q: q.mul(st[:, 1:2], st[:, 0:1], -1.0 / D))
    P.op("scalar", lambda q: q.activation(out=ln, in_=xt, func=AF.Identity,
                                          bias=st[:, 1:2], scale=1.0))
    P.op("scalar", lambda q: q.square(sq, ln))
    P.op("vector", lambda q: q.reduce_sum(out=st[:, 2:3], in_=sq, axis=AX.X))
    P.op("scalar", lambda q: q.activation(out=st[:, 3:4], in_=st[:, 2:3], func=AF.Sqrt,
                                          bias=float(eps), scale=1.0 / D))
    P.op("vector", lambda q: q.reciprocal(out=st[:, 4:5], in_=st[:, 3:4]))
    P.op("vector", lambda q: q.tensor_scalar(out=ln, in0=ln,
                                             scalar1=st[:, 4:5], scalar2=None,
                                             op0=OP.mult))


def _transpose_tile(P, src_bf, ptb, dst_bf, idb, nchunk=8):
    def mm(q):
        last = None
        for c in range(nchunk):
            last = q.transpose(out=ptb[:, 128 * c:128 * (c + 1)],
                               in_=src_bf[:, 128 * c:128 * (c + 1)],
                               identity=idb[:, :])
        return last
    P.op("tensor", mm)
    P.op("scalar", lambda q: q.copy(dst_bf, ptb[:, 0:128 * nchunk]))



def build_p1(nc=None, P=None, io=None, pref="", consts=True):
    standalone = nc is None
    if standalone:
        nc = bass.Bass("TRN2", target_bir_lowering=False)
        P = Prog(nc)
    io = io or {}
    def dram(name, shape, dt_, kind=None):
        if name in io:
            return io[name]
        if kind is None:
            return nc.dram_tensor(pref + name, shape, dt_).ap()
        return nc.dram_tensor(pref + name, shape, dt_, kind=kind).ap()
    x = dram("x", [S, D], F32, "ExternalInput")
    wq = dram("wq", [D, D], F32, "ExternalInput")
    wk = dram("wk", [D, D], F32, "ExternalInput")
    wv = dram("wv", [D, D], F32, "ExternalInput")
    wo = dram("wo", [D, D], F32, "ExternalInput")
    bout_t = dram("bout_t", [128, D], F32, "ExternalInput")
    ns_t = dram("ns_t", [128, 8], F32, "ExternalInput")
    ident = dram("ident", [128, 128], F32, "ExternalInput")
    x1 = dram("x1", [S, D], F32, "ExternalOutput")

    qT_h = dram("qT_h", [NT * 64, 2048], BF16)
    kT_h = dram("kT_h", [NT * 64, 2048], BF16)
    v_h = dram("v_h", [NT * 128, H * 65], BF16)

    with ExitStack() as es:
        sem = es.enter_context(nc.semaphore(pref + "m")) if standalone else None
        sb = lambda name, shape, dt=F32: es.enter_context(nc.sbuf_tensor(pref + name, shape, dt))
        ps = lambda name, shape, dt=F32: es.enter_context(nc.psum_tensor(pref + name, shape, dt))

        if consts:
            _const_pool(nc, P, es, [0.0, 1e-5, 1e-6, -SOFTMAX_SHIFT])
        wqb = sb("wqb", [128, DCH * D], BF16)
        wkb = sb("wkb", [128, DCH * D], BF16)
        wvb = sb("wvb", [128, DCH * D], BF16)
        wob = sb("wob", [128, DCH * D], BF16)
        bob = sb("bob", [128, D])
        nsc = sb("nsc", [128, 8])
        idf = sb("idf", [128, 128])
        idb = sb("idb", [128, 128], BF16)
        stg = sb("stg", [128, D])
        xt = sb("xt", [128, D])
        lnb = sb("ln", [128, D])
        sq = sb("sq", [128, D])
        stt = sb("st", [128, 8])
        hb = sb("hb", [128, D], BF16)
        hT = sb("hT", [128, D], BF16)
        qTt = sb("qTt", [128, D], BF16)
        kTt = sb("kTt", [128, D], BF16)
        vst = sb("vst", [128, H * 65], BF16)
        qTc = sb("qTc", [64, 4096], BF16)
        ktc = sb("ktc", [64, 2048], BF16)
        vtc = sb("vtc", [128, H * 65], BF16)
        PTk = sb("PTk", [128, H * 256], BF16)
        oq65 = sb("oq65", [128, H * 65])
        rcol = sb("rcol", [128, H])
        oqb = sb("oqb", [128, D], BF16)
        oqT = sb("oqT", [128, D], BF16)
        xr = sb("xr", [128, D])
        res = sb("res", [128, D])
        ptb = ps("ptb", [128, D], BF16)
        pacc = ps("pacc", [128, 512])
        pav0 = ps("pav0", [128, 455])
        pav1 = ps("pav1", [128, 455])
        pav2 = ps("pav2", [128, 130])
        pav3 = ps("pav3", [128, 455])
        pav4 = ps("pav4", [128, 455])
        pav5 = ps("pav5", [128, 130])
        pav = [[pav0, pav1, pav2], [pav3, pav4, pav5]]

        def av_slice(qs_, h):
            si = min(h // 7, 2)
            loc = h - 7 * si
            return pav[qs_][si][:, 65 * loc:65 * loc + 65]

        P.dma("sync", [(idf[:, :], ident), (bob[:, :], bout_t), (nsc[:, :], ns_t)])
        P.op("vector", lambda q: q.tensor_copy(out=idb[:, :], in_=idf[:, :]))
        for dc in range(DCH):
            for (src, dst, sc) in ((wq, wqb, 0.125), (wk, wkb, 1.0), (wv, wvb, 1.0),
                                   (wo, wob, None)):
                P.dma("sync", [(stg[:, :], src[128 * dc:128 * (dc + 1), :])])
                if sc is not None:
                    P.op("vector", lambda q, sc=sc, dc=dc: q.tensor_scalar(
                        out=stg[:, :], in0=stg[:, :], scalar1=nsc[:, dc:dc + 1],
                        scalar2=sc, op0=OP.mult, op1=OP.mult))
                P.op("vector", lambda q, dst=dst, dc=dc: q.tensor_copy(
                    out=dst[:, D * dc:D * (dc + 1)], in_=stg[:, :]))

        def set_ones(q):
            last = None
            for h in range(H):
                last = q.memset(vst[:, 65 * h + 64:65 * h + 65], 1.0)
            return last
        P.op("vector", set_ones)

        for i in range(NT if P1_LIMIT != "W" else 0):
            P.dma("sync", [(xt[:, :], x[128 * i:128 * (i + 1), :])])
            _ln_tile(P, xt[:, :], lnb[:, :], sq[:, :], stt)
            P.op("scalar", lambda q: q.square(sq[:, :], lnb[:, :]))
            P.op("vector", lambda q: q.reduce_sum(out=stt[:, 5:6], in_=sq[:, :], axis=AX.X))
            P.op("scalar", lambda q: q.activation(out=stt[:, 6:7], in_=stt[:, 5:6],
                                                  func=AF.Sqrt, bias=1e-6, scale=1.0 / D))
            P.op("vector", lambda q: q.reciprocal(out=stt[:, 7:8], in_=stt[:, 6:7]))
            P.op("vector", lambda q: q.tensor_scalar(out=hb[:, :], in0=lnb[:, :],
                                                     scalar1=stt[:, 7:8], scalar2=None,
                                                     op0=OP.mult))
            _transpose_tile(P, hb, ptb, hT[:, :], idb)
            for (wb, dst) in ((wqb, qTt), (wkb, kTt)):
                for og in range(2):
                    def mmq(q, wb=wb, og=og):
                        last = None
                        for oc2 in range(4):
                            oc = 4 * og + oc2
                            for dc2 in range(DCH):
                                last = q.matmul(pacc[:, 128 * oc2:128 * (oc2 + 1)],
                                                wb[:, D * dc2 + 128 * oc:D * dc2 + 128 * (oc + 1)],
                                                hT[:, 128 * dc2:128 * (dc2 + 1)],
                                                start=(dc2 == 0), stop=(dc2 == DCH - 1))
                        return last
                    P.op("tensor", mmq)
                    P.op("scalar", lambda q, dst=dst, og=og:
                         q.copy(dst[:, 512 * og:512 * (og + 1)], pacc[:, 0:512]))
            for half in range(2):
                def mmv(q, half=half):
                    last = None
                    for dc2 in range(DCH):
                        last = q.matmul(pacc[:, 0:512],
                                        hT[:, 128 * dc2:128 * (dc2 + 1)],
                                        wvb[:, D * dc2 + 512 * half:D * dc2 + 512 * (half + 1)],
                                        start=(dc2 == 0), stop=(dc2 == DCH - 1))
                    return last
                P.op("tensor", mmv)
                def evacv(q, half=half):
                    last = None
                    for hh in range(8):
                        h = 8 * half + hh
                        last = q.copy(vst[:, 65 * h:65 * h + 64],
                                      pacc[:, 64 * hh:64 * (hh + 1)])
                    return last
                P.op("scalar", evacv)
            def hm_dst(dr, i, par):
                return dr[64 * i:64 * (i + 1), :].rearrange(
                    "p (c two j) -> p c two j", two=2, j=128)[:, :, par:par + 1, :]
            def hm_src(t, par):
                return t[64 * par:64 * (par + 1), :].rearrange(
                    "p (c one j) -> p c one j", one=1, j=128)
            P.dma("gpsimd", [(hm_dst(qT_h, i, 0), hm_src(qTt, 0)),
                             (hm_dst(qT_h, i, 1), hm_src(qTt, 1)),
                             (hm_dst(kT_h, i, 0), hm_src(kTt, 0)),
                             (hm_dst(kT_h, i, 1), hm_src(kTt, 1)),
                             (v_h[128 * i:128 * (i + 1), :], vst[:, :])])

        for jc in range(NT // 2 if (P1_LIMIT is None or P1_LIMIT.startswith("B")) else 0):
            t0 = 2 * jc
            def clr_pav(q):
                last = None
                for qs_ in range(2):
                    for si in range(3):
                        last = q.memset(pav[qs_][si][:, :], 0.0)
                return last
            P.op("vector", clr_pav)
            P.dma("sync", [(qTc[0:64, 2048 * t:2048 * (t + 1)],
                            qT_h[64 * (t0 + t):64 * (t0 + t + 1), :])
                           for t in range(2)])
            for kt in range(NT):
                P.dma("sync", [(ktc[0:64, :], kT_h[64 * kt:64 * (kt + 1), :])])
                P.dma("gpsimd", [(vtc[:, :], v_h[128 * kt:128 * (kt + 1), :])])
                for hp in range(8):
                    def mmS(q, hp=hp):
                        last = None
                        for ho in range(2):
                            h = 2 * hp + ho
                            for tt in range(2):
                                last = q.matmul(
                                    pacc[:, 256 * ho + 128 * tt:256 * ho + 128 * (tt + 1)],
                                    ktc[0:64, 128 * h:128 * (h + 1)],
                                    qTc[0:64, 2048 * tt + 128 * h:2048 * tt + 128 * (h + 1)],
                                    start=True, stop=True)
                        return last
                    P.op("tensor", mmS)
                    if P1_LIMIT in ("B0", "B0a"):
                        continue
                    P.op("scalar", lambda q, hp=hp: q.activation(
                        out=PTk[:, 512 * hp:512 * (hp + 1)], in_=pacc[:, 0:512],
                        func=AF.Exp, bias=-SOFTMAX_SHIFT, scale=1.0))
                if P1_LIMIT in ("B0", "B0a", "B1"):
                    continue
                def mmav(q, kt=kt):
                    last = None
                    for qs_ in range(2):
                        for h in range(H):
                            last = q.matmul(av_slice(qs_, h),
                                            PTk[:, 256 * h + 128 * qs_:256 * h + 128 * (qs_ + 1)],
                                            vtc[:, 65 * h:65 * (h + 1)],
                                            start=False, stop=(kt == NT - 1),
                                            skip_group_check=True)
                    return last
                P.op("tensor", mmav)
            for qs_ in range(2):
                if P1_LIMIT in ("B0", "B0a", "B1", "B2"):
                    continue
                def evac(q, qs_=qs_):
                    last = None
                    for h in range(H):
                        last = q.copy(oq65[:, 65 * h:65 * (h + 1)], av_slice(qs_, h))
                    return last
                P.op("scalar", evac)
                def recips(q):
                    last = None
                    for h in range(H):
                        last = q.reciprocal(out=rcol[:, h:h + 1],
                                            in_=oq65[:, 65 * h + 64:65 * h + 65])
                    return last
                P.op("vector", recips)
                def divs(q):
                    last = None
                    for h in range(H):
                        last = q.tensor_scalar(out=oqb[:, 64 * h:64 * (h + 1)],
                                               in0=oq65[:, 65 * h:65 * h + 64],
                                               scalar1=rcol[:, h:h + 1], scalar2=None,
                                               op0=OP.mult)
                    return last
                P.op("vector", divs)
                _transpose_tile(P, oqb, ptb, oqT[:, :], idb)
                ti = 2 * jc + qs_
                P.dma("sync", [(xr[:, :], x[128 * ti:128 * (ti + 1), :])])
                for half in range(2):
                    def mmo(q, half=half):
                        last = None
                        for dc2 in range(DCH):
                            last = q.matmul(pacc[:, 0:512],
                                            oqT[:, 128 * dc2:128 * (dc2 + 1)],
                                            wob[:, D * dc2 + 512 * half:D * dc2 + 512 * (half + 1)],
                                            start=(dc2 == 0), stop=(dc2 == DCH - 1))
                        return last
                    P.op("tensor", mmo)
                    P.op("vector", lambda q, half=half: q.tensor_tensor(
                        out=res[:, 512 * half:512 * (half + 1)], in0=pacc[:, 0:512],
                        in1=bob[:, 512 * half:512 * (half + 1)], op=OP.add))
                P.op("vector", lambda q: q.tensor_tensor(
                    out=res[:, :], in0=res[:, :], in1=xr[:, :], op=OP.add))
                P.dma("gpsimd", [(x1[128 * ti:128 * (ti + 1), :], res[:, :])])

        if standalone:
            P.build([sem])
    return nc



def build_p2(nc=None, P=None, io=None, pref="", consts=True):
    standalone = nc is None
    if standalone:
        nc = bass.Bass("TRN2", target_bir_lowering=False)
        P = Prog(nc)
    io = io or {}
    def dram(name, shape, dt_, kind=None):
        if name in io:
            return io[name]
        if kind is None:
            return nc.dram_tensor(pref + name, shape, dt_).ap()
        return nc.dram_tensor(pref + name, shape, dt_, kind=kind).ap()
    x1 = dram("x1", [S, D], F32, "ExternalInput")
    Ws = dram("Ws", [D, 8], F32, "ExternalInput")
    bs_t = dram("bs_t", [128, 8], F32, "ExternalInput")
    ident = dram("ident", [128, 128], F32, "ExternalInput")
    probs = dram("probs", [S, 8], F32, "ExternalOutput")
    cc_in = io.get("cc_in")

    with ExitStack() as es:
        sem = es.enter_context(nc.semaphore(pref + "m")) if standalone else None
        sb = lambda name, shape, dt=F32: es.enter_context(nc.sbuf_tensor(pref + name, shape, dt))
        if consts:
            _const_pool(nc, P, es, [0.0, 1e-5])
        wsb = sb("wsb", [128, 64], BF16)
        wss = sb("wss", [128, 64])
        bsb = sb("bsb", [128, 8])
        idf = sb("idf", [128, 128])
        idb = sb("idb", [128, 128], BF16)
        xt = sb("xt", [128, D])
        lnb = sb("ln", [128, D])
        sq = sb("sq", [128, D])
        stt = sb("st", [128, 8])
        hb = sb("hb", [128, D], BF16)
        hT = sb("hT", [128, D], BF16)
        lg = sb("lg", [128, 8])
        pr = sb("pr", [128, 8])
        prT = sb("prT", [8, S]) if cc_in is not None else None
        ptb = es.enter_context(nc.psum_tensor(pref + "ptb", [128, D], BF16))
        plg = es.enter_context(nc.psum_tensor(pref + "plg", [128, 8], F32))
        pprT = (es.enter_context(nc.psum_tensor(pref + "pprT", [8, 128], F32))
                if cc_in is not None else None)

        P.dma("sync", [(idf[:, :], ident), (bsb[:, :], bs_t)] +
                      [(wss[:, 8 * dc:8 * (dc + 1)], Ws[128 * dc:128 * (dc + 1), :])
                       for dc in range(DCH)])
        P.op("vector", lambda q: q.tensor_copy(out=idb[:, :], in_=idf[:, :]))
        P.op("vector", lambda q: q.tensor_copy(out=wsb[:, :], in_=wss[:, :]))

        for i in range(NT):
            P.dma("sync", [(xt[:, :], x1[128 * i:128 * (i + 1), :])])
            _ln_tile(P, xt[:, :], lnb[:, :], sq[:, :], stt)
            P.op("vector", lambda q: q.tensor_copy(out=hb[:, :], in_=lnb[:, :]))
            _transpose_tile(P, hb, ptb, hT[:, :], idb)
            def mml(q):
                last = None
                for dc in range(DCH):
                    last = q.matmul(plg[:, 0:8], hT[:, 128 * dc:128 * (dc + 1)],
                                    wsb[:, 8 * dc:8 * (dc + 1)],
                                    start=(dc == 0), stop=(dc == DCH - 1))
                return last
            P.op("tensor", mml)
            P.op("vector", lambda q: q.tensor_tensor(out=lg[:, :], in0=plg[:, 0:8],
                                                     in1=bsb[:, :], op=OP.add))
            P.op("vector", lambda q: q.reduce_max(out=stt[:, 0:1], in_=lg[:, :],
                                                  axis=AX.X, negate=True))
            P.op("scalar", lambda q: q.activation(out=pr[:, :], in_=lg[:, :],
                                                  func=AF.Exp, bias=stt[:, 0:1], scale=1.0))
            P.op("vector", lambda q: q.reduce_sum(out=stt[:, 1:2], in_=pr[:, :], axis=AX.X))
            P.op("vector", lambda q: q.reciprocal(out=stt[:, 2:3], in_=stt[:, 1:2]))
            P.op("vector", lambda q: q.tensor_scalar(out=pr[:, :], in0=pr[:, :],
                                                     scalar1=stt[:, 2:3], scalar2=None,
                                                     op0=OP.mult))
            P.dma("gpsimd", [(probs[128 * i:128 * (i + 1), :], pr[:, :])])
            if cc_in is not None:
                P.op("tensor", lambda q: q.transpose(out=pprT[:, :], in_=pr[:, :],
                                                     identity=idf[:, :]))
                P.op("scalar", lambda q, i=i: q.copy(prT[:, 128 * i:128 * (i + 1)],
                                                     pprT[:, :]))
        if cc_in is not None:
            P.dma("gpsimd", [(cc_in.rearrange("(p h) c -> p (h c)", h=2), prT[:, :])])
        if standalone:
            P.build([sem])
    return nc



BISECT_ITERS = 38
TOPK = 4096
FF = 4096


def build_p3(nc=None, P=None, io=None, pref="", consts=True):
    standalone = nc is None
    if standalone:
        nc = bass.Bass("TRN2", target_bir_lowering=False)
        P = Prog(nc)
    io = io or {}
    def dram(name, shape, dt_, kind=None):
        if name in io:
            return io[name]
        if kind is None:
            return nc.dram_tensor(pref + name, shape, dt_).ap()
        return nc.dram_tensor(pref + name, shape, dt_, kind=kind).ap()
    x1 = dram("x1", [S, D], F32, "ExternalInput")
    pb_in = dram("pb", [128, 2048], F32, "ExternalInput")
    pown = dram("pown", [S, 8], F32, "ExternalInput")
    W1 = dram("W1", [8 * D, FF], F32, "ExternalInput")
    W2 = dram("W2", [8 * FF, D], F32, "ExternalInput")
    b1c = dram("b1c", [8 * 128, 32], F32, "ExternalInput")
    b2t = dram("b2t", [8 * 128, D], F32, "ExternalInput")
    sel8 = dram("sel8", [128, 8], F32, "ExternalInput")
    sel8T = dram("sel8T", [8, 128], F32, "ExternalInput")
    ones1 = dram("ones1", [1, 128], F32, "ExternalInput")
    ident = dram("ident", [128, 128], F32, "ExternalInput")
    x2 = dram("x2", [S, D], F32, "ExternalOutput")

    lnT_h = dram("lnT_h", [NT * 128, D], BF16)

    TC = 256
    NTC = S // TC

    with ExitStack() as es:
        sem = es.enter_context(nc.semaphore(pref + "m")) if standalone else None
        sb = lambda name, shape, dt=F32: es.enter_context(nc.sbuf_tensor(pref + name, shape, dt))
        ps = lambda name, shape, dt=F32: es.enter_context(nc.psum_tensor(pref + name, shape, dt))

        if consts:
            _const_pool(nc, P, es, [0.0, 1e-5])
        big = sb("big", [128, 5120])
        idf = sb("idf", [128, 128])
        idb = sb("idb", [128, 128], BF16)
        w_all = sb("w_all", [128, NT * 8])
        hb = sb("hb", [128, D], BF16)
        hT = sb("hT", [128, D], BF16)
        stt = sb("st", [128, 8])
        bis = sb("bis", [128, 8])
        sel_s = sb("sel_s", [128, 8])
        selT_s = sb("selT_s", [8, 128])
        on_s = sb("on_s", [1, 128])
        c8 = sb("c8", [8, 2])
        lo8 = sb("lo8", [1, 8])
        losb = sb("losb", [128, 8])
        pot = sb("pot", [128, 8])
        w1b = sb("w1b", [128, DCH * FF], BF16)
        w2b = sb("w2b", [128, 32 * D], BF16)
        lnc = sb("lnc", [128, 2 * D], BF16)
        atb = sb("atb", [128, 32 * TC], BF16)
        b1s = sb("b1s", [128, 32])
        b2s = sb("b2s", [128, D])
        ptb = ps("ptb", [128, D], BF16)
        ptiny = ps("ptiny", [128, 16])
        pA = ps("pA", [128, 512])
        pY = ps("pY", [128, D])

        xt = big[:, 0:1024]
        lnv = big[:, 1024:2048]
        sqv = big[:, 2048:3072]
        pbs = big[:, 0:2048]
        msk = big[:, 2048:4096]
        wst = big[:, 0:1024]
        ga = big[:, 1024:1280]
        gb = big[:, 1280:1536]
        gc = big[:, 1536:1792]
        acc = big[:, 2048:3072]
        yt = big[:, 3072:4096]

        P.dma("sync", [(idf[:, :], ident)])
        P.op("vector", lambda q: q.tensor_copy(out=idb[:, :], in_=idf[:, :]))

        for i in range(NT):
            P.dma("sync", [(xt, x1[128 * i:128 * (i + 1), :])])
            _ln_tile(P, xt, lnv, sqv, stt)
            P.op("vector", lambda q: q.tensor_copy(out=hb[:, :], in_=lnv))
            _transpose_tile(P, hb, ptb, hT[:, :], idb)
            P.dma("gpsimd", [(lnT_h[128 * i:128 * (i + 1), :], hT[:, :])])

        P.dma("sync", [(pbs, pb_in), (sel_s[:, :], sel8), (selT_s[:, :], sel8T),
                       (on_s[:, :], ones1)])
        LO, HI, MID, CNT = 0, 1, 2, 3
        P.op("vector", lambda q: q.memset(bis[:, LO:LO + 1], 0.0))
        P.op("vector", lambda q: q.memset(bis[:, HI:HI + 1], 2.0))
        cl, ch, nl, nh = 0, 1, 4, 5
        for _ in range(BISECT_ITERS):
            P.op("vector", lambda q, cl=cl, ch=ch: q.tensor_tensor(
                out=bis[:, MID:MID + 1], in0=bis[:, cl:cl + 1],
                in1=bis[:, ch:ch + 1], op=OP.add))
            P.op("scalar", lambda q: q.mul(bis[:, MID:MID + 1], bis[:, MID:MID + 1], 0.5))
            P.op("vector", lambda q: q.tensor_scalar(
                out=msk, in0=pbs, scalar1=bis[:, MID:MID + 1],
                scalar2=None, op0=OP.is_ge))
            P.op("vector", lambda q: q.reduce_sum(
                out=bis[:, CNT:CNT + 1], in_=msk, axis=AX.X))
            P.op("tensor", lambda q: q.matmul(ptiny[0:8, 0:1], sel_s[:, :],
                                              bis[:, CNT:CNT + 1], start=True, stop=True))
            P.op("scalar", lambda q: q.copy(c8[:, 0:1], ptiny[0:8, 0:1]))
            P.op("vector", lambda q: q.tensor_scalar(
                out=c8[:, 1:2], in0=c8[:, 0:1], scalar1=float(TOPK), scalar2=None,
                op0=OP.is_ge))
            P.op("tensor", lambda q: q.matmul(ptiny[:, 1:2], selT_s[:, :],
                                              c8[:, 1:2], start=True, stop=True))
            P.op("scalar", lambda q: q.copy(c8[:, 0:1] if False else losb[:, 0:1],
                                            ptiny[:, 1:2]))
            P.op("vector", lambda q, cl=cl: q.tensor_tensor(
                out=losb[:, 1:2], in0=bis[:, MID:MID + 1], in1=bis[:, cl:cl + 1],
                op=OP.subtract))
            P.op("vector", lambda q: q.tensor_tensor(
                out=losb[:, 1:2], in0=losb[:, 1:2], in1=losb[:, 0:1], op=OP.mult))
            P.op("vector", lambda q, cl=cl, nl=nl: q.tensor_tensor(
                out=bis[:, nl:nl + 1], in0=bis[:, cl:cl + 1], in1=losb[:, 1:2],
                op=OP.add))
            P.op("vector", lambda q, ch=ch: q.tensor_tensor(
                out=losb[:, 2:3], in0=bis[:, ch:ch + 1], in1=bis[:, MID:MID + 1],
                op=OP.subtract))
            P.op("vector", lambda q: q.tensor_tensor(
                out=losb[:, 2:3], in0=losb[:, 2:3], in1=losb[:, 0:1], op=OP.mult))
            P.op("vector", lambda q, nh=nh: q.tensor_tensor(
                out=bis[:, nh:nh + 1], in0=bis[:, MID:MID + 1], in1=losb[:, 2:3],
                op=OP.add))
            cl, nl = nl, cl
            ch, nh = nh, ch
        P.op("tensor", lambda q, cl=cl: q.matmul(ptiny[0:1, 2:10], bis[:, cl:cl + 1],
                                                 sel_s[:, :], start=True, stop=True))
        P.op("scalar", lambda q: q.mul(lo8[:, :], ptiny[0:1, 2:10], 1.0 / 16.0))
        P.op("tensor", lambda q: q.matmul(ptiny[:, 2:10], on_s[0:1, :],
                                          lo8[0:1, :], start=True, stop=True))
        P.op("scalar", lambda q: q.copy(losb[:, :], ptiny[:, 2:10]))
        for i in range(NT):
            P.dma("sync", [(pot[:, :], pown[128 * i:128 * (i + 1), :])])
            P.op("vector", lambda q: q.tensor_tensor(out=stt[:, :], in0=pot[:, :],
                                                     in1=losb[:, :], op=OP.is_ge))
            P.op("vector", lambda q, i=i: q.tensor_tensor(
                out=w_all[:, 8 * i:8 * (i + 1)], in0=pot[:, :], in1=stt[:, :],
                op=OP.mult))

        for e in range(8):
            P.dma("sync", [(b1s[:, :], b1c[128 * e:128 * (e + 1), :]),
                           (b2s[:, :], b2t[128 * e:128 * (e + 1), :])])
            for dc in range(DCH):
                for qt in range(4):
                    P.dma("sync", [(wst, W1[D * e + 128 * dc:D * e + 128 * (dc + 1),
                                            1024 * qt:1024 * (qt + 1)])])
                    P.op("vector", lambda q, dc=dc, qt=qt: q.tensor_copy(
                        out=w1b[:, FF * dc + 1024 * qt:FF * dc + 1024 * (qt + 1)],
                        in_=wst))
            for fc in range(32):
                P.dma("sync", [(wst, W2[FF * e + 128 * fc:FF * e + 128 * (fc + 1), :])])
                P.op("vector", lambda q, fc=fc: q.tensor_copy(
                    out=w2b[:, D * fc:D * (fc + 1)], in_=wst))
            for tc in range(NTC):
                P.dma("sync", [(lnc[:, D * t:D * (t + 1)],
                                lnT_h[128 * (2 * tc + t):128 * (2 * tc + t + 1), :])
                               for t in range(2)])
                for fc in range(32):
                    def mmA(q, fc=fc):
                        last = None
                        for tt in range(2):
                            for dc in range(DCH):
                                last = q.matmul(
                                    pA[:, 128 * tt:128 * (tt + 1)],
                                    w1b[:, FF * dc + 128 * fc:FF * dc + 128 * (fc + 1)],
                                    lnc[:, D * tt + 128 * dc:D * tt + 128 * (dc + 1)],
                                    start=(dc == 0), stop=(dc == DCH - 1),
                                    skip_group_check=True)
                        return last
                    P.op("tensor", mmA)
                    b1col = b1s[:, fc:fc + 1]
                    P.op("scalar", lambda q, b1col=b1col, fc=fc: q.activation(
                        out=atb[:, TC * fc:TC * (fc + 1)], in_=pA[:, 0:TC],
                        func=AF.Gelu_apprx_tanh, bias=b1col, scale=1.0))
                for ts in range(2):
                    ti = 2 * tc + ts
                    def mmY(q, ts=ts):
                        last = None
                        for fc in range(32):
                            for oh in range(2):
                                last = q.matmul(
                                    pY[:, 512 * oh:512 * (oh + 1)],
                                    atb[:, TC * fc + 128 * ts:TC * fc + 128 * (ts + 1)],
                                    w2b[:, D * fc + 512 * oh:D * fc + 512 * (oh + 1)],
                                    start=(fc == 0), stop=(fc == 31),
                                    skip_group_check=True)
                        return last
                    P.op("tensor", mmY)
                    src = x1 if e == 0 else x2
                    P.dma("sync", [(acc, src[128 * ti:128 * (ti + 1), :])])
                    P.op("vector", lambda q: q.tensor_tensor(
                        out=yt, in0=pY[:, 0:D], in1=b2s[:, :], op=OP.add))
                    P.op("vector", lambda q, ti=ti, e=e: q.tensor_scalar(
                        out=yt, in0=yt,
                        scalar1=w_all[:, 8 * ti + e:8 * ti + e + 1],
                        scalar2=None, op0=OP.mult))
                    P.op("vector", lambda q: q.tensor_tensor(
                        out=yt, in0=yt, in1=acc, op=OP.add))
                    P.dma("gpsimd", [(x2[128 * ti:128 * (ti + 1), :], yt)])
        if standalone:
            P.build([sem])
    return nc



def build_p4(nc=None, P=None, io=None, pref="", consts=True):
    standalone = nc is None
    if standalone:
        nc = bass.Bass("TRN2", target_bir_lowering=False)
        P = Prog(nc)
    io = io or {}
    def dram(name, shape, dt_, kind=None):
        if name in io:
            return io[name]
        if kind is None:
            return nc.dram_tensor(pref + name, shape, dt_).ap()
        return nc.dram_tensor(pref + name, shape, dt_, kind=kind).ap()
    x2 = dram("x2", [S, D], F32, "ExternalInput")
    Wq = dram("Wq", [D, D], F32, "ExternalInput")
    Wk = dram("Wk", [D, D], F32, "ExternalInput")
    Wv = dram("Wv", [D, D], F32, "ExternalInput")
    Wo = dram("Wo", [D, D], F32, "ExternalInput")
    bqc = dram("bqc", [128, 8], F32, "ExternalInput")
    bkb = dram("bkb", [128, D], F32, "ExternalInput")
    bvb = dram("bvb", [128, D], F32, "ExternalInput")
    bob = dram("bob", [128, D], F32, "ExternalInput")
    ones1 = dram("ones1", [1, 128], F32, "ExternalInput")
    ident = dram("ident", [128, 128], F32, "ExternalInput")
    x3 = dram("x3", [S, D], F32, "ExternalOutput")

    with ExitStack() as es:
        sem = es.enter_context(nc.semaphore(pref + "m")) if standalone else None
        sb = lambda name, shape, dt=F32: es.enter_context(nc.sbuf_tensor(pref + name, shape, dt))
        ps = lambda name, shape, dt=F32: es.enter_context(nc.psum_tensor(pref + name, shape, dt))

        if consts:
            _const_pool(nc, P, es, [0.0, 1e-5])
        wqT = sb("wqT", [128, DCH * D], BF16)
        wkT = sb("wkT", [128, DCH * D], BF16)
        wvT = sb("wvT", [128, DCH * D], BF16)
        woT = sb("woT", [128, DCH * D], BF16)
        qTr = sb("qTr", [128, DCH * S], BF16)
        bq_s = sb("bq_s", [128, 8])
        bk_s = sb("bk_s", [128, D])
        bv_s = sb("bv_s", [128, D])
        bo_s = sb("bo_s", [128, D])
        on_s = sb("on_s", [1, 128])
        onb = sb("onb", [128, 1], BF16)
        idf = sb("idf", [128, 128])
        idb = sb("idb", [128, 128], BF16)
        KVb = sb("KVb", [64, 8 * 130], BF16)
        xt = sb("xt", [128, D])
        lnb = sb("ln", [128, D])
        sq = sb("sq", [128, D])
        stt = sb("st", [128, 8])
        hb = sb("hb", [128, D], BF16)
        hT = sb("hT", [128, D], BF16)
        f1 = sb("f1", [128, 512])
        f2 = sb("f2", [128, 512])
        f3 = sb("f3", [128, 512])
        kb = sb("kb", [128, D], BF16)
        vb = sb("vb", [128, D], BF16)
        atb = sb("atb", [128, 8 * 512], BF16)
        qtmp = sb("qtmp", [64, 16 * 512], BF16)
        atm = sb("atm", [64, 512], BF16)
        res = sb("res", [128, D])
        ptb = ps("ptb", [128, D], BF16)
        pq = ps("pq", [128, 512])
        pkv0 = ps("pkv0", [64, 390])
        pkv1 = ps("pkv1", [64, 390])
        pkv2 = ps("pkv2", [64, 260])
        pbc = ps("pbc", [64, 512])
        po = ps("po", [128, D])
        pkv = (pkv0, pkv1, pkv2)
        pat = pq[0:65, :]

        def kv_slice(pair, col0, w):
            si = min(pair // 3, 2)
            loc = pair - 3 * si
            return pkv[si][:, 130 * loc + col0:130 * loc + col0 + w]

        P.dma("sync", [(idf[:, :], ident), (bq_s[:, :], bqc), (bk_s[:, :], bkb),
                       (bv_s[:, :], bvb), (bo_s[:, :], bob), (on_s[:, :], ones1)])
        P.op("vector", lambda q: q.tensor_copy(out=idb[:, :], in_=idf[:, :]))
        P.op("vector", lambda q: q.memset(onb[:, :], 1.0))

        for (wsrc, wdst) in ((Wq, wqT), (Wk, wkT), (Wv, wvT), (Wo, woT)):
            for r in range(8):
                P.dma("sync", [(xt[:, :], wsrc[128 * r:128 * (r + 1), :])])
                P.op("vector", lambda q: q.tensor_copy(out=hb[:, :], in_=xt[:, :]))
                def mmt(q):
                    last = None
                    for dc in range(DCH):
                        last = q.transpose(out=ptb[:, 128 * dc:128 * (dc + 1)],
                                           in_=hb[:, 128 * dc:128 * (dc + 1)],
                                           identity=idb[:, :])
                    return last
                P.op("tensor", mmt)
                def cpw(q, wdst=wdst, r=r):
                    last = None
                    for dc in range(DCH):
                        last = q.copy(wdst[:, D * dc + 128 * r:D * dc + 128 * (r + 1)],
                                      ptb[:, 128 * dc:128 * (dc + 1)])
                    return last
                P.op("scalar", cpw)

        def clr_pkv(q):
            last = None
            for t in pkv:
                last = q.memset(t[:, :], 0.0)
            return last
        P.op("vector", clr_pkv)
        for i in range(NT):
            P.dma("sync", [(xt[:, :], x2[128 * i:128 * (i + 1), :])])
            _ln_tile(P, xt[:, :], lnb[:, :], sq[:, :], stt)
            P.op("vector", lambda q: q.tensor_copy(out=hb[:, :], in_=lnb[:, :]))
            _transpose_tile(P, hb, ptb, hT[:, :], idb)
            for (wT, bias_b, dst, feat) in ((wkT, bk_s, kb, True),
                                            (wvT, bv_s, vb, False)):
                for half in range(2):
                    def mmkv(q, wT=wT, half=half):
                        last = None
                        for dc in range(DCH):
                            last = q.matmul(pq[:, 0:512],
                                            hT[:, 128 * dc:128 * (dc + 1)],
                                            wT[:, D * dc + 512 * half:D * dc + 512 * (half + 1)],
                                            start=(dc == 0), stop=(dc == DCH - 1))
                        return last
                    P.op("tensor", mmkv)
                    bs_ = bias_b[:, 512 * half:512 * (half + 1)]
                    if feat:
                        P.op("vector", lambda q, bs_=bs_: q.tensor_tensor(
                            out=f1[:, :], in0=pq[:, 0:512], in1=bs_, op=OP.add))
                        P.op("vector", lambda q: q.tensor_scalar(
                            out=f2[:, :], in0=f1[:, :], scalar1=0.0, scalar2=None,
                            op0=OP.min))
                        P.op("scalar", lambda q: q.activation(
                            out=f3[:, :], in_=f2[:, :], func=AF.Exp, bias=0.0, scale=1.0))
                        P.op("vector", lambda q: q.tensor_scalar(
                            out=f1[:, :], in0=f1[:, :], scalar1=0.0, scalar2=None,
                            op0=OP.max))
                        P.op("vector", lambda q, dst=dst, half=half: q.tensor_tensor(
                            out=dst[:, 512 * half:512 * (half + 1)], in0=f1[:, :],
                            in1=f3[:, :], op=OP.add))
                    else:
                        P.op("vector", lambda q, dst=dst, half=half, bs_=bs_:
                             q.tensor_tensor(out=dst[:, 512 * half:512 * (half + 1)],
                                             in0=pq[:, 0:512], in1=bs_, op=OP.add))
            for og in range(2):
                def mmq(q, og=og):
                    last = None
                    for oc2 in range(4):
                        oc = 4 * og + oc2
                        for dc in range(DCH):
                            last = q.matmul(pq[:, 128 * oc2:128 * (oc2 + 1)],
                                            wqT[:, D * dc + 128 * oc:D * dc + 128 * (oc + 1)],
                                            hT[:, 128 * dc:128 * (dc + 1)],
                                            start=(dc == 0), stop=(dc == DCH - 1))
                    return last
                P.op("tensor", mmq)
                for oc2 in range(4):
                    oc = 4 * og + oc2
                    pslice = pq[:, 128 * oc2:128 * (oc2 + 1)]
                    P.op("scalar", lambda q, pslice=pslice, oc=oc: q.activation(
                        out=f1[:, 0:128], in_=pslice, func=AF.Identity,
                        bias=bq_s[:, oc:oc + 1], scale=1.0))
                    P.op("vector", lambda q: q.tensor_scalar(
                        out=f2[:, 0:128], in0=f1[:, 0:128], scalar1=0.0,
                        scalar2=None, op0=OP.min))
                    P.op("scalar", lambda q: q.activation(
                        out=f3[:, 0:128], in_=f2[:, 0:128], func=AF.Exp,
                        bias=0.0, scale=1.0))
                    P.op("vector", lambda q: q.tensor_scalar(
                        out=f1[:, 0:128], in0=f1[:, 0:128], scalar1=0.0,
                        scalar2=None, op0=OP.max))
                    P.op("vector", lambda q, oc=oc, i=i: q.tensor_tensor(
                        out=qTr[:, S * oc + 128 * i:S * oc + 128 * (i + 1)],
                        in0=f1[:, 0:128], in1=f3[:, 0:128], op=OP.add))
            def mmkvacc(q, i=i):
                last = None
                for h in range(H):
                    pair = h // 2
                    c0 = 65 * (h % 2)
                    last = q.matmul(kv_slice(pair, c0, 64),
                                    kb[:, 64 * h:64 * (h + 1)],
                                    vb[:, 64 * h:64 * (h + 1)],
                                    start=False, stop=(i == NT - 1),
                                    skip_group_check=True)
                    last = q.matmul(kv_slice(pair, c0 + 64, 1),
                                    kb[:, 64 * h:64 * (h + 1)], onb[:, :],
                                    start=False, stop=(i == NT - 1),
                                    skip_group_check=True)
                return last
            P.op("tensor", mmkvacc)
        def evkv(q):
            last = None
            for pair in range(8):
                last = q.copy(KVb[:, 130 * pair:130 * (pair + 1)],
                              kv_slice(pair, 0, 130))
            return last
        P.op("scalar", evkv)

        for tc in range(8):
            def qsrc(par, tc=tc):
                return qTr[64 * par:64 * (par + 1), :].rearrange(
                    "p (oc t) -> p oc t", t=S)[:, :, 512 * tc:512 * (tc + 1)
                    ].rearrange("p oc j -> p oc () j")
            def qdst(par):
                return qtmp[0:64, :].rearrange(
                    "p (oc two j) -> p oc two j", two=2, j=512)[:, :, par:par + 1, :]
            P.dma("sync", [(qdst(0), qsrc(0)), (qdst(1), qsrc(1))])
            for h in range(H):
                oc = h // 2
                pair = h // 2
                c0 = 65 * (h % 2)
                P.op("tensor", lambda q, pair=pair, c0=c0, h=h:
                     q.matmul(pat[:, :],
                              KVb[:, 130 * pair + c0:130 * pair + c0 + 65],
                              qtmp[0:64, 512 * h:512 * (h + 1)],
                              start=True, stop=True))
                P.op("scalar", lambda q: q.copy(f3[0:64, :], pat[0:64, :]))
                P.op("scalar", lambda q: q.copy(f2[64:65, 0:512], pat[64:65, :]))
                P.dma("gpsimd", [(f1[0:1, 0:512], f2[64:65, 0:512])])
                P.op("vector", lambda q: q.tensor_scalar(
                    out=f2[0:1, 0:512], in0=f1[0:1, 0:512], scalar1=1e-6,
                    scalar2=None, op0=OP.add))
                P.op("vector", lambda q: q.reciprocal(out=f1[0:1, 0:512],
                                                      in_=f2[0:1, 0:512]))
                P.op("tensor", lambda q: q.matmul(pbc[:, :], on_s[0:1, 0:64],
                                                  f1[0:1, 0:512], start=True, stop=True))
                P.op("vector", lambda q: q.tensor_tensor(
                    out=atm[:, :], in0=f3[0:64, :], in1=pbc[:, :], op=OP.mult))
                pb0 = 64 * (h % 2)
                P.dma("gpsimd", [(atb[pb0:pb0 + 64, 512 * oc:512 * (oc + 1)],
                                  atm[:, :])])
            for ts in range(4):
                ti = 4 * tc + ts
                P.dma("sync", [(xt[:, :], x2[128 * ti:128 * (ti + 1), :])])
                for half in range(2):
                    def mmo(q, ts=ts, half=half):
                        last = None
                        for dc in range(DCH):
                            last = q.matmul(po[:, 512 * half:512 * (half + 1)],
                                            atb[:, 512 * dc + 128 * ts:512 * dc + 128 * (ts + 1)],
                                            woT[:, D * dc + 512 * half:D * dc + 512 * (half + 1)],
                                            start=(dc == 0), stop=(dc == DCH - 1))
                        return last
                    P.op("tensor", mmo)
                P.op("vector", lambda q: q.tensor_tensor(
                    out=res[:, :], in0=po[:, 0:D], in1=bo_s[:, :], op=OP.add))
                P.op("vector", lambda q: q.tensor_tensor(
                    out=res[:, :], in0=res[:, :], in1=xt[:, :], op=OP.add))
                P.dma("gpsimd", [(x3[128 * ti:128 * (ti + 1), :], res[:, :])])
        if standalone:
            P.build([sem])
    return nc



FI = 2816
NKC = FI // 128
QMIDS = (0.25, 0.75, 1.25, 1.75, 2.5, 3.5, 5.0)
QINCS = (0.5, 0.5, 0.5, 0.5, 1.0, 1.0, 2.0)


def _pair(ap, cw, idx):
    return ap[:, 0:cw].rearrange("p (a two) -> p a two", two=2)[:, :, idx:idx + 1]


def _quant_steps(P, src, dst_bf, width, B):
    nb = width // 16
    qa, qm, lv, qsc, sg = B["qa"], B["qm"], B["lv"], B["qsc"], B["sg"]
    P.op("scalar", lambda q: q.activation(out=qa[:, 0:width], in_=src, func=AF.Abs,
                                          bias=0.0, scale=1.0))
    P.op("scalar", lambda q: q.activation(out=sg[:, 0:width], in_=src, func=AF.Sign,
                                          bias=0.0, scale=1.0))
    cur, cw = qa, width
    for stp in range(4):
        nxt = qm if stp % 2 == 0 else lv
        P.op("vector", lambda q, cur=cur, nxt=nxt, cw=cw: q.tensor_tensor(
            out=nxt[:, 0:cw // 2].rearrange("p (a one) -> p a one", one=1),
            in0=_pair(cur, cw, 0), in1=_pair(cur, cw, 1),
            op=OP.max))
        cur, cw = nxt, cw // 2
    P.op("vector", lambda q, cur=cur: q.tensor_scalar(
        out=qsc[:, 0:nb], in0=cur[:, 0:nb], scalar1=1.0 / 6.0, scalar2=1e-12,
        op0=OP.mult, op1=OP.max))
    qsc3 = qsc[:, 0:nb].rearrange("p (a one) -> p a one", one=1).to_broadcast(
        [128, nb, 16])
    th3 = qsc[:, nb:2 * nb].rearrange("p (a one) -> p a one", one=1).to_broadcast(
        [128, nb, 16])
    a3 = qa[:, 0:width].rearrange("p (a b) -> p a b", b=16)
    m3 = qm[:, 0:width].rearrange("p (a b) -> p a b", b=16)
    l3 = lv[:, 0:width].rearrange("p (a b) -> p a b", b=16)
    for j, (mid, inc) in enumerate(zip(QMIDS, QINCS)):
        P.op("vector", lambda q, mid=mid: q.tensor_scalar(
            out=qsc[:, nb:2 * nb], in0=qsc[:, 0:nb], scalar1=mid, scalar2=None,
            op0=OP.mult))
        if j == 0:
            P.op("vector", lambda q: q.tensor_tensor(out=l3, in0=a3, in1=th3,
                                                     op=OP.is_gt))
            P.op("vector", lambda q, inc=inc: q.tensor_scalar(
                out=lv[:, 0:width], in0=lv[:, 0:width], scalar1=inc, scalar2=None,
                op0=OP.mult))
        else:
            P.op("vector", lambda q: q.tensor_tensor(out=m3, in0=a3, in1=th3,
                                                     op=OP.is_gt))
            P.op("vector", lambda q, inc=inc: q.scalar_tensor_tensor(
                out=lv[:, 0:width], in0=qm[:, 0:width], scalar=inc,
                in1=lv[:, 0:width], op0=OP.mult, op1=OP.add))
    P.op("vector", lambda q: q.tensor_tensor(out=lv[:, 0:width], in0=lv[:, 0:width],
                                             in1=sg[:, 0:width], op=OP.mult))
    d3 = dst_bf.rearrange("p (a b) -> p a b", b=16)
    P.op("vector", lambda q: q.tensor_tensor(out=d3, in0=l3, in1=qsc3, op=OP.mult))


def build_p5(nc=None, P=None, io=None, pref="", consts=True):
    standalone = nc is None
    if standalone:
        nc = bass.Bass("TRN2", target_bir_lowering=False)
        P = Prog(nc)
    io = io or {}
    def dram(name, shape, dt_, kind=None):
        if name in io:
            return io[name]
        if kind is None:
            return nc.dram_tensor(pref + name, shape, dt_).ap()
        return nc.dram_tensor(pref + name, shape, dt_, kind=kind).ap()
    x3 = dram("x3", [S, D], F32, "ExternalInput")
    wg = dram("wg", [FI, D], F32, "ExternalInput")
    wu = dram("wu", [FI, D], F32, "ExternalInput")
    wd = dram("wd", [D, FI], F32, "ExternalInput")
    ident = dram("ident", [128, 128], F32, "ExternalInput")
    out = dram("out", [S, D], F32, "ExternalOutput")

    NH = (512, 512, 512, 512, 512, 256)

    with ExitStack() as es:
        sem = es.enter_context(nc.semaphore(pref + "m")) if standalone else None
        sb = lambda name, shape, dt=F32: es.enter_context(nc.sbuf_tensor(pref + name, shape, dt))
        ps = lambda name, shape, dt=F32: es.enter_context(nc.psum_tensor(pref + name, shape, dt))

        if consts:
            _const_pool(nc, P, es, [0.0, 1e-5])
        wgT = sb("wgT", [128, DCH * FI], BF16)
        wuT = sb("wuT", [128, DCH * FI], BF16)
        wdT = sb("wdT", [128, NKC * D], BF16)
        idf = sb("idf", [128, 128])
        idb = sb("idb", [128, 128], BF16)
        qa = sb("qa", [128, 1408])
        qm = sb("qm", [128, 1408])
        lv = sb("lv", [128, 1408])
        sg_ = sb("sg_", [128, 1408])
        qsc = sb("qsc", [128, 2 * 88])
        stg = sb("stg", [128, 1408])
        qb = sb("qb", [128, FI], BF16)
        xt = sb("xt", [128, D])
        lnb = sb("lnv", [128, D])
        stt = sb("st", [128, 8])
        xqT = sb("xqT", [128, D], BF16)
        hq = sb("hq", [128, FI], BF16)
        hqT = sb("hqT", [128, FI], BF16)
        hc = sb("hc", [128, 512])
        ptb = ps("ptb", [128, D], BF16)
        pg = ps("pg", [128, 512])
        pu = ps("pu", [128, 512])
        po = ps("po", [128, D])
        B = {"qa": qa, "qm": qm, "lv": lv, "qsc": qsc, "sg": sg_}
        sgv = qa[:, 0:512]
        res = lnb

        P.dma("sync", [(idf[:, :], ident)])
        P.op("vector", lambda q: q.tensor_copy(out=idb[:, :], in_=idf[:, :]))

        for (wsrc, wdst) in ((wg, wgT), (wu, wuT)):
            for r in range(NKC):
                P.dma("sync", [(stg[:, 0:D], wsrc[128 * r:128 * (r + 1), :])])
                _quant_steps(P, stg[:, 0:D], qb[:, 0:D], D, B)
                def mmt(q):
                    last = None
                    for dc in range(DCH):
                        last = q.transpose(out=ptb[:, 128 * dc:128 * (dc + 1)],
                                           in_=qb[:, 128 * dc:128 * (dc + 1)],
                                           identity=idb[:, :])
                    return last
                P.op("tensor", mmt)
                def cpw(q, wdst=wdst, r=r):
                    last = None
                    for dc in range(DCH):
                        last = q.copy(wdst[:, FI * dc + 128 * r:FI * dc + 128 * (r + 1)],
                                      ptb[:, 128 * dc:128 * (dc + 1)])
                    return last
                P.op("scalar", cpw)
        for r in range(DCH):
            for hf in range(2):
                P.dma("sync", [(stg[:, :], wd[128 * r:128 * (r + 1),
                                              1408 * hf:1408 * (hf + 1)])])
                _quant_steps(P, stg[:, :], qb[:, 1408 * hf:1408 * (hf + 1)], 1408, B)
            for kg in range(3):
                nblk = min(8, NKC - 8 * kg)
                def mmt2(q, kg=kg, nblk=nblk):
                    last = None
                    for kk in range(nblk):
                        kc = 8 * kg + kk
                        last = q.transpose(out=ptb[:, 128 * kk:128 * (kk + 1)],
                                           in_=qb[:, 128 * kc:128 * (kc + 1)],
                                           identity=idb[:, :])
                    return last
                P.op("tensor", mmt2)
                def cpw2(q, kg=kg, nblk=nblk, r=r):
                    last = None
                    for kk in range(nblk):
                        kc = 8 * kg + kk
                        last = q.copy(wdT[:, D * kc + 128 * r:D * kc + 128 * (r + 1)],
                                      ptb[:, 128 * kk:128 * (kk + 1)])
                    return last
                P.op("scalar", cpw2)

        for i in range(NT):
            P.dma("sync", [(xt[:, :], x3[128 * i:128 * (i + 1), :])])
            _ln_tile(P, xt[:, :], lnb[:, :], qm[:, 0:D], stt)
            _quant_steps(P, lnb[:, :], qb[:, 0:D], D, B)
            _transpose_tile(P, qb[:, 0:D], ptb, xqT[:, :], idb)
            off = 0
            for w in NH:
                for (wT, pp) in ((wgT, pg), (wuT, pu)):
                    def mmg(q, wT=wT, pp=pp, off=off, w=w):
                        last = None
                        for dc in range(DCH):
                            last = q.matmul(pp[:, 0:w],
                                            xqT[:, 128 * dc:128 * (dc + 1)],
                                            wT[:, FI * dc + off:FI * dc + off + w],
                                            start=(dc == 0), stop=(dc == DCH - 1))
                        return last
                    P.op("tensor", mmg)
                P.op("scalar", lambda q, w=w: q.activation(
                    out=sgv[:, 0:w], in_=pg[:, 0:w], func=AF.Silu, bias=0.0, scale=1.0))
                P.op("vector", lambda q, w=w: q.tensor_tensor(
                    out=hc[:, 0:w], in0=sgv[:, 0:w], in1=pu[:, 0:w], op=OP.mult))
                _quant_steps(P, hc[:, 0:w], hq[:, off:off + w], w, B)
                off += w
            for kg in range(3):
                nblk = min(8, NKC - 8 * kg)
                def mmth(q, kg=kg, nblk=nblk):
                    last = None
                    for kk in range(nblk):
                        kc = 8 * kg + kk
                        last = q.transpose(out=ptb[:, 128 * kk:128 * (kk + 1)],
                                           in_=hq[:, 128 * kc:128 * (kc + 1)],
                                           identity=idb[:, :])
                    return last
                P.op("tensor", mmth)
                P.op("scalar", lambda q, kg=kg, nblk=nblk: q.copy(
                    hqT[:, 128 * 8 * kg:128 * (8 * kg + nblk)], ptb[:, 0:128 * nblk]))
            def mmd(q):
                last = None
                for oh in range(2):
                    for kc in range(NKC):
                        last = q.matmul(po[:, 512 * oh:512 * (oh + 1)],
                                        hqT[:, 128 * kc:128 * (kc + 1)],
                                        wdT[:, D * kc + 512 * oh:D * kc + 512 * (oh + 1)],
                                        start=(kc == 0), stop=(kc == NKC - 1),
                                        skip_group_check=True)
                return last
            P.op("tensor", mmd)
            P.op("vector", lambda q: q.tensor_tensor(
                out=res[:, :], in0=po[:, 0:D], in1=xt[:, :], op=OP.add))
            P.dma("gpsimd", [(out[128 * i:128 * (i + 1), :], res[:, :])])
        if standalone:
            P.build([sem])
    return nc



_IDENT = np.eye(128, dtype=np.float32)


def _tile128(v):
    return np.ascontiguousarray(np.tile(np.asarray(v, np.float32)[None, :], (128, 1)))


def _f(a):
    return np.ascontiguousarray(np.asarray(a, np.float32))





def build_fused():
    nc = bass.Bass("TRN2", target_bir_lowering=False)
    P = Prog(nc)
    E = lambda n, s, d: nc.dram_tensor(n, s, d, kind="ExternalInput").ap()
    ident = E("ident", [128, 128], F32)
    ones1 = E("ones1", [1, 128], F32)
    x = E("x", [S, D], F32)
    wq = E("wq", [D, D], F32); wk = E("wk", [D, D], F32)
    wv = E("wv", [D, D], F32); wo = E("wo", [D, D], F32)
    bout_t = E("bout_t", [128, D], F32); ns_t = E("ns_t", [128, 8], F32)
    Ws = E("Ws", [D, 8], F32); bs_t = E("bs_t", [128, 8], F32)
    W1 = E("W1", [8 * D, FF], F32); W2 = E("W2", [8 * FF, D], F32)
    b1c = E("b1c", [8 * 128, 32], F32); b2t = E("b2t", [8 * 128, D], F32)
    sel8 = E("sel8", [128, 8], F32); sel8T = E("sel8T", [8, 128], F32)
    Wq = E("Wq", [D, D], F32); Wk = E("Wk", [D, D], F32)
    Wv = E("Wv", [D, D], F32); Wo = E("Wo", [D, D], F32)
    bqc = E("bqc", [128, 8], F32); bkb = E("bkb", [128, D], F32)
    bvb = E("bvb", [128, D], F32); bob = E("bob", [128, D], F32)
    wg = E("wg", [FI, D], F32); wu = E("wu", [FI, D], F32)
    wd = E("wd", [D, FI], F32)
    out = nc.dram_tensor("out", [S, D], F32, kind="ExternalOutput").ap()
    x1_h = nc.dram_tensor("x1_h", [S, D], F32).ap()
    x2_h = nc.dram_tensor("x2_h", [S, D], F32).ap()
    x3_h = nc.dram_tensor("x3_h", [S, D], F32).ap()
    probs_h = nc.dram_tensor("probs_h", [S, 8], F32).ap()
    cc_in_t = nc.dram_tensor("cc_in", [16, 2048], F32)
    cc_out_t = nc.dram_tensor("cc_out", [128, 2048], F32)

    with ExitStack() as top:
        sems = [top.enter_context(nc.semaphore("ep%d" % i)) for i in range(5)]
        cc_sem = top.enter_context(nc.semaphore("ccs"))
        _const_pool(nc, P, top, [0.0, 1e-5, 1e-6, -SOFTMAX_SHIFT])
        build_p1(nc, P, io=dict(x=x, wq=wq, wk=wk, wv=wv, wo=wo, bout_t=bout_t,
                                ns_t=ns_t, ident=ident, x1=x1_h),
                 pref="a_", consts=False)
        P.new_epoch()
        build_p2(nc, P, io=dict(x1=x1_h, Ws=Ws, bs_t=bs_t, ident=ident,
                                probs=probs_h, cc_in=cc_in_t.ap()),
                 pref="b_", consts=False)
        P.new_epoch()

        def coll(q):
            q.collective_compute("AllGather", OP.bypass,
                                 replica_groups=[list(range(NCORES))],
                                 ins=[cc_in_t.ap()],
                                 outs=[cc_out_t.ap()]).then_inc(cc_sem)
            q.wait_ge(cc_sem, 1)
            return q.nop()
        P.op("gpsimd", coll)
        build_p3(nc, P, io=dict(x1=x1_h, pb=cc_out_t.ap(), pown=probs_h, W1=W1,
                                W2=W2, b1c=b1c, b2t=b2t, sel8=sel8, sel8T=sel8T,
                                ones1=ones1, ident=ident, x2=x2_h),
                 pref="c_", consts=False)
        P.new_epoch()
        build_p4(nc, P, io=dict(x2=x2_h, Wq=Wq, Wk=Wk, Wv=Wv, Wo=Wo, bqc=bqc,
                                bkb=bkb, bvb=bvb, bob=bob, ones1=ones1,
                                ident=ident, x3=x3_h),
                 pref="d_", consts=False)
        P.new_epoch()
        build_p5(nc, P, io=dict(x3=x3_h, wg=wg, wu=wu, wd=wd, ident=ident,
                                out=out),
                 pref="e_", consts=False)
        P.build(sems)
    return nc


def kernel(**inp):
    x = _f(inp["L0_attn_x"])
    cores = list(range(NCORES))
    sel = np.zeros((128, 8), np.float32)
    for p in range(128):
        sel[p, (p % 16) // 2] = 1.0
    nc = build_fused()
    m = {"ident": _IDENT, "ones1": np.ones((1, 128), np.float32),
         "wq": _f(inp["L0_attn_wq"]).reshape(D, D),
         "wk": _f(inp["L0_attn_wk"]).reshape(D, D),
         "wv": _f(inp["L0_attn_wv"]).reshape(D, D),
         "wo": _f(inp["L0_attn_w_out"]).reshape(D, D),
         "bout_t": _tile128(inp["L0_attn_b_out"]),
         "ns_t": np.ascontiguousarray(_f(inp["L0_attn_norm_scale"]).reshape(8, 128).T),
         "Ws": _f(inp["L0_ffn_Ws"]), "bs_t": _tile128(inp["L0_ffn_bs"]),
         "W1": _f(inp["L0_ffn_W1"]).reshape(8 * D, FF),
         "W2": _f(inp["L0_ffn_W2"]).reshape(8 * FF, D),
         "b1c": np.ascontiguousarray(_f(inp["L0_ffn_b1"]).reshape(8, 32, 128)
                                     .transpose(0, 2, 1).reshape(8 * 128, 32)),
         "b2t": np.ascontiguousarray(np.repeat(_f(inp["L0_ffn_b2"])[:, None, :],
                                               128, axis=1).reshape(8 * 128, D)),
         "sel8": sel, "sel8T": np.ascontiguousarray(sel.T),
         "Wq": _f(inp["L1_attn_Wq"]), "Wk": _f(inp["L1_attn_Wk"]),
         "Wv": _f(inp["L1_attn_Wv"]), "Wo": _f(inp["L1_attn_Wo"]),
         "bqc": np.ascontiguousarray(_f(inp["L1_attn_bq"]).reshape(8, 128).T),
         "bkb": _tile128(inp["L1_attn_bk"]), "bvb": _tile128(inp["L1_attn_bv"]),
         "bob": _tile128(inp["L1_attn_bo"]),
         "wg": _f(inp["L1_ffn_w_gate"]), "wu": _f(inp["L1_ffn_w_up"]),
         "wd": _f(inp["L1_ffn_w_down"])}
    r = run_bass_kernel_spmd(nc, [dict(m, x=_f(x[c])) for c in cores],
                             core_ids=cores)
    return np.stack([r.results[c]["out"] for c in cores]).astype(np.float32)


def kernel_unfused(**inp):
    x = _f(inp["L0_attn_x"])
    cores = list(range(NCORES))

    nc1 = build_p1()
    m1 = {"wq": _f(inp["L0_attn_wq"]).reshape(D, D),
          "wk": _f(inp["L0_attn_wk"]).reshape(D, D),
          "wv": _f(inp["L0_attn_wv"]).reshape(D, D),
          "wo": _f(inp["L0_attn_w_out"]).reshape(D, D),
          "bout_t": _tile128(inp["L0_attn_b_out"]),
          "ns_t": np.ascontiguousarray(_f(inp["L0_attn_norm_scale"]).reshape(8, 128).T),
          "ident": _IDENT}
    r1 = run_bass_kernel_spmd(nc1, [dict(m1, x=_f(x[c])) for c in cores],
                              core_ids=cores)
    x1 = np.stack([r1.results[c]["x1"] for c in cores])

    nc2 = build_p2()
    m2 = {"Ws": _f(inp["L0_ffn_Ws"]), "bs_t": _tile128(inp["L0_ffn_bs"]),
          "ident": _IDENT}
    r2 = run_bass_kernel_spmd(nc2, [dict(m2, x1=_f(x1[c])) for c in cores],
                              core_ids=cores)
    probs = np.stack([r2.results[c]["probs"] for c in cores])

    probs_T = np.ascontiguousarray(probs.reshape(NCORES * S, 8).T)
    sel = np.zeros((128, 8), np.float32)
    for p in range(128):
        sel[p, p // 16] = 1.0
    nc3 = build_p3()
    m3 = {"pb": np.ascontiguousarray(probs_T.reshape(128, 2048)),
          "W1": _f(inp["L0_ffn_W1"]).reshape(8 * D, FF),
          "W2": _f(inp["L0_ffn_W2"]).reshape(8 * FF, D),
          "b1c": np.ascontiguousarray(_f(inp["L0_ffn_b1"]).reshape(8, 32, 128)
                                      .transpose(0, 2, 1).reshape(8 * 128, 32)),
          "b2t": np.ascontiguousarray(np.repeat(_f(inp["L0_ffn_b2"])[:, None, :],
                                                128, axis=1).reshape(8 * 128, D)),
          "sel8": sel, "sel8T": np.ascontiguousarray(sel.T),
          "ones1": np.ones((1, 128), np.float32), "ident": _IDENT}
    r3 = run_bass_kernel_spmd(
        nc3, [dict(m3, x1=_f(x1[c]), pown=_f(probs[c])) for c in cores],
        core_ids=cores)
    x2 = np.stack([r3.results[c]["x2"] for c in cores])

    nc4 = build_p4()
    m4 = {"Wq": _f(inp["L1_attn_Wq"]), "Wk": _f(inp["L1_attn_Wk"]),
          "Wv": _f(inp["L1_attn_Wv"]), "Wo": _f(inp["L1_attn_Wo"]),
          "bqc": np.ascontiguousarray(_f(inp["L1_attn_bq"]).reshape(8, 128).T),
          "bkb": _tile128(inp["L1_attn_bk"]), "bvb": _tile128(inp["L1_attn_bv"]),
          "bob": _tile128(inp["L1_attn_bo"]),
          "ones1": np.ones((1, 128), np.float32), "ident": _IDENT}
    r4 = run_bass_kernel_spmd(nc4, [dict(m4, x2=_f(x2[c])) for c in cores],
                              core_ids=cores)
    x3 = np.stack([r4.results[c]["x3"] for c in cores])

    nc5 = build_p5()
    m5 = {"wg": _f(inp["L1_ffn_w_gate"]), "wu": _f(inp["L1_ffn_w_up"]),
          "wd": _f(inp["L1_ffn_w_down"]), "ident": _IDENT}
    r5 = run_bass_kernel_spmd(nc5, [dict(m5, x3=_f(x3[c])) for c in cores],
                              core_ids=cores)
    return np.stack([r5.results[c]["out"] for c in cores]).astype(np.float32)


# version 05
# speedup vs baseline: 1.4354x; 1.1302x over previous
import numpy as np
from contextlib import ExitStack

import concourse.bass as bass
import concourse.mybir as mybir
from concourse.bass_utils import run_bass_kernel_spmd

F32 = mybir.dt.float32
BF16 = mybir.dt.bfloat16
AF = mybir.ActivationFunctionType
OP = mybir.AluOpType
AX = mybir.AxisListType

NCORES = 8
S = 4096
D = 1024
NT = S // 128
DCH = D // 128
H = 16
DH = 64
SOFTMAX_SHIFT = 20.0
P1_LIMIT = None
GC0 = 0.7978845608028654
GC2 = GC0 * 0.044715


class Prog:

    def __init__(self, nc):
        self.nc = nc
        self.epochs = [[]]
        self.totals = [0]

    @property
    def total(self):
        return self.totals[-1]

    def new_epoch(self):
        if self.epochs[-1]:
            self.epochs.append([])
            self.totals.append(0)

    def op(self, eng, fn, inc=1):
        self.epochs[-1].append((eng, "op", fn, self.totals[-1], inc))
        self.totals[-1] += inc

    def dma(self, eng, pairs, **kw):
        n = len(pairs)
        self.epochs[-1].append((eng, "dma", (pairs, kw), self.totals[-1], 16 * n))
        self.totals[-1] += 16 * n

    def build(self, sems):
        nc = self.nc
        epochs = self.epochs
        totals = self.totals

        def runner(q, name):
            for ei, steps in enumerate(epochs):
                first = True
                for (eng, kind, payload, wait, inc) in steps:
                    if eng != name:
                        continue
                    if first and ei > 0:
                        q.wait_ge(sems[ei - 1], totals[ei - 1])
                    first = False
                    if wait > 0:
                        q.wait_ge(sems[ei], wait)
                    if kind == "dma":
                        pairs, kw = payload
                        for (o, i) in pairs:
                            q.dma_start(out=o, in_=i, **kw).then_inc(sems[ei], 16)
                    else:
                        last = payload(q)
                        last.then_inc(sems[ei], inc)
            if name == "sync":
                if len(epochs) > 1:
                    q.wait_ge(sems[len(epochs) - 2], totals[-2])
                q.wait_ge(sems[len(epochs) - 1], totals[-1])

        with nc.Block() as block:
            @block.sync
            def _(q):
                runner(q, "sync")

            @block.scalar
            def _(q):
                runner(q, "scalar")

            @block.vector
            def _(q):
                runner(q, "vector")

            @block.tensor
            def _(q):
                runner(q, "tensor")

            @block.gpsimd
            def _(q):
                runner(q, "gpsimd")



def _const_pool(nc, P, es, vals):
    vals = sorted(set(float(v) for v in vals))
    cp = es.enter_context(nc.sbuf_tensor("cpool", [128, max(len(vals), 1)], F32))
    def fill(q):
        last = None
        for j, v in enumerate(vals):
            last = q.memset(cp[:, j:j + 1], v)
        return last
    P.op("vector", fill)
    for j, v in enumerate(vals):
        nc.const_aps.register(F32, v, cp[:, j:j + 1])
    return cp

def _ln_tile(P, xt, ln, sq, st, eps=1e-5):
    P.op("vector", lambda q: q.reduce_sum(out=st[:, 0:1], in_=xt, axis=AX.X))
    P.op("scalar", lambda q: q.mul(st[:, 1:2], st[:, 0:1], -1.0 / D))
    P.op("scalar", lambda q: q.activation(out=ln, in_=xt, func=AF.Identity,
                                          bias=st[:, 1:2], scale=1.0))
    P.op("scalar", lambda q: q.square(sq, ln))
    P.op("vector", lambda q: q.reduce_sum(out=st[:, 2:3], in_=sq, axis=AX.X))
    P.op("scalar", lambda q: q.activation(out=st[:, 3:4], in_=st[:, 2:3], func=AF.Sqrt,
                                          bias=float(eps), scale=1.0 / D))
    P.op("vector", lambda q: q.reciprocal(out=st[:, 4:5], in_=st[:, 3:4]))
    P.op("vector", lambda q: q.tensor_scalar(out=ln, in0=ln,
                                             scalar1=st[:, 4:5], scalar2=None,
                                             op0=OP.mult))


def _transpose_tile(P, src_bf, ptb, dst_bf, idb, nchunk=8):
    def mm(q):
        last = None
        for c in range(nchunk):
            last = q.transpose(out=ptb[:, 128 * c:128 * (c + 1)],
                               in_=src_bf[:, 128 * c:128 * (c + 1)],
                               identity=idb[:, :])
        return last
    P.op("tensor", mm)
    P.op("scalar", lambda q: q.copy(dst_bf, ptb[:, 0:128 * nchunk]))



def build_p1(nc=None, P=None, io=None, pref="", consts=True):
    standalone = nc is None
    if standalone:
        nc = bass.Bass("TRN2", target_bir_lowering=False)
        P = Prog(nc)
    io = io or {}
    def dram(name, shape, dt_, kind=None):
        if name in io:
            return io[name]
        if kind is None:
            return nc.dram_tensor(pref + name, shape, dt_).ap()
        return nc.dram_tensor(pref + name, shape, dt_, kind=kind).ap()
    x = dram("x", [S, D], F32, "ExternalInput")
    wq = dram("wq", [D, D], F32, "ExternalInput")
    wk = dram("wk", [D, D], F32, "ExternalInput")
    wv = dram("wv", [D, D], F32, "ExternalInput")
    wo = dram("wo", [D, D], F32, "ExternalInput")
    bout_t = dram("bout_t", [128, D], F32, "ExternalInput")
    ns_t = dram("ns_t", [128, 8], F32, "ExternalInput")
    ident = dram("ident", [128, 128], F32, "ExternalInput")
    x1 = dram("x1", [S, D], F32, "ExternalOutput")

    qT_h = dram("qT_h", [NT * 64, 2048], BF16)
    kT_h = dram("kT_h", [NT * 64, 2048], BF16)
    v_h = dram("v_h", [NT * 128, H * 65], BF16)

    with ExitStack() as es:
        sem = es.enter_context(nc.semaphore(pref + "m")) if standalone else None
        sb = lambda name, shape, dt=F32: es.enter_context(nc.sbuf_tensor(pref + name, shape, dt))
        ps = lambda name, shape, dt=F32: es.enter_context(nc.psum_tensor(pref + name, shape, dt))

        if consts:
            _const_pool(nc, P, es, [0.0, 1e-5, 1e-6, -SOFTMAX_SHIFT])
        wqb = sb("wqb", [128, DCH * D], BF16)
        wkb = sb("wkb", [128, DCH * D], BF16)
        wvb = sb("wvb", [128, DCH * D], BF16)
        wob = sb("wob", [128, DCH * D], BF16)
        bob = sb("bob", [128, D])
        nsc = sb("nsc", [128, 8])
        idf = sb("idf", [128, 128])
        idb = sb("idb", [128, 128], BF16)
        stg = sb("stg", [128, D])
        xt = sb("xt", [128, D])
        lnb = sb("ln", [128, D])
        sq = sb("sq", [128, D])
        stt = sb("st", [128, 8])
        hb = sb("hb", [128, D], BF16)
        hT = sb("hT", [128, D], BF16)
        qTt = sb("qTt", [128, D], BF16)
        kTt = sb("kTt", [128, D], BF16)
        vst = sb("vst", [128, H * 65], BF16)
        qTc = sb("qTc", [64, 4096], BF16)
        ktc = sb("ktc", [64, 2048], BF16)
        vtc = sb("vtc", [128, H * 65], BF16)
        PTk = sb("PTk", [128, H * 256], BF16)
        oq65 = sb("oq65", [128, H * 65])
        rcol = sb("rcol", [128, H])
        oqb = sb("oqb", [128, D], BF16)
        oqT = sb("oqT", [128, D], BF16)
        xr = sb("xr", [128, D])
        res = sb("res", [128, D])
        ptb = ps("ptb", [128, D], BF16)
        pacc = ps("pacc", [128, 512])
        pav0 = ps("pav0", [128, 455])
        pav1 = ps("pav1", [128, 455])
        pav2 = ps("pav2", [128, 130])
        pav3 = ps("pav3", [128, 455])
        pav4 = ps("pav4", [128, 455])
        pav5 = ps("pav5", [128, 130])
        pav = [[pav0, pav1, pav2], [pav3, pav4, pav5]]

        def av_slice(qs_, h):
            si = min(h // 7, 2)
            loc = h - 7 * si
            return pav[qs_][si][:, 65 * loc:65 * loc + 65]

        P.dma("sync", [(idf[:, :], ident), (bob[:, :], bout_t), (nsc[:, :], ns_t)])
        P.op("vector", lambda q: q.tensor_copy(out=idb[:, :], in_=idf[:, :]))
        for dc in range(DCH):
            for (src, dst, sc) in ((wq, wqb, 0.125), (wk, wkb, 1.0), (wv, wvb, 1.0),
                                   (wo, wob, None)):
                P.dma("sync", [(stg[:, :], src[128 * dc:128 * (dc + 1), :])])
                if sc is not None:
                    P.op("vector", lambda q, sc=sc, dc=dc: q.tensor_scalar(
                        out=stg[:, :], in0=stg[:, :], scalar1=nsc[:, dc:dc + 1],
                        scalar2=sc, op0=OP.mult, op1=OP.mult))
                P.op("vector", lambda q, dst=dst, dc=dc: q.tensor_copy(
                    out=dst[:, D * dc:D * (dc + 1)], in_=stg[:, :]))

        def set_ones(q):
            last = None
            for h in range(H):
                last = q.memset(vst[:, 65 * h + 64:65 * h + 65], 1.0)
            return last
        P.op("vector", set_ones)

        for i in range(NT if P1_LIMIT != "W" else 0):
            P.dma("sync", [(xt[:, :], x[128 * i:128 * (i + 1), :])])
            _ln_tile(P, xt[:, :], lnb[:, :], sq[:, :], stt)
            P.op("scalar", lambda q: q.square(sq[:, :], lnb[:, :]))
            P.op("vector", lambda q: q.reduce_sum(out=stt[:, 5:6], in_=sq[:, :], axis=AX.X))
            P.op("scalar", lambda q: q.activation(out=stt[:, 6:7], in_=stt[:, 5:6],
                                                  func=AF.Sqrt, bias=1e-6, scale=1.0 / D))
            P.op("vector", lambda q: q.reciprocal(out=stt[:, 7:8], in_=stt[:, 6:7]))
            P.op("vector", lambda q: q.tensor_scalar(out=hb[:, :], in0=lnb[:, :],
                                                     scalar1=stt[:, 7:8], scalar2=None,
                                                     op0=OP.mult))
            _transpose_tile(P, hb, ptb, hT[:, :], idb)
            for (wb, dst) in ((wqb, qTt), (wkb, kTt)):
                for og in range(2):
                    def mmq(q, wb=wb, og=og):
                        last = None
                        for oc2 in range(4):
                            oc = 4 * og + oc2
                            for dc2 in range(DCH):
                                last = q.matmul(pacc[:, 128 * oc2:128 * (oc2 + 1)],
                                                wb[:, D * dc2 + 128 * oc:D * dc2 + 128 * (oc + 1)],
                                                hT[:, 128 * dc2:128 * (dc2 + 1)],
                                                start=(dc2 == 0), stop=(dc2 == DCH - 1))
                        return last
                    P.op("tensor", mmq)
                    P.op("scalar", lambda q, dst=dst, og=og:
                         q.copy(dst[:, 512 * og:512 * (og + 1)], pacc[:, 0:512]))
            for half in range(2):
                def mmv(q, half=half):
                    last = None
                    for dc2 in range(DCH):
                        last = q.matmul(pacc[:, 0:512],
                                        hT[:, 128 * dc2:128 * (dc2 + 1)],
                                        wvb[:, D * dc2 + 512 * half:D * dc2 + 512 * (half + 1)],
                                        start=(dc2 == 0), stop=(dc2 == DCH - 1))
                    return last
                P.op("tensor", mmv)
                def evacv(q, half=half):
                    last = None
                    for hh in range(8):
                        h = 8 * half + hh
                        last = q.copy(vst[:, 65 * h:65 * h + 64],
                                      pacc[:, 64 * hh:64 * (hh + 1)])
                    return last
                P.op("scalar", evacv)
            def hm_dst(dr, i, par):
                return dr[64 * i:64 * (i + 1), :].rearrange(
                    "p (c two j) -> p c two j", two=2, j=128)[:, :, par:par + 1, :]
            def hm_src(t, par):
                return t[64 * par:64 * (par + 1), :].rearrange(
                    "p (c one j) -> p c one j", one=1, j=128)
            P.dma("gpsimd", [(hm_dst(qT_h, i, 0), hm_src(qTt, 0)),
                             (hm_dst(qT_h, i, 1), hm_src(qTt, 1)),
                             (hm_dst(kT_h, i, 0), hm_src(kTt, 0)),
                             (hm_dst(kT_h, i, 1), hm_src(kTt, 1)),
                             (v_h[128 * i:128 * (i + 1), :], vst[:, :])])

        for jc in range(NT // 2 if (P1_LIMIT is None or P1_LIMIT.startswith("B")) else 0):
            t0 = 2 * jc
            def clr_pav(q):
                last = None
                for qs_ in range(2):
                    for si in range(3):
                        last = q.memset(pav[qs_][si][:, :], 0.0)
                return last
            P.op("vector", clr_pav)
            P.dma("sync", [(qTc[0:64, 2048 * t:2048 * (t + 1)],
                            qT_h[64 * (t0 + t):64 * (t0 + t + 1), :])
                           for t in range(2)])
            for kt in range(NT):
                P.dma("sync", [(ktc[0:64, :], kT_h[64 * kt:64 * (kt + 1), :]),
                               (vtc[:, :], v_h[128 * kt:128 * (kt + 1), :])])
                for hp in range(8):
                    def mmS(q, hp=hp):
                        last = None
                        for ho in range(2):
                            h = 2 * hp + ho
                            for tt in range(2):
                                last = q.matmul(
                                    pacc[:, 256 * ho + 128 * tt:256 * ho + 128 * (tt + 1)],
                                    ktc[0:64, 128 * h:128 * (h + 1)],
                                    qTc[0:64, 2048 * tt + 128 * h:2048 * tt + 128 * (h + 1)],
                                    start=True, stop=True)
                        return last
                    P.op("tensor", mmS)
                    if P1_LIMIT in ("B0", "B0a"):
                        continue
                    P.op("scalar", lambda q, hp=hp: q.activation(
                        out=PTk[:, 512 * hp:512 * (hp + 1)], in_=pacc[:, 0:512],
                        func=AF.Exp, bias=-SOFTMAX_SHIFT, scale=1.0))
                if P1_LIMIT in ("B0", "B0a", "B1"):
                    continue
                def mmav(q, kt=kt):
                    last = None
                    for qs_ in range(2):
                        for h in range(H):
                            last = q.matmul(av_slice(qs_, h),
                                            PTk[:, 256 * h + 128 * qs_:256 * h + 128 * (qs_ + 1)],
                                            vtc[:, 65 * h:65 * (h + 1)],
                                            start=False, stop=(kt == NT - 1),
                                            skip_group_check=True)
                    return last
                P.op("tensor", mmav)
            for qs_ in range(2):
                if P1_LIMIT in ("B0", "B0a", "B1", "B2"):
                    continue
                def evac(q, qs_=qs_):
                    last = None
                    for h in range(H):
                        last = q.copy(oq65[:, 65 * h:65 * (h + 1)], av_slice(qs_, h))
                    return last
                P.op("scalar", evac)
                def recips(q):
                    last = None
                    for h in range(H):
                        last = q.reciprocal(out=rcol[:, h:h + 1],
                                            in_=oq65[:, 65 * h + 64:65 * h + 65])
                    return last
                P.op("vector", recips)
                def divs(q):
                    last = None
                    for h in range(H):
                        last = q.tensor_scalar(out=oqb[:, 64 * h:64 * (h + 1)],
                                               in0=oq65[:, 65 * h:65 * h + 64],
                                               scalar1=rcol[:, h:h + 1], scalar2=None,
                                               op0=OP.mult)
                    return last
                P.op("vector", divs)
                _transpose_tile(P, oqb, ptb, oqT[:, :], idb)
                ti = 2 * jc + qs_
                P.dma("sync", [(xr[:, :], x[128 * ti:128 * (ti + 1), :])])
                for half in range(2):
                    def mmo(q, half=half):
                        last = None
                        for dc2 in range(DCH):
                            last = q.matmul(pacc[:, 0:512],
                                            oqT[:, 128 * dc2:128 * (dc2 + 1)],
                                            wob[:, D * dc2 + 512 * half:D * dc2 + 512 * (half + 1)],
                                            start=(dc2 == 0), stop=(dc2 == DCH - 1))
                        return last
                    P.op("tensor", mmo)
                    P.op("vector", lambda q, half=half: q.tensor_tensor(
                        out=res[:, 512 * half:512 * (half + 1)], in0=pacc[:, 0:512],
                        in1=bob[:, 512 * half:512 * (half + 1)], op=OP.add))
                P.op("vector", lambda q: q.tensor_tensor(
                    out=res[:, :], in0=res[:, :], in1=xr[:, :], op=OP.add))
                P.dma("gpsimd", [(x1[128 * ti:128 * (ti + 1), :], res[:, :])])

        if standalone:
            P.build([sem])
    return nc



def build_p2(nc=None, P=None, io=None, pref="", consts=True):
    standalone = nc is None
    if standalone:
        nc = bass.Bass("TRN2", target_bir_lowering=False)
        P = Prog(nc)
    io = io or {}
    def dram(name, shape, dt_, kind=None):
        if name in io:
            return io[name]
        if kind is None:
            return nc.dram_tensor(pref + name, shape, dt_).ap()
        return nc.dram_tensor(pref + name, shape, dt_, kind=kind).ap()
    x1 = dram("x1", [S, D], F32, "ExternalInput")
    Ws = dram("Ws", [D, 8], F32, "ExternalInput")
    bs_t = dram("bs_t", [128, 8], F32, "ExternalInput")
    ident = dram("ident", [128, 128], F32, "ExternalInput")
    probs = dram("probs", [S, 8], F32, "ExternalOutput")
    cc_in = io.get("cc_in")

    with ExitStack() as es:
        sem = es.enter_context(nc.semaphore(pref + "m")) if standalone else None
        sb = lambda name, shape, dt=F32: es.enter_context(nc.sbuf_tensor(pref + name, shape, dt))
        if consts:
            _const_pool(nc, P, es, [0.0, 1e-5])
        wsb = sb("wsb", [128, 64], BF16)
        wss = sb("wss", [128, 64])
        bsb = sb("bsb", [128, 8])
        idf = sb("idf", [128, 128])
        idb = sb("idb", [128, 128], BF16)
        xt = sb("xt", [128, D])
        lnb = sb("ln", [128, D])
        sq = sb("sq", [128, D])
        stt = sb("st", [128, 8])
        hb = sb("hb", [128, D], BF16)
        hT = sb("hT", [128, D], BF16)
        lg = sb("lg", [128, 8])
        pr = sb("pr", [128, 8])
        prT = sb("prT", [8, S]) if cc_in is not None else None
        ptb = es.enter_context(nc.psum_tensor(pref + "ptb", [128, D], BF16))
        plg = es.enter_context(nc.psum_tensor(pref + "plg", [128, 8], F32))
        pprT = (es.enter_context(nc.psum_tensor(pref + "pprT", [8, 128], F32))
                if cc_in is not None else None)

        P.dma("sync", [(idf[:, :], ident), (bsb[:, :], bs_t)] +
                      [(wss[:, 8 * dc:8 * (dc + 1)], Ws[128 * dc:128 * (dc + 1), :])
                       for dc in range(DCH)])
        P.op("vector", lambda q: q.tensor_copy(out=idb[:, :], in_=idf[:, :]))
        P.op("vector", lambda q: q.tensor_copy(out=wsb[:, :], in_=wss[:, :]))

        for i in range(NT):
            P.dma("sync", [(xt[:, :], x1[128 * i:128 * (i + 1), :])])
            _ln_tile(P, xt[:, :], lnb[:, :], sq[:, :], stt)
            P.op("vector", lambda q: q.tensor_copy(out=hb[:, :], in_=lnb[:, :]))
            _transpose_tile(P, hb, ptb, hT[:, :], idb)
            def mml(q):
                last = None
                for dc in range(DCH):
                    last = q.matmul(plg[:, 0:8], hT[:, 128 * dc:128 * (dc + 1)],
                                    wsb[:, 8 * dc:8 * (dc + 1)],
                                    start=(dc == 0), stop=(dc == DCH - 1))
                return last
            P.op("tensor", mml)
            P.op("vector", lambda q: q.tensor_tensor(out=lg[:, :], in0=plg[:, 0:8],
                                                     in1=bsb[:, :], op=OP.add))
            P.op("vector", lambda q: q.reduce_max(out=stt[:, 0:1], in_=lg[:, :],
                                                  axis=AX.X, negate=True))
            P.op("scalar", lambda q: q.activation(out=pr[:, :], in_=lg[:, :],
                                                  func=AF.Exp, bias=stt[:, 0:1], scale=1.0))
            P.op("vector", lambda q: q.reduce_sum(out=stt[:, 1:2], in_=pr[:, :], axis=AX.X))
            P.op("vector", lambda q: q.reciprocal(out=stt[:, 2:3], in_=stt[:, 1:2]))
            P.op("vector", lambda q: q.tensor_scalar(out=pr[:, :], in0=pr[:, :],
                                                     scalar1=stt[:, 2:3], scalar2=None,
                                                     op0=OP.mult))
            P.dma("gpsimd", [(probs[128 * i:128 * (i + 1), :], pr[:, :])])
            if cc_in is not None:
                P.op("tensor", lambda q: q.transpose(out=pprT[:, :], in_=pr[:, :],
                                                     identity=idf[:, :]))
                P.op("scalar", lambda q, i=i: q.copy(prT[:, 128 * i:128 * (i + 1)],
                                                     pprT[:, :]))
        if cc_in is not None:
            P.dma("gpsimd", [(cc_in.rearrange("(p h) c -> p (h c)", h=2), prT[:, :])])
        if standalone:
            P.build([sem])
    return nc



BISECT_ITERS = 38
TOPK = 4096
FF = 4096


def build_p3(nc=None, P=None, io=None, pref="", consts=True):
    standalone = nc is None
    if standalone:
        nc = bass.Bass("TRN2", target_bir_lowering=False)
        P = Prog(nc)
    io = io or {}
    def dram(name, shape, dt_, kind=None):
        if name in io:
            return io[name]
        if kind is None:
            return nc.dram_tensor(pref + name, shape, dt_).ap()
        return nc.dram_tensor(pref + name, shape, dt_, kind=kind).ap()
    x1 = dram("x1", [S, D], F32, "ExternalInput")
    pb_in = dram("pb", [128, 2048], F32, "ExternalInput")
    pown = dram("pown", [S, 8], F32, "ExternalInput")
    W1 = dram("W1", [8 * D, FF], F32, "ExternalInput")
    W2 = dram("W2", [8 * FF, D], F32, "ExternalInput")
    b1c = dram("b1c", [8 * 128, 32], F32, "ExternalInput")
    b2t = dram("b2t", [8 * 128, D], F32, "ExternalInput")
    sel8 = dram("sel8", [128, 8], F32, "ExternalInput")
    sel8T = dram("sel8T", [8, 128], F32, "ExternalInput")
    ones1 = dram("ones1", [1, 128], F32, "ExternalInput")
    ident = dram("ident", [128, 128], F32, "ExternalInput")
    x2 = dram("x2", [S, D], F32, "ExternalOutput")

    lnT_h = dram("lnT_h", [NT * 128, D], BF16)

    TC = 256
    NTC = S // TC

    with ExitStack() as es:
        sem = es.enter_context(nc.semaphore(pref + "m")) if standalone else None
        sb = lambda name, shape, dt=F32: es.enter_context(nc.sbuf_tensor(pref + name, shape, dt))
        ps = lambda name, shape, dt=F32: es.enter_context(nc.psum_tensor(pref + name, shape, dt))

        if consts:
            _const_pool(nc, P, es, [0.0, 1e-5])
        big = sb("big", [128, 5120])
        idf = sb("idf", [128, 128])
        idb = sb("idb", [128, 128], BF16)
        w_all = sb("w_all", [128, NT * 8])
        hb = sb("hb", [128, D], BF16)
        hT = sb("hT", [128, D], BF16)
        stt = sb("st", [128, 8])
        bis = sb("bis", [128, 8])
        sel_s = sb("sel_s", [128, 8])
        selT_s = sb("selT_s", [8, 128])
        on_s = sb("on_s", [1, 128])
        c8 = sb("c8", [8, 2])
        lo8 = sb("lo8", [1, 8])
        losb = sb("losb", [128, 8])
        pot = sb("pot", [128, 8])
        w1b = sb("w1b", [128, DCH * FF], BF16)
        w2b = sb("w2b", [128, 32 * D], BF16)
        lnc = sb("lnc", [128, 2 * D], BF16)
        atb = sb("atb", [128, 32 * TC], BF16)
        b1s = sb("b1s", [128, 32])
        b2s = sb("b2s", [128, D])
        ptb = ps("ptb", [128, D], BF16)
        ptiny = ps("ptiny", [128, 16])
        pA = ps("pA", [128, 512])
        pA2 = ps("pA2", [128, 512])
        pY = ps("pY", [128, D])

        xt = big[:, 0:1024]
        lnv = big[:, 1024:2048]
        sqv = big[:, 2048:3072]
        pbs = big[:, 0:2048]
        msk = big[:, 2048:4096]
        wst = big[:, 0:1024]
        ga = big[:, 1024:1280]
        gb = big[:, 1280:1536]
        gc = big[:, 1536:1792]
        acc = big[:, 2048:3072]
        yt = big[:, 3072:4096]

        P.dma("sync", [(idf[:, :], ident)])
        P.op("vector", lambda q: q.tensor_copy(out=idb[:, :], in_=idf[:, :]))

        for i in range(NT):
            P.dma("sync", [(xt, x1[128 * i:128 * (i + 1), :])])
            _ln_tile(P, xt, lnv, sqv, stt)
            P.op("vector", lambda q: q.tensor_copy(out=hb[:, :], in_=lnv))
            _transpose_tile(P, hb, ptb, hT[:, :], idb)
            P.dma("gpsimd", [(lnT_h[128 * i:128 * (i + 1), :], hT[:, :])])

        P.dma("sync", [(pbs, pb_in), (sel_s[:, :], sel8), (selT_s[:, :], sel8T),
                       (on_s[:, :], ones1)])
        LO, HI, MID, CNT = 0, 1, 2, 3
        P.op("vector", lambda q: q.memset(bis[:, LO:LO + 1], 0.0))
        P.op("vector", lambda q: q.memset(bis[:, HI:HI + 1], 2.0))
        cl, ch, nl, nh = 0, 1, 4, 5
        for _ in range(BISECT_ITERS):
            P.op("vector", lambda q, cl=cl, ch=ch: q.tensor_tensor(
                out=bis[:, MID:MID + 1], in0=bis[:, cl:cl + 1],
                in1=bis[:, ch:ch + 1], op=OP.add))
            P.op("scalar", lambda q: q.mul(bis[:, MID:MID + 1], bis[:, MID:MID + 1], 0.5))
            P.op("vector", lambda q: q.tensor_scalar(
                out=msk, in0=pbs, scalar1=bis[:, MID:MID + 1],
                scalar2=None, op0=OP.is_ge))
            P.op("vector", lambda q: q.reduce_sum(
                out=bis[:, CNT:CNT + 1], in_=msk, axis=AX.X))
            P.op("tensor", lambda q: q.matmul(ptiny[0:8, 0:1], sel_s[:, :],
                                              bis[:, CNT:CNT + 1], start=True, stop=True))
            P.op("scalar", lambda q: q.copy(c8[:, 0:1], ptiny[0:8, 0:1]))
            P.op("vector", lambda q: q.tensor_scalar(
                out=c8[:, 1:2], in0=c8[:, 0:1], scalar1=float(TOPK), scalar2=None,
                op0=OP.is_ge))
            P.op("tensor", lambda q: q.matmul(ptiny[:, 1:2], selT_s[:, :],
                                              c8[:, 1:2], start=True, stop=True))
            P.op("scalar", lambda q: q.copy(c8[:, 0:1] if False else losb[:, 0:1],
                                            ptiny[:, 1:2]))
            P.op("vector", lambda q, cl=cl: q.tensor_tensor(
                out=losb[:, 1:2], in0=bis[:, MID:MID + 1], in1=bis[:, cl:cl + 1],
                op=OP.subtract))
            P.op("vector", lambda q: q.tensor_tensor(
                out=losb[:, 1:2], in0=losb[:, 1:2], in1=losb[:, 0:1], op=OP.mult))
            P.op("vector", lambda q, cl=cl, nl=nl: q.tensor_tensor(
                out=bis[:, nl:nl + 1], in0=bis[:, cl:cl + 1], in1=losb[:, 1:2],
                op=OP.add))
            P.op("vector", lambda q, ch=ch: q.tensor_tensor(
                out=losb[:, 2:3], in0=bis[:, ch:ch + 1], in1=bis[:, MID:MID + 1],
                op=OP.subtract))
            P.op("vector", lambda q: q.tensor_tensor(
                out=losb[:, 2:3], in0=losb[:, 2:3], in1=losb[:, 0:1], op=OP.mult))
            P.op("vector", lambda q, nh=nh: q.tensor_tensor(
                out=bis[:, nh:nh + 1], in0=bis[:, MID:MID + 1], in1=losb[:, 2:3],
                op=OP.add))
            cl, nl = nl, cl
            ch, nh = nh, ch
        P.op("tensor", lambda q, cl=cl: q.matmul(ptiny[0:1, 2:10], bis[:, cl:cl + 1],
                                                 sel_s[:, :], start=True, stop=True))
        P.op("scalar", lambda q: q.mul(lo8[:, :], ptiny[0:1, 2:10], 1.0 / 16.0))
        P.op("tensor", lambda q: q.matmul(ptiny[:, 2:10], on_s[0:1, :],
                                          lo8[0:1, :], start=True, stop=True))
        P.op("scalar", lambda q: q.copy(losb[:, :], ptiny[:, 2:10]))
        for i in range(NT):
            P.dma("sync", [(pot[:, :], pown[128 * i:128 * (i + 1), :])])
            P.op("vector", lambda q: q.tensor_tensor(out=stt[:, :], in0=pot[:, :],
                                                     in1=losb[:, :], op=OP.is_ge))
            P.op("vector", lambda q, i=i: q.tensor_tensor(
                out=w_all[:, 8 * i:8 * (i + 1)], in0=pot[:, :], in1=stt[:, :],
                op=OP.mult))

        for e in range(8):
            P.dma("sync", [(b1s[:, :], b1c[128 * e:128 * (e + 1), :]),
                           (b2s[:, :], b2t[128 * e:128 * (e + 1), :])])
            for dc in range(DCH):
                for qt in range(4):
                    P.dma("sync", [(wst, W1[D * e + 128 * dc:D * e + 128 * (dc + 1),
                                            1024 * qt:1024 * (qt + 1)])])
                    P.op("vector", lambda q, dc=dc, qt=qt: q.tensor_copy(
                        out=w1b[:, FF * dc + 1024 * qt:FF * dc + 1024 * (qt + 1)],
                        in_=wst))
            for fc in range(32):
                P.dma("sync", [(wst, W2[FF * e + 128 * fc:FF * e + 128 * (fc + 1), :])])
                P.op("vector", lambda q, fc=fc: q.tensor_copy(
                    out=w2b[:, D * fc:D * (fc + 1)], in_=wst))
            for tc in range(NTC):
                P.dma("sync", [(lnc[:, D * t:D * (t + 1)],
                                lnT_h[128 * (2 * tc + t):128 * (2 * tc + t + 1), :])
                               for t in range(2)])
                for fc2 in range(16):
                    def mmA(q, fc2=fc2):
                        last = None
                        for k in range(2):
                            fc = 2 * fc2 + k
                            bank = pA if k == 0 else pA2
                            for tt in range(2):
                                for dc in range(DCH):
                                    last = q.matmul(
                                        bank[:, 128 * tt:128 * (tt + 1)],
                                        w1b[:, FF * dc + 128 * fc:FF * dc + 128 * (fc + 1)],
                                        lnc[:, D * tt + 128 * dc:D * tt + 128 * (dc + 1)],
                                        start=(dc == 0), stop=(dc == DCH - 1),
                                        skip_group_check=True)
                        return last
                    P.op("tensor", mmA)
                    def gel(q, fc2=fc2):
                        last = None
                        for k in range(2):
                            fc = 2 * fc2 + k
                            bank = pA if k == 0 else pA2
                            last = q.activation(
                                out=atb[:, TC * fc:TC * (fc + 1)], in_=bank[:, 0:TC],
                                func=AF.Gelu_apprx_tanh, bias=b1s[:, fc:fc + 1],
                                scale=1.0)
                        return last
                    P.op("scalar", gel)
                for ts in range(2):
                    ti = 2 * tc + ts
                    def mmY(q, ts=ts):
                        last = None
                        for fc in range(32):
                            for oh in range(2):
                                last = q.matmul(
                                    pY[:, 512 * oh:512 * (oh + 1)],
                                    atb[:, TC * fc + 128 * ts:TC * fc + 128 * (ts + 1)],
                                    w2b[:, D * fc + 512 * oh:D * fc + 512 * (oh + 1)],
                                    start=(fc == 0), stop=(fc == 31),
                                    skip_group_check=True)
                        return last
                    P.op("tensor", mmY)
                    src = x1 if e == 0 else x2
                    P.dma("sync", [(acc, src[128 * ti:128 * (ti + 1), :])])
                    P.op("vector", lambda q: q.tensor_tensor(
                        out=yt, in0=pY[:, 0:D], in1=b2s[:, :], op=OP.add))
                    P.op("vector", lambda q, ti=ti, e=e: q.tensor_scalar(
                        out=yt, in0=yt,
                        scalar1=w_all[:, 8 * ti + e:8 * ti + e + 1],
                        scalar2=None, op0=OP.mult))
                    P.op("vector", lambda q: q.tensor_tensor(
                        out=yt, in0=yt, in1=acc, op=OP.add))
                    P.dma("gpsimd", [(x2[128 * ti:128 * (ti + 1), :], yt)])
        if standalone:
            P.build([sem])
    return nc



def build_p4(nc=None, P=None, io=None, pref="", consts=True):
    standalone = nc is None
    if standalone:
        nc = bass.Bass("TRN2", target_bir_lowering=False)
        P = Prog(nc)
    io = io or {}
    def dram(name, shape, dt_, kind=None):
        if name in io:
            return io[name]
        if kind is None:
            return nc.dram_tensor(pref + name, shape, dt_).ap()
        return nc.dram_tensor(pref + name, shape, dt_, kind=kind).ap()
    x2 = dram("x2", [S, D], F32, "ExternalInput")
    Wq = dram("Wq", [D, D], F32, "ExternalInput")
    Wk = dram("Wk", [D, D], F32, "ExternalInput")
    Wv = dram("Wv", [D, D], F32, "ExternalInput")
    Wo = dram("Wo", [D, D], F32, "ExternalInput")
    bqc = dram("bqc", [128, 8], F32, "ExternalInput")
    bkb = dram("bkb", [128, D], F32, "ExternalInput")
    bvb = dram("bvb", [128, D], F32, "ExternalInput")
    bob = dram("bob", [128, D], F32, "ExternalInput")
    ones1 = dram("ones1", [1, 128], F32, "ExternalInput")
    ident = dram("ident", [128, 128], F32, "ExternalInput")
    x3 = dram("x3", [S, D], F32, "ExternalOutput")

    with ExitStack() as es:
        sem = es.enter_context(nc.semaphore(pref + "m")) if standalone else None
        sb = lambda name, shape, dt=F32: es.enter_context(nc.sbuf_tensor(pref + name, shape, dt))
        ps = lambda name, shape, dt=F32: es.enter_context(nc.psum_tensor(pref + name, shape, dt))

        if consts:
            _const_pool(nc, P, es, [0.0, 1e-5])
        wqT = sb("wqT", [128, DCH * D], BF16)
        wkT = sb("wkT", [128, DCH * D], BF16)
        wvT = sb("wvT", [128, DCH * D], BF16)
        woT = sb("woT", [128, DCH * D], BF16)
        qTr = sb("qTr", [128, DCH * S], BF16)
        bq_s = sb("bq_s", [128, 8])
        bk_s = sb("bk_s", [128, D])
        bv_s = sb("bv_s", [128, D])
        bo_s = sb("bo_s", [128, D])
        on_s = sb("on_s", [1, 128])
        onb = sb("onb", [128, 1], BF16)
        idf = sb("idf", [128, 128])
        idb = sb("idb", [128, 128], BF16)
        KVb = sb("KVb", [64, 8 * 130], BF16)
        xt = sb("xt", [128, D])
        lnb = sb("ln", [128, D])
        sq = sb("sq", [128, D])
        stt = sb("st", [128, 8])
        hb = sb("hb", [128, D], BF16)
        hT = sb("hT", [128, D], BF16)
        f1 = sb("f1", [128, 512])
        f2 = sb("f2", [128, 512])
        f3 = sb("f3", [128, 512])
        kb = sb("kb", [128, D], BF16)
        vb = sb("vb", [128, D], BF16)
        atb = sb("atb", [128, 8 * 512], BF16)
        qtmp = sb("qtmp", [64, 16 * 512], BF16)
        atm = sb("atm", [64, 512], BF16)
        res = sb("res", [128, D])
        ptb = ps("ptb", [128, D], BF16)
        pq = ps("pq", [128, 512])
        pkv0 = ps("pkv0", [64, 390])
        pkv1 = ps("pkv1", [64, 390])
        pkv2 = ps("pkv2", [64, 260])
        pbc = ps("pbc", [64, 512])
        po = ps("po", [128, D])
        pkv = (pkv0, pkv1, pkv2)
        pat = pq[0:65, :]

        def kv_slice(pair, col0, w):
            si = min(pair // 3, 2)
            loc = pair - 3 * si
            return pkv[si][:, 130 * loc + col0:130 * loc + col0 + w]

        P.dma("sync", [(idf[:, :], ident), (bq_s[:, :], bqc), (bk_s[:, :], bkb),
                       (bv_s[:, :], bvb), (bo_s[:, :], bob), (on_s[:, :], ones1)])
        P.op("vector", lambda q: q.tensor_copy(out=idb[:, :], in_=idf[:, :]))
        P.op("vector", lambda q: q.memset(onb[:, :], 1.0))

        for (wsrc, wdst) in ((Wq, wqT), (Wk, wkT), (Wv, wvT), (Wo, woT)):
            for r in range(8):
                P.dma("sync", [(xt[:, :], wsrc[128 * r:128 * (r + 1), :])])
                P.op("vector", lambda q: q.tensor_copy(out=hb[:, :], in_=xt[:, :]))
                def mmt(q):
                    last = None
                    for dc in range(DCH):
                        last = q.transpose(out=ptb[:, 128 * dc:128 * (dc + 1)],
                                           in_=hb[:, 128 * dc:128 * (dc + 1)],
                                           identity=idb[:, :])
                    return last
                P.op("tensor", mmt)
                def cpw(q, wdst=wdst, r=r):
                    last = None
                    for dc in range(DCH):
                        last = q.copy(wdst[:, D * dc + 128 * r:D * dc + 128 * (r + 1)],
                                      ptb[:, 128 * dc:128 * (dc + 1)])
                    return last
                P.op("scalar", cpw)

        def clr_pkv(q):
            last = None
            for t in pkv:
                last = q.memset(t[:, :], 0.0)
            return last
        P.op("vector", clr_pkv)
        for i in range(NT):
            P.dma("sync", [(xt[:, :], x2[128 * i:128 * (i + 1), :])])
            _ln_tile(P, xt[:, :], lnb[:, :], sq[:, :], stt)
            P.op("vector", lambda q: q.tensor_copy(out=hb[:, :], in_=lnb[:, :]))
            _transpose_tile(P, hb, ptb, hT[:, :], idb)
            for (wT, bias_b, dst, feat) in ((wkT, bk_s, kb, True),
                                            (wvT, bv_s, vb, False)):
                for half in range(2):
                    def mmkv(q, wT=wT, half=half):
                        last = None
                        for dc in range(DCH):
                            last = q.matmul(pq[:, 0:512],
                                            hT[:, 128 * dc:128 * (dc + 1)],
                                            wT[:, D * dc + 512 * half:D * dc + 512 * (half + 1)],
                                            start=(dc == 0), stop=(dc == DCH - 1))
                        return last
                    P.op("tensor", mmkv)
                    bs_ = bias_b[:, 512 * half:512 * (half + 1)]
                    if feat:
                        P.op("vector", lambda q, bs_=bs_: q.tensor_tensor(
                            out=f1[:, :], in0=pq[:, 0:512], in1=bs_, op=OP.add))
                        P.op("vector", lambda q: q.tensor_scalar(
                            out=f2[:, :], in0=f1[:, :], scalar1=0.0, scalar2=None,
                            op0=OP.min))
                        P.op("scalar", lambda q: q.activation(
                            out=f3[:, :], in_=f2[:, :], func=AF.Exp, bias=0.0, scale=1.0))
                        P.op("vector", lambda q: q.tensor_scalar(
                            out=f1[:, :], in0=f1[:, :], scalar1=0.0, scalar2=None,
                            op0=OP.max))
                        P.op("vector", lambda q, dst=dst, half=half: q.tensor_tensor(
                            out=dst[:, 512 * half:512 * (half + 1)], in0=f1[:, :],
                            in1=f3[:, :], op=OP.add))
                    else:
                        P.op("vector", lambda q, dst=dst, half=half, bs_=bs_:
                             q.tensor_tensor(out=dst[:, 512 * half:512 * (half + 1)],
                                             in0=pq[:, 0:512], in1=bs_, op=OP.add))
            for og in range(2):
                def mmq(q, og=og):
                    last = None
                    for oc2 in range(4):
                        oc = 4 * og + oc2
                        for dc in range(DCH):
                            last = q.matmul(pq[:, 128 * oc2:128 * (oc2 + 1)],
                                            wqT[:, D * dc + 128 * oc:D * dc + 128 * (oc + 1)],
                                            hT[:, 128 * dc:128 * (dc + 1)],
                                            start=(dc == 0), stop=(dc == DCH - 1))
                    return last
                P.op("tensor", mmq)
                for oc2 in range(4):
                    oc = 4 * og + oc2
                    pslice = pq[:, 128 * oc2:128 * (oc2 + 1)]
                    P.op("scalar", lambda q, pslice=pslice, oc=oc: q.activation(
                        out=f1[:, 0:128], in_=pslice, func=AF.Identity,
                        bias=bq_s[:, oc:oc + 1], scale=1.0))
                    P.op("vector", lambda q: q.tensor_scalar(
                        out=f2[:, 0:128], in0=f1[:, 0:128], scalar1=0.0,
                        scalar2=None, op0=OP.min))
                    P.op("scalar", lambda q: q.activation(
                        out=f3[:, 0:128], in_=f2[:, 0:128], func=AF.Exp,
                        bias=0.0, scale=1.0))
                    P.op("vector", lambda q: q.tensor_scalar(
                        out=f1[:, 0:128], in0=f1[:, 0:128], scalar1=0.0,
                        scalar2=None, op0=OP.max))
                    P.op("vector", lambda q, oc=oc, i=i: q.tensor_tensor(
                        out=qTr[:, S * oc + 128 * i:S * oc + 128 * (i + 1)],
                        in0=f1[:, 0:128], in1=f3[:, 0:128], op=OP.add))
            def mmkvacc(q, i=i):
                last = None
                for h in range(H):
                    pair = h // 2
                    c0 = 65 * (h % 2)
                    last = q.matmul(kv_slice(pair, c0, 64),
                                    kb[:, 64 * h:64 * (h + 1)],
                                    vb[:, 64 * h:64 * (h + 1)],
                                    start=False, stop=(i == NT - 1),
                                    skip_group_check=True)
                    last = q.matmul(kv_slice(pair, c0 + 64, 1),
                                    kb[:, 64 * h:64 * (h + 1)], onb[:, :],
                                    start=False, stop=(i == NT - 1),
                                    skip_group_check=True)
                return last
            P.op("tensor", mmkvacc)
        def evkv(q):
            last = None
            for pair in range(8):
                last = q.copy(KVb[:, 130 * pair:130 * (pair + 1)],
                              kv_slice(pair, 0, 130))
            return last
        P.op("scalar", evkv)

        for tc in range(8):
            def qsrc(par, tc=tc):
                return qTr[64 * par:64 * (par + 1), :].rearrange(
                    "p (oc t) -> p oc t", t=S)[:, :, 512 * tc:512 * (tc + 1)
                    ].rearrange("p oc j -> p oc () j")
            def qdst(par):
                return qtmp[0:64, :].rearrange(
                    "p (oc two j) -> p oc two j", two=2, j=512)[:, :, par:par + 1, :]
            P.dma("sync", [(qdst(0), qsrc(0)), (qdst(1), qsrc(1))])
            for h in range(H):
                oc = h // 2
                pair = h // 2
                c0 = 65 * (h % 2)
                P.op("tensor", lambda q, pair=pair, c0=c0, h=h:
                     q.matmul(pat[:, :],
                              KVb[:, 130 * pair + c0:130 * pair + c0 + 65],
                              qtmp[0:64, 512 * h:512 * (h + 1)],
                              start=True, stop=True))
                P.op("scalar", lambda q: q.copy(f3[0:64, :], pat[0:64, :]))
                P.op("scalar", lambda q: q.copy(f2[64:65, 0:512], pat[64:65, :]))
                P.dma("gpsimd", [(f1[0:1, 0:512], f2[64:65, 0:512])])
                P.op("vector", lambda q: q.tensor_scalar(
                    out=f2[0:1, 0:512], in0=f1[0:1, 0:512], scalar1=1e-6,
                    scalar2=None, op0=OP.add))
                P.op("vector", lambda q: q.reciprocal(out=f1[0:1, 0:512],
                                                      in_=f2[0:1, 0:512]))
                P.op("tensor", lambda q: q.matmul(pbc[:, :], on_s[0:1, 0:64],
                                                  f1[0:1, 0:512], start=True, stop=True))
                P.op("vector", lambda q: q.tensor_tensor(
                    out=atm[:, :], in0=f3[0:64, :], in1=pbc[:, :], op=OP.mult))
                pb0 = 64 * (h % 2)
                P.dma("gpsimd", [(atb[pb0:pb0 + 64, 512 * oc:512 * (oc + 1)],
                                  atm[:, :])])
            for ts in range(4):
                ti = 4 * tc + ts
                P.dma("sync", [(xt[:, :], x2[128 * ti:128 * (ti + 1), :])])
                for half in range(2):
                    def mmo(q, ts=ts, half=half):
                        last = None
                        for dc in range(DCH):
                            last = q.matmul(po[:, 512 * half:512 * (half + 1)],
                                            atb[:, 512 * dc + 128 * ts:512 * dc + 128 * (ts + 1)],
                                            woT[:, D * dc + 512 * half:D * dc + 512 * (half + 1)],
                                            start=(dc == 0), stop=(dc == DCH - 1))
                        return last
                    P.op("tensor", mmo)
                P.op("vector", lambda q: q.tensor_tensor(
                    out=res[:, :], in0=po[:, 0:D], in1=bo_s[:, :], op=OP.add))
                P.op("vector", lambda q: q.tensor_tensor(
                    out=res[:, :], in0=res[:, :], in1=xt[:, :], op=OP.add))
                P.dma("gpsimd", [(x3[128 * ti:128 * (ti + 1), :], res[:, :])])
        if standalone:
            P.build([sem])
    return nc



FI = 2816
NKC = FI // 128
QMIDS = (0.25, 0.75, 1.25, 1.75, 2.5, 3.5, 5.0)
QINCS = (0.5, 0.5, 0.5, 0.5, 1.0, 1.0, 2.0)


def _pair(ap, cw, idx):
    return ap[:, 0:cw].rearrange("p (a two) -> p a two", two=2)[:, :, idx:idx + 1]


def _quant_steps(P, src, dst_bf, width, B):
    nb = width // 16
    qa, qm, lv, qsc, sg = B["qa"], B["qm"], B["lv"], B["qsc"], B["sg"]
    P.op("scalar", lambda q: q.activation(out=qa[:, 0:width], in_=src, func=AF.Abs,
                                          bias=0.0, scale=1.0))
    P.op("scalar", lambda q: q.activation(out=sg[:, 0:width], in_=src, func=AF.Sign,
                                          bias=0.0, scale=1.0))
    cur, cw = qa, width
    for stp in range(4):
        nxt = qm if stp % 2 == 0 else lv
        P.op("vector", lambda q, cur=cur, nxt=nxt, cw=cw: q.tensor_tensor(
            out=nxt[:, 0:cw // 2].rearrange("p (a one) -> p a one", one=1),
            in0=_pair(cur, cw, 0), in1=_pair(cur, cw, 1),
            op=OP.max))
        cur, cw = nxt, cw // 2
    P.op("vector", lambda q, cur=cur: q.tensor_scalar(
        out=qsc[:, 0:nb], in0=cur[:, 0:nb], scalar1=1.0 / 6.0, scalar2=1e-12,
        op0=OP.mult, op1=OP.max))
    qsc3 = qsc[:, 0:nb].rearrange("p (a one) -> p a one", one=1).to_broadcast(
        [128, nb, 16])
    th3 = qsc[:, nb:2 * nb].rearrange("p (a one) -> p a one", one=1).to_broadcast(
        [128, nb, 16])
    a3 = qa[:, 0:width].rearrange("p (a b) -> p a b", b=16)
    m3 = qm[:, 0:width].rearrange("p (a b) -> p a b", b=16)
    l3 = lv[:, 0:width].rearrange("p (a b) -> p a b", b=16)
    for j, (mid, inc) in enumerate(zip(QMIDS, QINCS)):
        P.op("vector", lambda q, mid=mid: q.tensor_scalar(
            out=qsc[:, nb:2 * nb], in0=qsc[:, 0:nb], scalar1=mid, scalar2=None,
            op0=OP.mult))
        if j == 0:
            P.op("vector", lambda q: q.tensor_tensor(out=l3, in0=a3, in1=th3,
                                                     op=OP.is_gt))
            P.op("vector", lambda q, inc=inc: q.tensor_scalar(
                out=lv[:, 0:width], in0=lv[:, 0:width], scalar1=inc, scalar2=None,
                op0=OP.mult))
        else:
            P.op("vector", lambda q: q.tensor_tensor(out=m3, in0=a3, in1=th3,
                                                     op=OP.is_gt))
            P.op("vector", lambda q, inc=inc: q.scalar_tensor_tensor(
                out=lv[:, 0:width], in0=qm[:, 0:width], scalar=inc,
                in1=lv[:, 0:width], op0=OP.mult, op1=OP.add))
    P.op("vector", lambda q: q.tensor_tensor(out=lv[:, 0:width], in0=lv[:, 0:width],
                                             in1=sg[:, 0:width], op=OP.mult))
    d3 = dst_bf.rearrange("p (a b) -> p a b", b=16)
    P.op("vector", lambda q: q.tensor_tensor(out=d3, in0=l3, in1=qsc3, op=OP.mult))


def build_p5(nc=None, P=None, io=None, pref="", consts=True):
    standalone = nc is None
    if standalone:
        nc = bass.Bass("TRN2", target_bir_lowering=False)
        P = Prog(nc)
    io = io or {}
    def dram(name, shape, dt_, kind=None):
        if name in io:
            return io[name]
        if kind is None:
            return nc.dram_tensor(pref + name, shape, dt_).ap()
        return nc.dram_tensor(pref + name, shape, dt_, kind=kind).ap()
    x3 = dram("x3", [S, D], F32, "ExternalInput")
    wg = dram("wg", [FI, D], F32, "ExternalInput")
    wu = dram("wu", [FI, D], F32, "ExternalInput")
    wd = dram("wd", [D, FI], F32, "ExternalInput")
    ident = dram("ident", [128, 128], F32, "ExternalInput")
    out = dram("out", [S, D], F32, "ExternalOutput")

    NH = (512, 512, 512, 512, 512, 256)

    with ExitStack() as es:
        sem = es.enter_context(nc.semaphore(pref + "m")) if standalone else None
        sb = lambda name, shape, dt=F32: es.enter_context(nc.sbuf_tensor(pref + name, shape, dt))
        ps = lambda name, shape, dt=F32: es.enter_context(nc.psum_tensor(pref + name, shape, dt))

        if consts:
            _const_pool(nc, P, es, [0.0, 1e-5])
        wgT = sb("wgT", [128, DCH * FI], BF16)
        wuT = sb("wuT", [128, DCH * FI], BF16)
        wdT = sb("wdT", [128, NKC * D], BF16)
        idf = sb("idf", [128, 128])
        idb = sb("idb", [128, 128], BF16)
        qa = sb("qa", [128, 1408])
        qm = sb("qm", [128, 1408])
        lv = sb("lv", [128, 1408])
        sg_ = sb("sg_", [128, 1408])
        qsc = sb("qsc", [128, 2 * 88])
        stg = sb("stg", [128, 1408])
        qb = sb("qb", [128, FI], BF16)
        xt = sb("xt", [128, D])
        lnb = sb("lnv", [128, D])
        stt = sb("st", [128, 8])
        xqT = sb("xqT", [128, D], BF16)
        hq = sb("hq", [128, FI], BF16)
        hqT = sb("hqT", [128, FI], BF16)
        hc = sb("hc", [128, 512])
        ptb = ps("ptb", [128, D], BF16)
        pg = ps("pg", [128, 512])
        pu = ps("pu", [128, 512])
        po = ps("po", [128, D])
        B = {"qa": qa, "qm": qm, "lv": lv, "qsc": qsc, "sg": sg_}
        sgv = qa[:, 0:512]
        res = lnb

        P.dma("sync", [(idf[:, :], ident)])
        P.op("vector", lambda q: q.tensor_copy(out=idb[:, :], in_=idf[:, :]))

        for (wsrc, wdst) in ((wg, wgT), (wu, wuT)):
            for r in range(NKC):
                P.dma("sync", [(stg[:, 0:D], wsrc[128 * r:128 * (r + 1), :])])
                _quant_steps(P, stg[:, 0:D], qb[:, 0:D], D, B)
                def mmt(q):
                    last = None
                    for dc in range(DCH):
                        last = q.transpose(out=ptb[:, 128 * dc:128 * (dc + 1)],
                                           in_=qb[:, 128 * dc:128 * (dc + 1)],
                                           identity=idb[:, :])
                    return last
                P.op("tensor", mmt)
                def cpw(q, wdst=wdst, r=r):
                    last = None
                    for dc in range(DCH):
                        last = q.copy(wdst[:, FI * dc + 128 * r:FI * dc + 128 * (r + 1)],
                                      ptb[:, 128 * dc:128 * (dc + 1)])
                    return last
                P.op("scalar", cpw)
        for r in range(DCH):
            for hf in range(2):
                P.dma("sync", [(stg[:, :], wd[128 * r:128 * (r + 1),
                                              1408 * hf:1408 * (hf + 1)])])
                _quant_steps(P, stg[:, :], qb[:, 1408 * hf:1408 * (hf + 1)], 1408, B)
            for kg in range(3):
                nblk = min(8, NKC - 8 * kg)
                def mmt2(q, kg=kg, nblk=nblk):
                    last = None
                    for kk in range(nblk):
                        kc = 8 * kg + kk
                        last = q.transpose(out=ptb[:, 128 * kk:128 * (kk + 1)],
                                           in_=qb[:, 128 * kc:128 * (kc + 1)],
                                           identity=idb[:, :])
                    return last
                P.op("tensor", mmt2)
                def cpw2(q, kg=kg, nblk=nblk, r=r):
                    last = None
                    for kk in range(nblk):
                        kc = 8 * kg + kk
                        last = q.copy(wdT[:, D * kc + 128 * r:D * kc + 128 * (r + 1)],
                                      ptb[:, 128 * kk:128 * (kk + 1)])
                    return last
                P.op("scalar", cpw2)

        for i in range(NT):
            P.dma("sync", [(xt[:, :], x3[128 * i:128 * (i + 1), :])])
            _ln_tile(P, xt[:, :], lnb[:, :], qm[:, 0:D], stt)
            _quant_steps(P, lnb[:, :], qb[:, 0:D], D, B)
            _transpose_tile(P, qb[:, 0:D], ptb, xqT[:, :], idb)
            off = 0
            for w in NH:
                for (wT, pp) in ((wgT, pg), (wuT, pu)):
                    def mmg(q, wT=wT, pp=pp, off=off, w=w):
                        last = None
                        for dc in range(DCH):
                            last = q.matmul(pp[:, 0:w],
                                            xqT[:, 128 * dc:128 * (dc + 1)],
                                            wT[:, FI * dc + off:FI * dc + off + w],
                                            start=(dc == 0), stop=(dc == DCH - 1))
                        return last
                    P.op("tensor", mmg)
                P.op("scalar", lambda q, w=w: q.activation(
                    out=sgv[:, 0:w], in_=pg[:, 0:w], func=AF.Silu, bias=0.0, scale=1.0))
                P.op("vector", lambda q, w=w: q.tensor_tensor(
                    out=hc[:, 0:w], in0=sgv[:, 0:w], in1=pu[:, 0:w], op=OP.mult))
                _quant_steps(P, hc[:, 0:w], hq[:, off:off + w], w, B)
                off += w
            for kg in range(3):
                nblk = min(8, NKC - 8 * kg)
                def mmth(q, kg=kg, nblk=nblk):
                    last = None
                    for kk in range(nblk):
                        kc = 8 * kg + kk
                        last = q.transpose(out=ptb[:, 128 * kk:128 * (kk + 1)],
                                           in_=hq[:, 128 * kc:128 * (kc + 1)],
                                           identity=idb[:, :])
                    return last
                P.op("tensor", mmth)
                P.op("scalar", lambda q, kg=kg, nblk=nblk: q.copy(
                    hqT[:, 128 * 8 * kg:128 * (8 * kg + nblk)], ptb[:, 0:128 * nblk]))
            def mmd(q):
                last = None
                for oh in range(2):
                    for kc in range(NKC):
                        last = q.matmul(po[:, 512 * oh:512 * (oh + 1)],
                                        hqT[:, 128 * kc:128 * (kc + 1)],
                                        wdT[:, D * kc + 512 * oh:D * kc + 512 * (oh + 1)],
                                        start=(kc == 0), stop=(kc == NKC - 1),
                                        skip_group_check=True)
                return last
            P.op("tensor", mmd)
            P.op("vector", lambda q: q.tensor_tensor(
                out=res[:, :], in0=po[:, 0:D], in1=xt[:, :], op=OP.add))
            P.dma("gpsimd", [(out[128 * i:128 * (i + 1), :], res[:, :])])
        if standalone:
            P.build([sem])
    return nc



_IDENT = np.eye(128, dtype=np.float32)


def _tile128(v):
    return np.ascontiguousarray(np.tile(np.asarray(v, np.float32)[None, :], (128, 1)))


def _f(a):
    return np.ascontiguousarray(np.asarray(a, np.float32))





def build_fused():
    nc = bass.Bass("TRN2", target_bir_lowering=False)
    P = Prog(nc)
    E = lambda n, s, d: nc.dram_tensor(n, s, d, kind="ExternalInput").ap()
    ident = E("ident", [128, 128], F32)
    ones1 = E("ones1", [1, 128], F32)
    x = E("x", [S, D], F32)
    wq = E("wq", [D, D], F32); wk = E("wk", [D, D], F32)
    wv = E("wv", [D, D], F32); wo = E("wo", [D, D], F32)
    bout_t = E("bout_t", [128, D], F32); ns_t = E("ns_t", [128, 8], F32)
    Ws = E("Ws", [D, 8], F32); bs_t = E("bs_t", [128, 8], F32)
    W1 = E("W1", [8 * D, FF], F32); W2 = E("W2", [8 * FF, D], F32)
    b1c = E("b1c", [8 * 128, 32], F32); b2t = E("b2t", [8 * 128, D], F32)
    sel8 = E("sel8", [128, 8], F32); sel8T = E("sel8T", [8, 128], F32)
    Wq = E("Wq", [D, D], F32); Wk = E("Wk", [D, D], F32)
    Wv = E("Wv", [D, D], F32); Wo = E("Wo", [D, D], F32)
    bqc = E("bqc", [128, 8], F32); bkb = E("bkb", [128, D], F32)
    bvb = E("bvb", [128, D], F32); bob = E("bob", [128, D], F32)
    wg = E("wg", [FI, D], F32); wu = E("wu", [FI, D], F32)
    wd = E("wd", [D, FI], F32)
    out = nc.dram_tensor("out", [S, D], F32, kind="ExternalOutput").ap()
    x1_h = nc.dram_tensor("x1_h", [S, D], F32).ap()
    x2_h = nc.dram_tensor("x2_h", [S, D], F32).ap()
    x3_h = nc.dram_tensor("x3_h", [S, D], F32).ap()
    probs_h = nc.dram_tensor("probs_h", [S, 8], F32).ap()
    cc_in_t = nc.dram_tensor("cc_in", [16, 2048], F32)
    cc_out_t = nc.dram_tensor("cc_out", [128, 2048], F32)

    with ExitStack() as top:
        sems = [top.enter_context(nc.semaphore("ep%d" % i)) for i in range(5)]
        cc_sem = top.enter_context(nc.semaphore("ccs"))
        _const_pool(nc, P, top, [0.0, 1e-5, 1e-6, -SOFTMAX_SHIFT])
        build_p1(nc, P, io=dict(x=x, wq=wq, wk=wk, wv=wv, wo=wo, bout_t=bout_t,
                                ns_t=ns_t, ident=ident, x1=x1_h),
                 pref="a_", consts=False)
        P.new_epoch()
        build_p2(nc, P, io=dict(x1=x1_h, Ws=Ws, bs_t=bs_t, ident=ident,
                                probs=probs_h, cc_in=cc_in_t.ap()),
                 pref="b_", consts=False)
        P.new_epoch()

        def coll(q):
            q.collective_compute("AllGather", OP.bypass,
                                 replica_groups=[list(range(NCORES))],
                                 ins=[cc_in_t.ap()],
                                 outs=[cc_out_t.ap()]).then_inc(cc_sem)
            q.wait_ge(cc_sem, 1)
            return q.nop()
        P.op("gpsimd", coll)
        build_p3(nc, P, io=dict(x1=x1_h, pb=cc_out_t.ap(), pown=probs_h, W1=W1,
                                W2=W2, b1c=b1c, b2t=b2t, sel8=sel8, sel8T=sel8T,
                                ones1=ones1, ident=ident, x2=x2_h),
                 pref="c_", consts=False)
        P.new_epoch()
        build_p4(nc, P, io=dict(x2=x2_h, Wq=Wq, Wk=Wk, Wv=Wv, Wo=Wo, bqc=bqc,
                                bkb=bkb, bvb=bvb, bob=bob, ones1=ones1,
                                ident=ident, x3=x3_h),
                 pref="d_", consts=False)
        P.new_epoch()
        build_p5(nc, P, io=dict(x3=x3_h, wg=wg, wu=wu, wd=wd, ident=ident,
                                out=out),
                 pref="e_", consts=False)
        P.build(sems)
    return nc


def kernel(**inp):
    x = _f(inp["L0_attn_x"])
    cores = list(range(NCORES))
    sel = np.zeros((128, 8), np.float32)
    for p in range(128):
        sel[p, (p % 16) // 2] = 1.0
    nc = build_fused()
    m = {"ident": _IDENT, "ones1": np.ones((1, 128), np.float32),
         "wq": _f(inp["L0_attn_wq"]).reshape(D, D),
         "wk": _f(inp["L0_attn_wk"]).reshape(D, D),
         "wv": _f(inp["L0_attn_wv"]).reshape(D, D),
         "wo": _f(inp["L0_attn_w_out"]).reshape(D, D),
         "bout_t": _tile128(inp["L0_attn_b_out"]),
         "ns_t": np.ascontiguousarray(_f(inp["L0_attn_norm_scale"]).reshape(8, 128).T),
         "Ws": _f(inp["L0_ffn_Ws"]), "bs_t": _tile128(inp["L0_ffn_bs"]),
         "W1": _f(inp["L0_ffn_W1"]).reshape(8 * D, FF),
         "W2": _f(inp["L0_ffn_W2"]).reshape(8 * FF, D),
         "b1c": np.ascontiguousarray(_f(inp["L0_ffn_b1"]).reshape(8, 32, 128)
                                     .transpose(0, 2, 1).reshape(8 * 128, 32)),
         "b2t": np.ascontiguousarray(np.repeat(_f(inp["L0_ffn_b2"])[:, None, :],
                                               128, axis=1).reshape(8 * 128, D)),
         "sel8": sel, "sel8T": np.ascontiguousarray(sel.T),
         "Wq": _f(inp["L1_attn_Wq"]), "Wk": _f(inp["L1_attn_Wk"]),
         "Wv": _f(inp["L1_attn_Wv"]), "Wo": _f(inp["L1_attn_Wo"]),
         "bqc": np.ascontiguousarray(_f(inp["L1_attn_bq"]).reshape(8, 128).T),
         "bkb": _tile128(inp["L1_attn_bk"]), "bvb": _tile128(inp["L1_attn_bv"]),
         "bob": _tile128(inp["L1_attn_bo"]),
         "wg": _f(inp["L1_ffn_w_gate"]), "wu": _f(inp["L1_ffn_w_up"]),
         "wd": _f(inp["L1_ffn_w_down"])}
    r = run_bass_kernel_spmd(nc, [dict(m, x=_f(x[c])) for c in cores],
                             core_ids=cores)
    return np.stack([r.results[c]["out"] for c in cores]).astype(np.float32)


def kernel_unfused(**inp):
    x = _f(inp["L0_attn_x"])
    cores = list(range(NCORES))

    nc1 = build_p1()
    m1 = {"wq": _f(inp["L0_attn_wq"]).reshape(D, D),
          "wk": _f(inp["L0_attn_wk"]).reshape(D, D),
          "wv": _f(inp["L0_attn_wv"]).reshape(D, D),
          "wo": _f(inp["L0_attn_w_out"]).reshape(D, D),
          "bout_t": _tile128(inp["L0_attn_b_out"]),
          "ns_t": np.ascontiguousarray(_f(inp["L0_attn_norm_scale"]).reshape(8, 128).T),
          "ident": _IDENT}
    r1 = run_bass_kernel_spmd(nc1, [dict(m1, x=_f(x[c])) for c in cores],
                              core_ids=cores)
    x1 = np.stack([r1.results[c]["x1"] for c in cores])

    nc2 = build_p2()
    m2 = {"Ws": _f(inp["L0_ffn_Ws"]), "bs_t": _tile128(inp["L0_ffn_bs"]),
          "ident": _IDENT}
    r2 = run_bass_kernel_spmd(nc2, [dict(m2, x1=_f(x1[c])) for c in cores],
                              core_ids=cores)
    probs = np.stack([r2.results[c]["probs"] for c in cores])

    probs_T = np.ascontiguousarray(probs.reshape(NCORES * S, 8).T)
    sel = np.zeros((128, 8), np.float32)
    for p in range(128):
        sel[p, p // 16] = 1.0
    nc3 = build_p3()
    m3 = {"pb": np.ascontiguousarray(probs_T.reshape(128, 2048)),
          "W1": _f(inp["L0_ffn_W1"]).reshape(8 * D, FF),
          "W2": _f(inp["L0_ffn_W2"]).reshape(8 * FF, D),
          "b1c": np.ascontiguousarray(_f(inp["L0_ffn_b1"]).reshape(8, 32, 128)
                                      .transpose(0, 2, 1).reshape(8 * 128, 32)),
          "b2t": np.ascontiguousarray(np.repeat(_f(inp["L0_ffn_b2"])[:, None, :],
                                                128, axis=1).reshape(8 * 128, D)),
          "sel8": sel, "sel8T": np.ascontiguousarray(sel.T),
          "ones1": np.ones((1, 128), np.float32), "ident": _IDENT}
    r3 = run_bass_kernel_spmd(
        nc3, [dict(m3, x1=_f(x1[c]), pown=_f(probs[c])) for c in cores],
        core_ids=cores)
    x2 = np.stack([r3.results[c]["x2"] for c in cores])

    nc4 = build_p4()
    m4 = {"Wq": _f(inp["L1_attn_Wq"]), "Wk": _f(inp["L1_attn_Wk"]),
          "Wv": _f(inp["L1_attn_Wv"]), "Wo": _f(inp["L1_attn_Wo"]),
          "bqc": np.ascontiguousarray(_f(inp["L1_attn_bq"]).reshape(8, 128).T),
          "bkb": _tile128(inp["L1_attn_bk"]), "bvb": _tile128(inp["L1_attn_bv"]),
          "bob": _tile128(inp["L1_attn_bo"]),
          "ones1": np.ones((1, 128), np.float32), "ident": _IDENT}
    r4 = run_bass_kernel_spmd(nc4, [dict(m4, x2=_f(x2[c])) for c in cores],
                              core_ids=cores)
    x3 = np.stack([r4.results[c]["x3"] for c in cores])

    nc5 = build_p5()
    m5 = {"wg": _f(inp["L1_ffn_w_gate"]), "wu": _f(inp["L1_ffn_w_up"]),
          "wd": _f(inp["L1_ffn_w_down"]), "ident": _IDENT}
    r5 = run_bass_kernel_spmd(nc5, [dict(m5, x3=_f(x3[c])) for c in cores],
                              core_ids=cores)
    return np.stack([r5.results[c]["out"] for c in cores]).astype(np.float32)
